# version 15
# baseline (speedup 1.0000x reference)
import numpy as np
import ml_dtypes
import concourse.bass as bass
import concourse.mybir as mybir
from concourse.bass_utils import run_bass_kernel_spmd

F32 = mybir.dt.float32
BF16 = mybir.dt.bfloat16
I32 = mybir.dt.int32
AL = mybir.AluOpType
AF = mybir.ActivationFunctionType
AX = mybir.AxisListType

NCORES = 8
D = 2048
L = 16384
T = L // NCORES
C = 512
CG = 4
DTR = 128
FF = 8192
NP = 1152
EPS = 1e-5
TT = 256
NG = T // TT
NTT = TT // 128

TAPS = {}
STOP_AFTER = None
DEV_BARRIER = False

KB = 1024


def rnd(x, m=32):
    return (x + m - 1) // m * m


class Prog:
    def __init__(self, nc):
        self.nc = nc
        self.eng = {e: [] for e in ("sync", "vector", "scalar", "tensor", "gpsimd")}
        self.sem = {}
        self.ctr = {}

    def add_sem(self, name):
        self.sem[name] = self.nc.semaphore(name).__enter__()
        self.ctr[name] = 0

    def op(self, engine, fn, inc=None, inc_by=1):
        if inc is not None:
            self.ctr[inc] += inc_by
        self.eng[engine].append(("op", fn, inc, inc_by))
        return self.ctr[inc] if inc is not None else None

    def dma(self, engine, out, in_, inc, slow=False):
        self.ctr[inc] += 16
        self.eng[engine].append(
            ("op", (lambda e, o=out, i=in_, s=slow: e.dma_start(
                out=o, in_=i, allow_slow_non_contiguous=s)), inc, 16))
        return self.ctr[inc]

    def drain(self, engine):
        self.eng[engine].append(("op", (lambda e: e.drain()), None, 1))

    def wait(self, engine, semname, value=None):
        v = self.ctr[semname] if value is None else value
        self.eng[engine].append(("wait", semname, v))
        return v

    def now(self, semname):
        return self.ctr[semname]

    def emit(self, engine_name, eng):
        for entry in self.eng[engine_name]:
            if entry[0] == "wait":
                eng.wait_ge(self.sem[entry[1]], entry[2])
            else:
                _, fn, inc, inc_by = entry
                inst = fn(eng)
                if inc is not None:
                    inst.then_inc(self.sem[inc], inc_by)


class Arena:
    SB_BYTES = 206 * KB

    def __init__(self, nc):
        self.nc = nc
        self.backing = nc.sbuf_tensor(
            "arena", [128, self.SB_BYTES], mybir.dt.uint8).__enter__()
        self.base = nc.lookup_mloc(self.backing).addr

    def t(self, name, shape, dtype, off):
        full = [128] + list(shape)
        per_part = int(np.prod(shape)) * mybir.dt.size(dtype)
        assert off % 32 == 0, (name, off)
        assert off + per_part <= self.SB_BYTES, (name, off, per_part, off + per_part)
        h = self.nc.virtual_sbuf_tensor(name, full, dtype)
        self.nc._tensor(h.name, h.shape, h.dtype, type="SB")
        mloc = self.nc.lookup_mloc(h)
        mloc.allocated = True
        mloc.addr = self.base + off
        return h


(S_SUM, S_SSQ, S_MU, S_VAR, S_DEN, S_RHO, S_MAX, S_MIN, S_MX, S_SX, S_RQ,
 S_AA, S_BB, S_SG, S_SU, S_MM, S_SM, S_RQM, S_SMW, S_MURHO, S_SUM2, S_T1,
 S_T2, S_T3, S_RHO2, S_MR2, S_SA, S_SB2, S_Q1, S_Q2) = range(30)


def build_program():
    nc = bass.Bass("TRN2", target_bir_lowering=False)
    A = Arena(nc)
    dt = {}

    def din(name, shape, dtype=F32):
        dt[name] = nc.dram_tensor(name, list(shape), dtype, kind="ExternalInput").ap()

    def dout(name, shape, dtype=F32):
        dt[name] = nc.dram_tensor(name, list(shape), dtype, kind="ExternalOutput").ap()

    def dint(name, shape, dtype=F32):
        dt[name] = nc.dram_tensor(name, list(shape), dtype, kind="Internal").ap()

    din("hsT", (D, T + 3))
    din("hs", (T, D))
    din("wxT", (D, C))
    din("wzT", (D, C))
    din("wpT", (C, NP))
    din("woT", (C, D))
    din("wdtT_rep", (DTR, C))
    din("conv_w", (128, CG * 4))
    din("conv_b", (128, CG))
    din("a_log", (128, CG))
    din("bdt_rep", (128, CG))
    din("d_vec", (128, CG))
    din("ident", (128, 128), BF16)
    din("ident32", (128, 128), F32)
    din("pmask", (128, NCORES))
    din("pmask1m", (128, NCORES))
    din("zcol", (128, 1))
    din("cvals", (128, 4))
    din("ones1", (1, 128))
    din("wgT_sl", (D, FF // NCORES))
    din("wuT_sl", (D, FF // NCORES))
    din("wdT_sl", (FF // NCORES, D))
    dout("out", (T, D))

    dint("ar_in", (128, 40))
    dint("ar_out", (128, 40))
    dint("red_lin", (40 * 128,))
    dint("agw_in_g", (4, 128, 16, 256), BF16)
    dint("agw_out_g", (NCORES, 4, 128, 16, 256), BF16)
    dint("agw_in_u", (4, 128, 16, 256), BF16)
    dint("agw_out_u", (NCORES, 4, 128, 16, 256), BF16)
    dint("agw_in_d", (FF // NCORES, D), BF16)
    dint("agw_out_d", (FF, D), BF16)
    dint("ags_in", (1024,))
    dint("ags_out", (NCORES, 1024))

    O_Y = 170 * KB
    O_WO = 186 * KB
    O_PERM = 202 * KB + 512
    y_bf = A.t("y_bf", (CG, T), BF16, O_Y)
    woT_bf = A.t("woT_bf", (CG, D), BF16, O_WO)

    consts = {}
    pbase = [O_PERM]

    def perm(name, shape, dtype):
        consts[name] = A.t("c_" + name, shape, dtype, pbase[0])
        pbase[0] += rnd(int(np.prod(shape)) * mybir.dt.size(dtype))
        return consts[name]

    perm("conv_w", (CG * 4,), F32)
    perm("conv_b", (CG,), F32)
    perm("a_flat", (CG,), F32)
    perm("bdt", (CG,), F32)
    perm("d_vec", (CG,), F32)
    perm("ident", (128,), BF16)
    perm("ident32", (128,), F32)
    perm("pmask", (NCORES,), F32)
    perm("pmask1m", (NCORES,), F32)
    perm("zcol", (1,), F32)
    perm("ones1", (128,), F32)
    perm("thbc", (12,), F32)
    perm("red", (48,), F32)
    perm("allst", (NCORES, 2, CG), F32)
    perm("hprev", (2, CG), F32)
    perm("ptot", (CG,), F32)

    perm("stats", (2 * NTT, 30), F32)
    perm("cvals", (4,), F32)
    assert pbase[0] <= Arena.SB_BYTES, pbase[0]
    nc.const_aps.register(F32, EPS, consts["cvals"].ap()[:, 0:1])
    nc.const_aps.register(F32, 1.0, consts["cvals"].ap()[:, 1:2])
    nc.const_aps.register(F32, 0.5, consts["cvals"].ap()[:, 2:3])
    nc.const_aps.register(F32, -0.5, consts["cvals"].ap()[:, 3:4])

    HS_W = 544
    hsT_r = [A.t(f"hsT_r{i}", (16, HS_W), BF16, i * rnd(16 * HS_W * 2))
             for i in range(2)]
    o = 2 * rnd(16 * HS_W * 2)
    O_WX = o
    wxT_bf = A.t("wxT_bf", (16, C), BF16, o); o += rnd(16 * C * 2)
    wzT_bf = A.t("wzT_bf", (16, C), BF16, o); o += rnd(16 * C * 2)
    wpT_bf = A.t("wpT_bf", (CG, NP), BF16, o); o += rnd(CG * NP * 2)
    wdt_bf = A.t("wdt_bf", (C,), BF16, o); o += rnd(C * 2)
    O_XFM = o
    x_fm = A.t("x_fm", (CG, T + 3), F32, o); o += rnd(CG * (T + 3) * 4)
    O_Z = max(rnd(o), 112 * KB)
    z_bf = A.t("z_bf", (CG, T), BF16, O_Z)
    O_ACC = O_Z + rnd(CG * T * 2)
    acc_fm = A.t("acc_fm", (CG, T), F32, O_ACC)
    b_fm = A.t("b_fm", (CG, T), F32, O_ACC)
    O_DT = O_ACC + rnd(CG * T * 4)
    dt_bf = A.t("dt_bf", (T,), BF16, O_DT)
    assert O_DT + rnd(T * 2) <= O_Y, (O_DT + rnd(T * 2), O_Y)

    x_act_bf = A.t("x_act_bf", (CG, T), BF16, 0)
    c_fm = A.t("c_fm", (CG, T), F32, 16 * KB)
    h_fm = A.t("h_fm", (CG, T), F32, 48 * KB)
    O_DELTA = 80 * KB
    delta_fm = A.t("delta_fm", (CG, T), F32, O_DELTA)
    assert O_DELTA >= O_XFM
    assert O_DELTA + CG * T * 4 <= O_Z, (O_DELTA + CG * T * 4, O_Z)
    assert 48 * KB + CG * T * 4 <= O_DELTA

    red1 = A.t("red1", (40 * 128,), F32, O_ACC)

    a1_ring = [A.t(f"a1_r{i}", (FF,), F32, i * rnd(FF * 4)) for i in range(2)]
    a2_ring = [A.t(f"a2_r{i}", (FF,), F32, i * rnd(FF * 4)) for i in range(2)]
    a2_out = [A.t(f"a2_o{i}", (FF,), BF16, 2 * rnd(FF * 4) + i * rnd(FF * 2))
              for i in range(2)]
    a2_sc = A.t("a2_sc", (FF,), F32, 2 * rnd(FF * 4) + 2 * rnd(FF * 2))

    m_ar = A.t("m_ar", (NTT, FF), F32, 0)
    ui_sc = A.t("ui_sc", (1024,), I32, 64 * KB)
    c_sc = A.t("c_sc", (1024,), F32, 68 * KB)
    mqs = A.t("mqs", (1024,), BF16, 72 * KB)
    r1_t = A.t("r1_t", (NTT, D), F32, 74 * KB)
    O_BLK = 90 * KB
    sq_tr = A.t("sq_tr", (1024,), F32, O_BLK)
    xq_tm = A.t("xq_tm", (NTT, D), BF16, O_BLK + 4 * KB)
    xqT = A.t("xqT", (16, TT), BF16, O_BLK + 12 * KB)
    sg_sl = [A.t(f"sg_s{i}", (512,), F32, O_BLK + 20 * KB + i * rnd(512 * 4))
             for i in range(4)]
    mqT = A.t("mqT", (2, 64, 128), BF16, O_BLK)
    O_WD = 122 * KB
    wd_ring = [A.t(f"wd_r{i}", (D,), BF16, O_WD + i * rnd(D * 2))
               for i in range(2)]
    hs_sb = A.t("hs_sb", (D,), F32, 130 * KB)
    O_GU = 138 * KB
    gu_ring = [A.t(f"gu_r{i}", (16, 512), BF16, O_GU + i * rnd(16 * 512 * 2))
               for i in range(2)]
    assert O_GU + 2 * rnd(16 * 512 * 2) <= O_Y, (O_GU + 2 * rnd(16 * 512 * 2), O_Y)

    psD = nc.psum_tensor("psD", [128, 2048], F32).__enter__()
    psD2 = nc.psum_tensor("psD2", [128, 2048], F32).__enter__()
    psA = psD2[:, 0:512]
    psB = psD2[:, 512:1024]
    psE = psD2[:, 1024:1536]
    psT = psD2[:, 1536:2048].bitcast(BF16)
    psU = psD2[:, 1536:2048]

    P = Prog(nc)
    for s in ("qc", "qmw", "qr", "qx", "qs", "qh", "qt0", "qt1",
              "qa0", "qa1", "qo0", "qo1",
              "qu0", "qu1", "qu2", "qd0", "qd1", "qd2",
              "v", "s", "t", "cc"):
        P.add_sem(s)

    consts["ags_sb"] = A.t("c_ags_sb", (1024,), F32, O_DT)
    ID = consts["ident"].ap()
    ID32 = consts["ident32"].ap()
    ST = consts["stats"].ap()
    thbc = consts["thbc"].ap()

    def tap(name, src_ap, shape, dtype=F32, waits=(), freeze=()):
        if name not in TAPS:
            return
        d = nc.dram_tensor("tap_" + name, list(shape), dtype,
                           kind="ExternalOutput").ap()
        for sem, val in waits:
            P.wait("gpsimd", sem, val)
        qv = P.dma("gpsimd", d, src_ap, "qs")
        for engn in freeze:
            P.wait(engn, "qs", qv)

    ld = {}
    for nm, tgt in [("conv_w", "conv_w"), ("conv_b", "conv_b"),
                    ("a_log", "a_flat"), ("bdt_rep", "bdt"),
                    ("d_vec", "d_vec"), ("ident", "ident"),
                    ("ident32", "ident32"), ("pmask", "pmask"),
                    ("pmask1m", "pmask1m"), ("zcol", "zcol"),
                    ("cvals", "cvals")]:
        ld[tgt] = P.dma("sync", consts[tgt].ap()[:, :], dt[nm], "qc")
    ld["ones1"] = P.dma("sync", consts["ones1"].ap()[0:1, :], dt["ones1"][0:1, :],
                        "qc")
    QC_ALL = P.now("qc")
    for k in ld:
        ld[k] = QC_ALL
    P.wait("scalar", "qc", ld["a_flat"])
    s_af = P.op("scalar", lambda e: e.activation(
        consts["a_flat"].ap()[:, :], consts["a_flat"].ap()[:, :], AF.Exp), inc="s")
    P.wait("vector", "s", s_af)
    v_af = P.op("vector", lambda e: e.tensor_scalar(
        consts["a_flat"].ap()[:, :], consts["a_flat"].ap()[:, :], -1.0, 0.0,
        AL.mult, AL.add), inc="v")

    a_slices = ([("wgT_sl", i, FF // NCORES) for i in range(16)]
                + [("wuT_sl", i, FF // NCORES) for i in range(16)]
                + [("wdT_sl", i, D) for i in range(8)])
    red = consts["red"].ap()
    red_done = []
    for j, (nm, i, w) in enumerate(a_slices):
        if j >= 2:
            P.wait("sync", "v", red_done[j - 2])
        v = P.dma("sync", a1_ring[j % 2].ap()[:, 0:w],
                  dt[nm][128 * i:128 * (i + 1), :], f"qa{j % 2}")
        P.wait("vector", f"qa{j % 2}", v)
        P.op("vector", lambda e, j=j, w=w: e.tensor_reduce(
            red[:, j:j + 1], a1_ring[j % 2].ap()[:, 0:w], AX.X, AL.add,
            apply_absolute_value=True), inc="v")
        red_done.append(P.now("v"))
    v_a1 = P.now("v")

    P.wait("gpsimd", "v", v_a1)
    v = P.dma("gpsimd", dt["ar_in"][:, 0:40], red[:, 0:40], "qr")
    P.wait("gpsimd", "qr", v)
    P.op("gpsimd", lambda e: e.collective_compute(
        "AllReduce", AL.add, replica_groups=[list(range(NCORES))],
        ins=[dt["ar_in"].opt()], outs=[dt["ar_out"].opt()]), inc="cc")
    P.wait("gpsimd", "cc", P.now("cc"))
    v = P.dma("gpsimd", dt["red_lin"][:].rearrange("(p k) -> p k", p=128),
              dt["ar_out"][:, :], "qr")
    P.wait("gpsimd", "qr", v)
    v = P.dma("gpsimd", red1.ap()[0:1, :], dt["red_lin"].rearrange("(o a) -> o a", o=1),
              "qr")
    P.wait("vector", "qr", v)
    rview = red1.ap()[0:1, :].rearrange("a (p k) -> a p k", k=40)
    P.op("vector", lambda e: e.tensor_reduce(
        red[0:1, 0:1], rview[:, :, 0:16], AX.XY, AL.add))
    P.op("vector", lambda e: e.tensor_reduce(
        red[0:1, 1:2], rview[:, :, 16:32], AX.XY, AL.add))
    P.op("vector", lambda e: e.tensor_reduce(
        red[0:1, 2:3], rview[:, :, 32:40], AX.XY, AL.add))
    P.drain("vector")
    P.op("vector", lambda e: e.tensor_scalar(
        red[0:1, 9:12], red[0:1, 0:3], 1.0 / (FF * D), EPS, AL.mult, AL.add))
    P.drain("vector")
    P.op("vector", lambda e: e.tensor_scalar(
        red[0:1, 3:6], red[0:1, 9:12], 0.5 * (1.0 - 6e-7), 0.0, AL.mult, AL.add))
    P.drain("vector")
    P.op("vector", lambda e: e.tensor_scalar(
        red[0:1, 6:9], red[0:1, 3:6], -1.0, 0.0, AL.mult, AL.add), inc="v")
    P.drain("vector")
    v_thr = P.now("v")
    P.wait("tensor", "qc", ld["ones1"])
    P.wait("tensor", "v", v_thr)
    P.op("tensor", lambda e: e.matmul(
        psE[:, 0:9], consts["ones1"].ap()[0:1, :], red[0:1, 3:12],
        start=True, stop=True), inc="t")
    t_bc = P.now("t")
    P.wait("scalar", "t", t_bc)
    s_thbc = P.op("scalar", lambda e: e.copy(thbc[:, 0:9], psE[:, 0:9]), inc="s")
    tap("thbc", thbc[:, 0:9], (128, 9), waits=[("s", s_thbc)])
    tap("red", red[:, 0:16], (128, 16), waits=[("s", s_thbc)])

    P.dma("gpsimd", wxT_bf.ap()[:, :, :],
          dt["wxT"].rearrange("(a p) b -> p a b", p=128), "qmw")
    P.dma("gpsimd", wzT_bf.ap()[:, :, :],
          dt["wzT"].rearrange("(a p) b -> p a b", p=128), "qmw")
    P.dma("gpsimd", wpT_bf.ap()[:, :, :],
          dt["wpT"].rearrange("(a p) b -> p a b", p=128), "qmw")
    P.dma("gpsimd", woT_bf.ap()[:, :, :],
          dt["woT"].rearrange("(a p) b -> p a b", p=128), "qmw")
    P.dma("gpsimd", wdt_bf.ap()[:, :], dt["wdtT_rep"], "qmw")
    g_wx = g_wz = g_wp = g_wo = g_wdt = P.now("qmw")

    hsTr = dt["hsT"].rearrange("(a p) b -> p a b", p=128)
    strip_v = []
    P.wait("gpsimd", "v", v_a1)
    for s in range(4):
        c0 = 512 * s
        w = 515
        v = P.dma("gpsimd", hsT_r[s % 2].ap()[:, :, 0:w], hsTr[:, :, c0:c0 + w],
                  f"qt{s % 2}")
        strip_v.append(v)
        if s == 1:
            break
    P.wait("tensor", "qmw", g_wx)
    P.wait("tensor", "qmw", g_wz)

    def mm_xz(ps_ap, w_bf, m, slot, col0, n):
        def fn(e):
            r = None
            for k in range(16):
                r = e.matmul(ps_ap, w_bf.ap()[:, k, 128 * m:128 * (m + 1)],
                             hsT_r[slot].ap()[:, k, col0:col0 + n],
                             start=(k == 0), stop=(k == 15))
            return r
        return fn

    strip_t = []
    for s in range(4):
        if s >= 2:
            P.wait("gpsimd", "t", strip_t[s - 2])
            c0 = 512 * s
            w = 515 if s < 3 else 515
            v = P.dma("gpsimd", hsT_r[s % 2].ap()[:, :, 0:w],
                      hsTr[:, :, c0:c0 + w], f"qt{s % 2}")
            strip_v.append(v)
        P.wait("tensor", f"qt{s % 2}", strip_v[s])
        for m in range(CG):
            P.op("tensor", mm_xz(psD[:, 512 * m:512 * m + 512], wxT_bf, m,
                                 s % 2, 0, 512), inc="t")
        t_x = P.now("t")
        P.wait("scalar", "t", t_x)
        for m in range(CG):
            P.op("scalar", lambda e, m=m, s=s: e.copy(
                x_fm.ap()[:, m, 512 * s:512 * s + 512],
                psD[:, 512 * m:512 * m + 512]), inc="s")
        s_xe = P.now("s")
        P.wait("tensor", "s", s_xe)
        for m in range(CG):
            P.op("tensor", mm_xz(psD[:, 512 * m:512 * m + 512], wzT_bf, m,
                                 s % 2, 3, 512), inc="t")
        t_z = P.now("t")
        P.wait("vector", "t", t_z)
        for m in range(CG):
            P.op("vector", lambda e, m=m, s=s: e.tensor_copy(
                z_bf.ap()[:, m, 512 * s:512 * s + 512],
                psD[:, 512 * m:512 * m + 512]), inc="v")
        v_z = P.now("v")
        P.wait("tensor", "v", v_z)
        strip_t.append(P.now("t"))
    for m in range(CG):
        def tailf(e, m=m):
            r = None
            for k in range(16):
                r = e.matmul(psA[:, 4 * m:4 * m + 3],
                             wxT_bf.ap()[:, k, 128 * m:128 * (m + 1)],
                             hsT_r[1].ap()[:, k, 512:515],
                             start=(k == 0), stop=(k == 15))
            return r
        P.op("tensor", tailf, inc="t")
    t_tail = P.now("t")
    P.wait("scalar", "t", t_tail)
    for m in range(CG):
        P.op("scalar", lambda e, m=m: e.copy(
            x_fm.ap()[:, m, T:T + 3], psA[:, 4 * m:4 * m + 3]), inc="s")
    s_xall = P.now("s")
    P.wait("tensor", "s", s_xall)

    P.wait("vector", "s", s_xall)
    P.wait("vector", "qc", ld["conv_w"])
    cw = consts["conv_w"].ap()
    for m in range(CG):
        P.op("vector", lambda e, m=m: e.tensor_scalar(
            acc_fm.ap()[:, m, :], x_fm.ap()[:, m, 0:T],
            cw[:, 4 * m:4 * m + 1], 0.0, AL.mult, AL.add), inc="v")
    P.drain("vector")
    for k in range(1, 4):
        for m in range(CG):
            P.op("vector", lambda e, m=m, k=k: e.scalar_tensor_tensor(
                acc_fm.ap()[:, m, :], x_fm.ap()[:, m, k:k + T],
                cw[:, 4 * m + k:4 * m + k + 1], acc_fm.ap()[:, m, :],
                AL.mult, AL.add), inc="v")
        P.drain("vector")
    v_conv = P.now("v")
    P.wait("scalar", "v", v_conv)
    P.wait("scalar", "qc", ld["conv_b"])
    for m in range(CG):
        P.op("scalar", lambda e, m=m: e.activation(
            x_act_bf.ap()[:, m, :], acc_fm.ap()[:, m, :], AF.Silu,
            bias=consts["conv_b"].ap()[:, m:m + 1], scale=1.0), inc="s")
    s_act = P.now("s")
    tap("x_act", x_act_bf.ap()[:, :, :], (128, CG, T), BF16, waits=[("s", s_act)])

    P.wait("tensor", "s", s_act)
    P.wait("tensor", "qmw", g_wp)

    def mm_par(ms, strip, ps_ap):
        def fn(e):
            r = None
            for k in range(CG):
                r = e.matmul(ps_ap, wpT_bf.ap()[:, k, 128 * ms:128 * (ms + 1)],
                             x_act_bf.ap()[:, k, 512 * strip:512 * strip + 512],
                             start=(k == 0), stop=(k == CG - 1))
            return r
        return fn

    for strip in range(4):
        P.op("tensor", mm_par(0, strip, psA), inc="t")
        for m in range(1, 5):
            P.op("tensor", mm_par(m, strip, psD[:, 512 * (m - 1):512 * m]), inc="t")
        t1 = P.now("t")
        P.wait("vector", "t", t1)
        P.op("vector", lambda e, strip=strip: e.tensor_copy(
            dt_bf.ap()[:, 512 * strip:512 * strip + 512], psA), inc="v")
        P.wait("scalar", "t", t1)
        for m in range(1, 5):
            P.op("scalar", lambda e, m=m, strip=strip: e.copy(
                b_fm.ap()[:, m - 1, 512 * strip:512 * strip + 512],
                psD[:, 512 * (m - 1):512 * m]), inc="s")
        P.wait("tensor", "v", P.now("v"))
        P.wait("tensor", "s", P.now("s"))
        for m in range(5, 9):
            P.op("tensor", mm_par(m, strip, psD[:, 512 * (m - 5):512 * (m - 4)]),
                 inc="t")
        t2 = P.now("t")
        P.wait("scalar", "t", t2)
        for m in range(5, 9):
            P.op("scalar", lambda e, m=m, strip=strip: e.copy(
                c_fm.ap()[:, m - 5, 512 * strip:512 * strip + 512],
                psD[:, 512 * (m - 5):512 * (m - 4)]), inc="s")
        P.wait("tensor", "s", P.now("s"))
    v_par = P.now("v")

    P.wait("tensor", "v", v_par)
    P.wait("tensor", "qmw", g_wdt)
    P.wait("scalar", "qc", ld["bdt"])
    for strip in range(4):
        for m in range(CG):
            P.op("tensor", lambda e, m=m, strip=strip: e.matmul(
                psE, wdt_bf.ap()[:, 128 * m:128 * (m + 1)],
                dt_bf.ap()[:, 512 * strip:512 * strip + 512],
                start=True, stop=True), inc="t")
            tm = P.now("t")
            P.wait("scalar", "t", tm)
            P.op("scalar", lambda e, m=m, strip=strip: e.activation(
                delta_fm.ap()[:, m, 512 * strip:512 * strip + 512], psE,
                AF.Exp, bias=consts["bdt"].ap()[:, m:m + 1], scale=1.0), inc="s")
            P.wait("tensor", "s", P.now("s"))
    P.drain("scalar")
    for m in range(CG):
        P.op("scalar", lambda e, m=m: e.activation(
            delta_fm.ap()[:, m, :], delta_fm.ap()[:, m, :], AF.Ln, bias=1.0),
            inc="s")
    P.drain("scalar")
    s_delta = P.now("s")
    tap("delta", delta_fm.ap()[:, :, :], (128, CG, T), waits=[("s", s_delta)])

    P.wait("vector", "s", s_delta)
    for m in range(CG):
        P.op("vector", lambda e, m=m: e.tensor_tensor(
            b_fm.ap()[:, m, :], b_fm.ap()[:, m, :], delta_fm.ap()[:, m, :],
            AL.mult), inc="v")
    P.drain("vector")
    for m in range(CG):
        P.op("vector", lambda e, m=m: e.tensor_tensor(
            b_fm.ap()[:, m, :], b_fm.ap()[:, m, :], x_act_bf.ap()[:, m, :],
            AL.mult), inc="v")
    P.drain("vector")
    for m in range(CG):
        P.op("vector", lambda e, m=m: e.tensor_scalar(
            delta_fm.ap()[:, m, :], delta_fm.ap()[:, m, :],
            consts["a_flat"].ap()[:, m:m + 1], 0.0, AL.mult, AL.add), inc="v")
    P.drain("vector")
    for m in range(CG):
        P.op("vector", lambda e, m=m: e.tensor_reduce(
            consts["ptot"].ap()[:, m:m + 1], delta_fm.ap()[:, m, :], AX.X,
            AL.add), inc="v")
    P.drain("vector")
    v_su = P.now("v")
    if "slog" in TAPS:
        dslog = nc.dram_tensor("tap_slog", [128, CG, T], F32,
                               kind="ExternalOutput").ap()
        P.wait("gpsimd", "v", v_su)
        qv = P.dma("gpsimd", dslog, delta_fm.ap()[:, :, :], "qs")
        P.wait("scalar", "qs", qv)
    P.wait("scalar", "v", v_su)
    P.op("scalar", lambda e: e.activation(
        consts["ptot"].ap()[:, :], consts["ptot"].ap()[:, :], AF.Exp), inc="s")
    for m in range(CG):
        P.op("scalar", lambda e, m=m: e.activation(
            delta_fm.ap()[:, m, :], delta_fm.ap()[:, m, :], AF.Exp), inc="s")
    P.drain("scalar")
    s_abar = P.now("s")

    tap("bu", b_fm.ap()[:, :, :], (128, CG, T), waits=[("v", v_su)])
    tap("aflat", consts["a_flat"].ap()[:, :], (128, CG), waits=[("v", v_su)])
    tap("abar", delta_fm.ap()[:, :, :], (128, CG, T), waits=[("s", s_abar)])
    P.wait("vector", "s", s_abar)
    P.wait("vector", "qc", ld["zcol"])
    for m in range(CG):
        P.op("vector", lambda e, m=m: e.tensor_tensor_scan(
            h_fm.ap()[:, m, :], delta_fm.ap()[:, m, :], b_fm.ap()[:, m, :],
            consts["zcol"].ap()[:, 0:1], AL.mult, AL.add), inc="v")
    P.drain("vector")
    v_scan1 = P.now("v")

    tap("h1", h_fm.ap()[:, :, :], (128, CG, T), waits=[("v", v_scan1)])
    P.wait("gpsimd", "v", v_scan1)
    P.dma("gpsimd", dt["ags_in"][0:512].rearrange("(g p) -> p g", p=128),
          consts["ptot"].ap()[:, 0:CG], "qx", slow=True)
    v = P.dma("gpsimd", dt["ags_in"][512:1024].rearrange("(g p) -> p g", p=128),
              h_fm.ap()[:, :, T - 1:T].rearrange("p g a -> p (g a)"), "qx",
              slow=True)
    P.wait("gpsimd", "qx", v)
    P.op("gpsimd", lambda e: e.collective_compute(
        "AllGather", AL.bypass, replica_groups=[list(range(NCORES))],
        ins=[dt["ags_in"].opt()], outs=[dt["ags_out"].opt()]), inc="cc")
    P.wait("gpsimd", "cc", P.now("cc"))
    v_ag = P.dma("gpsimd", consts["ags_sb"].ap()[0:8, :], dt["ags_out"][:, :],
                 "qx")
    P.wait("tensor", "qx", v_ag)
    P.wait("tensor", "qc", ld["ident32"])
    for kk in range(8):
        P.op("tensor", lambda e, kk=kk: e.transpose(
            psD[:, 8 * kk:8 * kk + 8],
            consts["ags_sb"].ap()[0:8, 128 * kk:128 * (kk + 1)],
            ID32[0:8, 0:8]), inc="t")
    t_ag = P.now("t")
    P.wait("vector", "t", t_ag)
    al = consts["allst"].ap()
    P.op("vector", lambda e: e.tensor_copy(
        al[:, :, :, :], psD[:, 0:64].rearrange("p (k g j) -> p j k g", k=2, g=4)),
        inc="v")
    P.drain("vector")
    hp = consts["hprev"].ap()
    P.op("vector", lambda e: e.memset(hp[:, 1, :], 0.0), inc="v")
    P.drain("vector")
    P.wait("vector", "qc", ld["pmask1m"])
    for j in range(NCORES):
        P.op("vector", lambda e, j=j: e.tensor_scalar(
            al[:, j, 0, :], al[:, j, 0, :], consts["pmask"].ap()[:, j:j + 1],
            consts["pmask1m"].ap()[:, j:j + 1], AL.mult, AL.add), inc="v")
        P.drain("vector")
        P.op("vector", lambda e, j=j: e.tensor_tensor(
            hp[:, 1, :], hp[:, 1, :], al[:, j, 0, :], AL.mult), inc="v")
        P.drain("vector")
        P.op("vector", lambda e, j=j: e.scalar_tensor_tensor(
            hp[:, 1, :], al[:, j, 1, :], consts["pmask"].ap()[:, j:j + 1],
            hp[:, 1, :], AL.mult, AL.add), inc="v")
        P.drain("vector")
    v_fold = P.now("v")
    P.wait("scalar", "v", v_fold)
    s_hp = P.op("scalar", lambda e: e.copy(hp[:, 0, :], hp[:, 1, :]), inc="s")

    tap("hprev", hp[:, :, :], (128, 2, CG), waits=[("s", s_hp)])
    tap("allst", al[:, :, :, :], (128, NCORES, 2, CG), waits=[("v", v_fold)])
    tap("ptot", consts["ptot"].ap()[:, :], (128, CG), waits=[("s", s_abar)])
    P.wait("vector", "s", s_hp)
    for m in range(CG):
        P.op("vector", lambda e, m=m: e.tensor_tensor_scan(
            h_fm.ap()[:, m, :], delta_fm.ap()[:, m, :], b_fm.ap()[:, m, :],
            hp[:, 0, m:m + 1], AL.mult, AL.add), inc="v")
    P.drain("vector")
    v_scan2 = P.now("v")
    tap("h", h_fm.ap()[:, :, :], (128, CG, T), waits=[("v", v_scan2)])

    P.wait("scalar", "v", v_scan2)
    for m in range(CG):
        P.op("scalar", lambda e, m=m: e.activation(
            b_fm.ap()[:, m, :], z_bf.ap()[:, m, :], AF.Silu), inc="s")
    s_sz = P.now("s")
    for m in range(CG):
        P.op("vector", lambda e, m=m: e.tensor_tensor(
            c_fm.ap()[:, m, :], c_fm.ap()[:, m, :], h_fm.ap()[:, m, :], AL.mult),
            inc="v")
    P.drain("vector")
    P.wait("vector", "qc", ld["d_vec"])
    for m in range(CG):
        P.op("vector", lambda e, m=m: e.scalar_tensor_tensor(
            c_fm.ap()[:, m, :], x_act_bf.ap()[:, m, :],
            consts["d_vec"].ap()[:, m:m + 1], c_fm.ap()[:, m, :],
            AL.mult, AL.add), inc="v")
    P.drain("vector")
    P.wait("vector", "s", s_sz)
    for m in range(CG):
        P.op("vector", lambda e, m=m: e.tensor_tensor(
            y_bf.ap()[:, m, :], c_fm.ap()[:, m, :], b_fm.ap()[:, m, :], AL.mult),
            inc="v")
    P.drain("vector")
    v_y = P.now("v")
    tap("y", y_bf.ap()[:, :, :], (128, CG, T), BF16, waits=[("v", v_y)])

    if STOP_AFTER == "mamba":
        P.wait("gpsimd", "v", v_y)
        P.wait("gpsimd", "qs", P.now("qs"))
        return _finish(nc, P, dt)
    P.wait("sync", "v", v_y)
    P.wait("vector", "s", s_thbc)
    a2_store = []
    vq_done = []
    for j, (nm, i, w) in enumerate(a_slices):
        col = {"wgT_sl": 0, "wuT_sl": 1, "wdT_sl": 2}[nm]
        agname = {"wgT_sl": "agw_in_g", "wuT_sl": "agw_in_u",
                  "wdT_sl": "agw_in_d"}[nm]
        if j >= 2:
            P.wait("sync", "v", vq_done[j - 2])
        v = P.dma("sync", a2_ring[j % 2].ap()[:, 0:w],
                  dt[nm][128 * i:128 * (i + 1), :], f"qa{j % 2}")
        P.wait("vector", f"qa{j % 2}", v)
        if j >= 2:
            P.wait("vector", a2_store[j - 2][0], a2_store[j - 2][1])
        P.op("vector", lambda e, j=j, w=w, col=col: e.tensor_scalar(
            a2_sc.ap()[:, 0:w], a2_ring[j % 2].ap()[:, 0:w],
            thbc[:, col:col + 1], 0.0, AL.is_gt, AL.add), inc="v")
        P.op("vector", lambda e, j=j, w=w, col=col: e.tensor_scalar(
            a2_ring[j % 2].ap()[:, 0:w], a2_ring[j % 2].ap()[:, 0:w],
            thbc[:, 3 + col:4 + col], 0.0, AL.is_lt, AL.add), inc="v")
        P.drain("vector")
        P.op("vector", lambda e, j=j, w=w: e.tensor_tensor(
            a2_out[j % 2].ap()[:, 0:w], a2_sc.ap()[:, 0:w],
            a2_ring[j % 2].ap()[:, 0:w], AL.subtract), inc="v")
        P.drain("vector")
        vq_done.append(P.now("v"))
        P.wait("gpsimd", "v", vq_done[j])
        qo = f"qo{j % 2}"
        if nm == "wdT_sl":
            v = P.dma("gpsimd", dt[agname][128 * i:128 * (i + 1), :],
                      a2_out[j % 2].ap()[:, 0:w], qo)
        else:
            for jc in range(4):
                v = P.dma("gpsimd", dt[agname][jc, :, i, :],
                          a2_out[j % 2].ap()[:, 256 * jc:256 * jc + 256], qo)
        a2_store.append((qo, v))

    P.wait("gpsimd", a2_store[14][0], a2_store[14][1])
    P.wait("gpsimd", a2_store[15][0], a2_store[15][1])
    P.op("gpsimd", lambda e: e.collective_compute(
        "AllGather", AL.bypass, replica_groups=[list(range(NCORES))],
        ins=[dt["agw_in_g"].opt()], outs=[dt["agw_out_g"].opt()]), inc="cc")
    cc_g = P.now("cc")
    P.wait("gpsimd", a2_store[30][0], a2_store[30][1])
    P.wait("gpsimd", a2_store[31][0], a2_store[31][1])
    P.op("gpsimd", lambda e: e.collective_compute(
        "AllGather", AL.bypass, replica_groups=[list(range(NCORES))],
        ins=[dt["agw_in_u"].opt()], outs=[dt["agw_out_u"].opt()]), inc="cc")
    cc_u = P.now("cc")
    P.wait("gpsimd", a2_store[38][0], a2_store[38][1])
    P.wait("gpsimd", a2_store[39][0], a2_store[39][1])
    P.op("gpsimd", lambda e: e.collective_compute(
        "AllGather", AL.bypass, replica_groups=[list(range(NCORES))],
        ins=[dt["agw_in_d"].opt()], outs=[dt["agw_out_d"].opt()]), inc="cc")
    cc_d = P.now("cc")
    tap("wgq", dt["agw_out_g"][:, :, :, :, :], (NCORES, 4, 128, 16, 256), BF16, waits=[("cc", cc_d)])
    tap("wdq", dt["agw_out_d"][:, :], (FF, D), BF16, waits=[("cc", cc_d)])

    if STOP_AFTER == "a2":
        P.wait("gpsimd", "cc", cc_d)
        P.wait("gpsimd", "qs", P.now("qs"))
        return _finish(nc, P, dt)
    wgTr = dt["agw_out_g"]
    wuTr = dt["agw_out_u"]
    RC = 1.0 / D

    v_a2 = P.now("v")

    gu_consumed = {}
    wd_consumed = {}
    prev_gu = []
    prev_wd = []

    def load_gu(key, src, c):
        slot = len(prev_gu) % 2
        if len(prev_gu) == 0:
            P.wait("sync", "cc", cc_g)
        elif len(prev_gu) == 1:
            P.wait("sync", "cc", cc_u)
        if len(prev_gu) >= 2:
            P.wait("sync", "t", gu_consumed[prev_gu[-2]])
        b0 = 2 * c
        P.dma("sync", gu_ring[slot].ap()[:, :, 0:256],
              src[b0 // 4, b0 % 4, :, :, :], f"qu{slot}")
        v = P.dma("sync", gu_ring[slot].ap()[:, :, 256:512],
                  src[(b0 + 1) // 4, (b0 + 1) % 4, :, :, :], f"qu{slot}")
        prev_gu.append(key)
        return slot, v

    def load_wd(key, k):
        slot = len(prev_wd) % 2
        if len(prev_wd) >= 2:
            P.wait("gpsimd", "t", wd_consumed[prev_wd[-2]])
        v = P.dma("gpsimd", wd_ring[slot].ap()[:, :],
                  dt["agw_out_d"][128 * k:128 * (k + 1), :], f"qd{slot}")
        prev_wd.append(key)
        return slot, v

    hs_cons = [v_a2]

    def load_hs(tg):
        P.wait("gpsimd", "v", hs_cons[0])
        return P.dma("gpsimd", hs_sb.ap()[:, :],
                     dt["hs"][128 * tg:128 * (tg + 1), :], "qh")

    def vst(fn):
        P.op("vector", fn, inc="v")
        P.drain("vector")

    P.wait("tensor", "qmw", g_wo)
    P.wait("tensor", "v", v_y)
    P.wait("sync", "cc", cc_d)
    P.wait("gpsimd", "cc", cc_d)
    prev_qs = [P.now("qs")]

    STAGE = 3 * D

    def emit_wout(gx, tt):
        tg = gx * NTT + tt

        def wof(e, tg=tg):
            r = None
            for n in range(4):
                for k in range(CG):
                    r = e.matmul(psD[:, 512 * n:512 * n + 512],
                                 y_bf.ap()[:, k, 128 * tg:128 * (tg + 1)],
                                 woT_bf.ap()[:, k, 512 * n:512 * n + 512],
                                 start=(k == 0), stop=(k == CG - 1))
            return r
        P.op("tensor", wof, inc="t")
        return P.now("t")

    def emit_r1evac(gx, tt, dst, v_hs, t_wo):
        tq = tt + NTT * (gx % 2)
        P.wait("vector", "qh", v_hs)
        P.wait("vector", "t", t_wo)
        vst(lambda e, tq=tq, dst=dst: e.scalar_tensor_tensor(
            dst, psD[:, 0:2048], 1.0, hs_sb.ap()[:, :],
            AL.mult, AL.add, accum_out=ST[:, tq, S_SUM:S_SUM + 1]))
        hs_cons[0] = P.now("v")
        return P.now("v")

    def emit_ln1(gx, src, t_gate=None, v_gate=None):
        if t_gate is not None:
            P.wait("scalar", "t", t_gate)
        if v_gate is not None:
            P.wait("scalar", "v", v_gate)
        for tt in range(NTT):
            tq = tt + NTT * (gx % 2)
            for hh in range(2):
                P.op("scalar", lambda e, tq=tq, tt=tt, hh=hh: e.activation(
                    sq_tr.ap()[:, :], src(tt, slice(1024 * hh, 1024 * (hh + 1))),
                    AF.Square, accum_out=ST[:, tq, S_Q1 + hh:S_Q1 + hh + 1]),
                    inc="s")
                P.drain("scalar")
        s_sq = P.now("s")
        for tt in range(NTT):
            tq = tt + NTT * (gx % 2)
            vst(lambda e, tq=tq, tt=tt: e.tensor_reduce(
                ST[:, tq, S_MAX:S_MAX + 1], src(tt, slice(0, 2048)), AX.X, AL.max))
            vst(lambda e, tq=tq, tt=tt: e.tensor_reduce(
                ST[:, tq, S_MIN:S_MIN + 1], src(tt, slice(0, 2048)), AX.X, AL.min))
        P.wait("vector", "s", s_sq)
        for tt in range(NTT):
            tq = tt + NTT * (gx % 2)
            vst(lambda e, tq=tq: e.tensor_tensor(
                ST[:, tq, S_SSQ:S_SSQ + 1], ST[:, tq, S_Q1:S_Q1 + 1],
                ST[:, tq, S_Q2:S_Q2 + 1], AL.add))
            vst(lambda e, tq=tq: e.tensor_scalar(
                ST[:, tq, S_MU:S_MU + 1], ST[:, tq, S_SUM:S_SUM + 1], RC, 0.0,
                AL.mult, AL.add))
            vst(lambda e, tq=tq: e.tensor_tensor(
                ST[:, tq, S_T1:S_T1 + 1], ST[:, tq, S_MU:S_MU + 1],
                ST[:, tq, S_MU:S_MU + 1], AL.mult))
            vst(lambda e, tq=tq: e.scalar_tensor_tensor(
                ST[:, tq, S_VAR:S_VAR + 1], ST[:, tq, S_SSQ:S_SSQ + 1], RC,
                ST[:, tq, S_T1:S_T1 + 1], AL.mult, AL.subtract))
        v_var = P.now("v")
        P.wait("scalar", "v", v_var)
        for tt in range(NTT):
            tq = tt + NTT * (gx % 2)
            P.op("scalar", lambda e, tq=tq: e.activation(
                ST[:, tq, S_DEN:S_DEN + 1], ST[:, tq, S_VAR:S_VAR + 1], AF.Sqrt,
                bias=EPS, scale=1.0), inc="s")
        s_den = P.now("s")
        P.wait("vector", "s", s_den)
        for tt in range(NTT):
            tq = tt + NTT * (gx % 2)
            vst(lambda e, tq=tq: e.reciprocal(
                ST[:, tq, S_T1:S_T1 + 1], ST[:, tq, S_DEN:S_DEN + 1]))
            vst(lambda e, tq=tq: e.tensor_tensor(
                ST[:, tq, S_T2:S_T2 + 1], ST[:, tq, S_DEN:S_DEN + 1],
                ST[:, tq, S_T1:S_T1 + 1], AL.mult))
            vst(lambda e, tq=tq: e.tensor_scalar(
                ST[:, tq, S_T2:S_T2 + 1], ST[:, tq, S_T2:S_T2 + 1], -1.0, 2.0,
                AL.mult, AL.add))
            vst(lambda e, tq=tq: e.tensor_tensor(
                ST[:, tq, S_RHO:S_RHO + 1], ST[:, tq, S_T1:S_T1 + 1],
                ST[:, tq, S_T2:S_T2 + 1], AL.mult))
            vst(lambda e, tq=tq: e.tensor_tensor(
                ST[:, tq, S_T1:S_T1 + 1], ST[:, tq, S_MAX:S_MAX + 1],
                ST[:, tq, S_MU:S_MU + 1], AL.subtract))
            vst(lambda e, tq=tq: e.tensor_tensor(
                ST[:, tq, S_T2:S_T2 + 1], ST[:, tq, S_MU:S_MU + 1],
                ST[:, tq, S_MIN:S_MIN + 1], AL.subtract))
            vst(lambda e, tq=tq: e.tensor_tensor(
                ST[:, tq, S_MX:S_MX + 1], ST[:, tq, S_T1:S_T1 + 1],
                ST[:, tq, S_T2:S_T2 + 1], AL.max))
            vst(lambda e, tq=tq: e.tensor_tensor(
                ST[:, tq, S_T1:S_T1 + 1], ST[:, tq, S_MX:S_MX + 1],
                ST[:, tq, S_RHO:S_RHO + 1], AL.mult))
            vst(lambda e, tq=tq: e.tensor_scalar(
                ST[:, tq, S_SX:S_SX + 1], ST[:, tq, S_T1:S_T1 + 1], 1.0 / 127.0,
                EPS, AL.mult, AL.add))
            vst(lambda e, tq=tq: e.reciprocal(
                ST[:, tq, S_T1:S_T1 + 1], ST[:, tq, S_SX:S_SX + 1]))
            vst(lambda e, tq=tq: e.tensor_tensor(
                ST[:, tq, S_T2:S_T2 + 1], ST[:, tq, S_SX:S_SX + 1],
                ST[:, tq, S_T1:S_T1 + 1], AL.mult))
            vst(lambda e, tq=tq: e.tensor_scalar(
                ST[:, tq, S_T2:S_T2 + 1], ST[:, tq, S_T2:S_T2 + 1], -1.0, 2.0,
                AL.mult, AL.add))
            vst(lambda e, tq=tq: e.tensor_tensor(
                ST[:, tq, S_RQ:S_RQ + 1], ST[:, tq, S_T1:S_T1 + 1],
                ST[:, tq, S_T2:S_T2 + 1], AL.mult))
            vst(lambda e, tq=tq: e.tensor_tensor(
                ST[:, tq, S_AA:S_AA + 1], ST[:, tq, S_RHO:S_RHO + 1],
                ST[:, tq, S_RQ:S_RQ + 1], AL.mult))
            vst(lambda e, tq=tq: e.tensor_tensor(
                ST[:, tq, S_BB:S_BB + 1], ST[:, tq, S_MU:S_MU + 1],
                ST[:, tq, S_AA:S_AA + 1], AL.mult))
            vst(lambda e, tq=tq: e.tensor_tensor(
                ST[:, tq, S_SG:S_SG + 1], ST[:, tq, S_SX:S_SX + 1],
                thbc[:, 6:7], AL.mult))
            vst(lambda e, tq=tq: e.tensor_tensor(
                ST[:, tq, S_SU:S_SU + 1], ST[:, tq, S_SX:S_SX + 1],
                thbc[:, 7:8], AL.mult))
            vst(lambda e, tq=tq: e.tensor_tensor(
                ST[:, tq, S_MURHO:S_MURHO + 1], ST[:, tq, S_MU:S_MU + 1],
                ST[:, tq, S_RHO:S_RHO + 1], AL.mult))
        v_stats = P.now("v")
        P.wait("scalar", "v", v_stats)
        P.op("scalar", lambda e: e.copy(
            ST[:, :, S_SA:S_SA + 2], ST[:, :, S_AA:S_AA + 2]), inc="s")
        P.wait("vector", "s", P.now("s"))

    def emit_xq(gx, src, t_gate=None):
        if t_gate is not None:
            P.wait("vector", "t", t_gate)
        for tt in range(NTT):
            tq = tt + NTT * (gx % 2)
            for hh in range(2):
                cl = slice(1024 * hh, 1024 * (hh + 1))
                vst(lambda e, tq=tq, tt=tt, hh=hh, cl=cl: e.tensor_scalar(
                    c_sc.ap()[:, :], src(tt, cl),
                    ST[:, tq, S_SA:S_SA + 1], ST[:, tq, S_SB2:S_SB2 + 1],
                    AL.mult, AL.subtract))
                vst(lambda e: e.tensor_copy(ui_sc.ap()[:, :], c_sc.ap()[:, :]))
                vst(lambda e, tt=tt, cl=cl: e.tensor_copy(
                    xq_tm.ap()[:, tt, cl], ui_sc.ap()[:, :]))
        return P.now("v")

    def emit_xqT(v_xq, v_bank=None):
        P.wait("tensor", "v", v_xq)
        if v_bank is not None:
            P.wait("tensor", "v", v_bank)
        for tt in range(NTT):
            for kb in range(2):
                for j in range(8):
                    k = kb * 8 + j
                    P.op("tensor", lambda e, tt=tt, k=k, j=j: e.transpose(
                        psT[:, 128 * j:128 * (j + 1)],
                        xq_tm.ap()[:, tt, 128 * k:128 * (k + 1)], ID), inc="t")
                t_tr = P.now("t")
                P.wait("scalar", "t", t_tr)
                P.op("scalar", lambda e, tt=tt, kb=kb: e.copy(
                    xqT.ap()[:, 8 * kb:8 * kb + 8, 128 * tt:128 * (tt + 1)],
                    psT[:, 0:1024].rearrange("p (j b) -> p j b", j=8)), inc="s")
                P.wait("tensor", "s", P.now("s"))
        return P.now("s")

    v_hs = load_hs(0)
    t_wo = emit_wout(0, 0)
    v_r1 = emit_r1evac(0, 0, r1_t.ap()[:, 0, :], v_hs, t_wo)
    v_hs = load_hs(1)
    P.wait("tensor", "v", v_r1)
    t_wo = emit_wout(0, 1)
    v_r1 = emit_r1evac(0, 1, r1_t.ap()[:, 1, :], v_hs, t_wo)

    def src_r1t(tt, cl):
        return r1_t.ap()[:, tt, cl]

    def src_stage(tt, cl):
        return m_ar.ap()[:, tt, STAGE + cl.start:STAGE + cl.stop]

    emit_ln1(0, src_r1t, v_gate=v_r1)
    v_xq = emit_xq(0, src_r1t)
    s_xqT = {0: emit_xqT(v_xq)}
    ln2v = {-1: (0, 0)}

    for g in range(NG):
        gp = g % 2
        last = (g == NG - 1)

        P.wait("tensor", "s", s_xqT[g])
        if g >= 1:
            P.wait("tensor", "v", ln2v[g - 1][0])
        if not last:
            v_hs = load_hs((g + 1) * NTT + 0)
        sil_prev = [0]
        mstt_prev = [0]
        for c in range(16):
            slot_g, vg = load_gu((g, 2 * c), wgTr, c)
            slot_u, vu = load_gu((g, 2 * c + 1), wuTr, c)
            if c == 0:
                P.wait("vector", "qs", prev_qs[0])
            P.wait("tensor", f"qu{slot_g}", vg)
            P.wait("tensor", "s", sil_prev[0])
            for tt in range(NTT):
                ps = psA if tt == 0 else psB
                def gmf(e, tt=tt, ps=ps, slot=slot_g):
                    r = None
                    for k in range(16):
                        r = e.matmul(ps[:, 0:512],
                                     xqT.ap()[:, k, 128 * tt:128 * (tt + 1)],
                                     gu_ring[slot].ap()[:, k, :],
                                     start=(k == 0), stop=(k == 15))
                    return r
                P.op("tensor", gmf, inc="t")
            gu_consumed[(g, 2 * c)] = P.now("t")
            t_gate = P.now("t")
            P.wait("scalar", "t", t_gate)
            for tt in range(NTT):
                ps = psA if tt == 0 else psB
                tq = tt + NTT * gp
                sgi = 2 * (c % 2) + tt
                P.op("scalar", lambda e, tq=tq, ps=ps, sgi=sgi: e.activation(
                    sg_sl[sgi].ap()[:, 0:512], ps[:, 0:512],
                    AF.Silu, scale=ST[:, tq, S_SG:S_SG + 1]), inc="s")
            sil_prev[0] = P.now("s")
            P.wait("tensor", f"qu{slot_u}", vu)
            P.wait("tensor", "v", mstt_prev[0])
            for tt in range(NTT):
                ps = psE if tt == 0 else psU
                def umf(e, tt=tt, ps=ps, slot=slot_u):
                    r = None
                    for k in range(16):
                        r = e.matmul(ps[:, 0:512],
                                     xqT.ap()[:, k, 128 * tt:128 * (tt + 1)],
                                     gu_ring[slot].ap()[:, k, :],
                                     start=(k == 0), stop=(k == 15))
                    return r
                P.op("tensor", umf, inc="t")
            gu_consumed[(g, 2 * c + 1)] = P.now("t")
            t_up = P.now("t")
            P.wait("vector", "t", t_up)
            P.wait("vector", "s", sil_prev[0])
            for tt in range(NTT):
                ps = psE if tt == 0 else psU
                tq = tt + NTT * gp
                sgi = 2 * (c % 2) + tt
                P.op("vector", lambda e, tq=tq, tt=tt, ps=ps, c=c, sgi=sgi:
                     e.scalar_tensor_tensor(
                         m_ar.ap()[:, tt, 512 * c:512 * c + 512], ps[:, 0:512],
                         ST[:, tq, S_SU:S_SU + 1], sg_sl[sgi].ap()[:, 0:512],
                         AL.mult, AL.mult), inc="v")
            mstt_prev[0] = P.now("v")
        P.drain("vector")

        if not last:
            P.wait("tensor", "v", ln2v[g - 1][1])
            t_wo0 = emit_wout(g + 1, 0)

        for tt in range(NTT):
            tq = tt + NTT * gp
            vst(lambda e, tq=tq, tt=tt: e.tensor_reduce(
                ST[:, tq, S_MM:S_MM + 1], m_ar.ap()[:, tt, :], AX.X, AL.max,
                apply_absolute_value=True))
            vst(lambda e, tq=tq: e.tensor_scalar(
                ST[:, tq, S_SM:S_SM + 1], ST[:, tq, S_MM:S_MM + 1], 1.0 / 127.0,
                EPS, AL.mult, AL.add))
            vst(lambda e, tq=tq: e.reciprocal(
                ST[:, tq, S_T1:S_T1 + 1], ST[:, tq, S_SM:S_SM + 1]))
            vst(lambda e, tq=tq: e.tensor_tensor(
                ST[:, tq, S_T2:S_T2 + 1], ST[:, tq, S_SM:S_SM + 1],
                ST[:, tq, S_T1:S_T1 + 1], AL.mult))
            vst(lambda e, tq=tq: e.tensor_scalar(
                ST[:, tq, S_T2:S_T2 + 1], ST[:, tq, S_T2:S_T2 + 1], -1.0, 2.0,
                AL.mult, AL.add))
            vst(lambda e, tq=tq: e.tensor_tensor(
                ST[:, tq, S_RQM:S_RQM + 1], ST[:, tq, S_T1:S_T1 + 1],
                ST[:, tq, S_T2:S_T2 + 1], AL.mult))
            vst(lambda e, tq=tq: e.tensor_tensor(
                ST[:, tq, S_SMW:S_SMW + 1], ST[:, tq, S_SM:S_SM + 1],
                thbc[:, 8:9], AL.mult))
        v_msc = P.now("v")
        P.wait("scalar", "v", v_msc)
        P.op("scalar", lambda e: e.copy(
            ST[:, :, S_SA:S_SA + 1], ST[:, :, S_RQM:S_RQM + 1]), inc="s")
        P.wait("vector", "s", P.now("s"))

        tr_prev = [0]

        def qchunk(tt, cc):
            tq = tt + NTT * gp
            cl = slice(1024 * cc, 1024 * (cc + 1))
            vst(lambda e, tq=tq, tt=tt, cl=cl: e.tensor_scalar(
                m_ar.ap()[:, tt, cl], m_ar.ap()[:, tt, cl],
                ST[:, tq, S_SA:S_SA + 1], 0.0, AL.mult, AL.add))
            vst(lambda e, tt=tt, cl=cl: e.tensor_copy(
                ui_sc.ap()[:, :], m_ar.ap()[:, tt, cl]))
            P.wait("vector", "t", tr_prev[0])
            vst(lambda e: e.tensor_copy(mqs.ap()[:, :], ui_sc.ap()[:, :]))
            v_mq = P.now("v")
            P.wait("tensor", "v", v_mq)
            for j in range(8):
                P.op("tensor", lambda e, j=j: e.transpose(
                    psT[:, 128 * j:128 * (j + 1)],
                    mqs.ap()[:, 128 * j:128 * (j + 1)], ID), inc="t")
            tr_prev[0] = P.now("t")
            P.wait("scalar", "t", tr_prev[0])
            P.op("scalar", lambda e, tt=tt, cc=cc: e.copy(
                mqT.ap()[:, tt, 8 * cc:8 * cc + 8, :],
                psT[:, 0:1024].rearrange("p (j b) -> p j b", j=8)), inc="s")
            P.wait("tensor", "s", P.now("s"))

        for cc in range(8):
            qchunk(0, cc)
        if not last:
            v_r1e = emit_r1evac(g + 1, 0, m_ar.ap()[:, 0, STAGE:STAGE + 2048],
                                v_hs, t_wo0)
            v_hs = load_hs((g + 1) * NTT + 1)
            P.wait("tensor", "v", v_r1e)
            t_wo1 = emit_wout(g + 1, 1)
        for cc in range(8):
            qchunk(1, cc)
        if not last:
            v_r1e = emit_r1evac(g + 1, 1, m_ar.ap()[:, 1, STAGE:STAGE + 2048],
                                v_hs, t_wo1)

        if not last:
            P.wait("tensor", "v", v_r1e)
        t_k16 = t_k48 = None
        for k in range(64):
            slot, vw = load_wd((g, k), k)
            P.wait("tensor", f"qd{slot}", vw)
            for tt in range(NTT):
                ps = psD if tt == 0 else psD2
                for n in range(4):
                    P.op("tensor", lambda e, tt=tt, k=k, n=n, ps=ps, slot=slot:
                         e.matmul(ps[:, 512 * n:512 * n + 512],
                                  mqT.ap()[:, tt, k, :],
                                  wd_ring[slot].ap()[:, 512 * n:512 * n + 512],
                                  start=(k == 0), stop=(k == 63)), inc="t")
            wd_consumed[(g, k)] = P.now("t")
            if k == 15:
                t_k16 = P.now("t")
            elif k == 47:
                t_k48 = P.now("t")
        t_down = P.now("t")

        if not last:
            emit_ln1(g + 1, src_stage, t_gate=t_k16, v_gate=v_r1e)
            v_xq = emit_xq(g + 1, src_stage, t_gate=t_k48)

        P.wait("vector", "t", t_down)
        marks = {}
        for tt in (1, 0):
            tq = tt + NTT * gp
            ps = psD if tt == 0 else psD2
            for hh in range(2):
                cl = slice(1024 * hh, 1024 * (hh + 1))
                vst(lambda e, tq=tq, tt=tt, ps=ps, cl=cl: e.tensor_scalar(
                    c_sc.ap()[:, :], ps[:, cl],
                    ST[:, tq, S_SMW:S_SMW + 1], ST[:, tq, S_MURHO:S_MURHO + 1],
                    AL.mult, AL.subtract))
                vst(lambda e, tq=tq, tt=tt, hh=hh, cl=cl:
                    e.scalar_tensor_tensor(
                        m_ar.ap()[:, tt, cl], r1_t.ap()[:, tt, cl],
                        ST[:, tq, S_RHO:S_RHO + 1], c_sc.ap()[:, :],
                        AL.mult, AL.add,
                        accum_out=ST[:, tq, S_T1 + hh:S_T1 + hh + 1]))
            vst(lambda e, tq=tq: e.tensor_tensor(
                ST[:, tq, S_SUM2:S_SUM2 + 1], ST[:, tq, S_T1:S_T1 + 1],
                ST[:, tq, S_T2:S_T2 + 1], AL.add))
            marks[tt] = P.now("v")
            if tt == 1 and not last:
                s_xqT[g + 1] = emit_xqT(v_xq, v_bank=marks[1])
        ln2v[g] = (marks[1], marks[0])

        if not last:
            for tt in range(NTT):
                vst(lambda e, tt=tt: e.tensor_copy(
                    r1_t.ap()[:, tt, :], m_ar.ap()[:, tt, STAGE:STAGE + 2048]))

        P.wait("scalar", "v", marks[0])
        for tt in range(NTT):
            tq = tt + NTT * gp
            for hh in range(2):
                P.op("scalar", lambda e, tq=tq, tt=tt, hh=hh: e.activation(
                    sq_tr.ap()[:, :],
                    m_ar.ap()[:, tt, 1024 * hh:1024 * (hh + 1)],
                    AF.Square, accum_out=ST[:, tq, S_Q1 + hh:S_Q1 + hh + 1]),
                    inc="s")
                P.drain("scalar")
        s_sq2 = P.now("s")
        P.wait("vector", "s", s_sq2)
        for tt in range(NTT):
            tq = tt + NTT * gp
            vst(lambda e, tq=tq: e.tensor_tensor(
                ST[:, tq, S_SSQ:S_SSQ + 1], ST[:, tq, S_Q1:S_Q1 + 1],
                ST[:, tq, S_Q2:S_Q2 + 1], AL.add))
            vst(lambda e, tq=tq: e.tensor_scalar(
                ST[:, tq, S_MU:S_MU + 1], ST[:, tq, S_SUM2:S_SUM2 + 1], RC, 0.0,
                AL.mult, AL.add))
            vst(lambda e, tq=tq: e.tensor_tensor(
                ST[:, tq, S_T1:S_T1 + 1], ST[:, tq, S_MU:S_MU + 1],
                ST[:, tq, S_MU:S_MU + 1], AL.mult))
            vst(lambda e, tq=tq: e.scalar_tensor_tensor(
                ST[:, tq, S_VAR:S_VAR + 1], ST[:, tq, S_SSQ:S_SSQ + 1], RC,
                ST[:, tq, S_T1:S_T1 + 1], AL.mult, AL.subtract))
        v_var2 = P.now("v")
        P.wait("scalar", "v", v_var2)
        for tt in range(NTT):
            tq = tt + NTT * gp
            P.op("scalar", lambda e, tq=tq: e.activation(
                ST[:, tq, S_DEN:S_DEN + 1], ST[:, tq, S_VAR:S_VAR + 1], AF.Sqrt,
                bias=EPS, scale=1.0), inc="s")
        s_den2 = P.now("s")
        P.wait("vector", "s", s_den2)
        for tt in range(NTT):
            tq = tt + NTT * gp
            vst(lambda e, tq=tq: e.reciprocal(
                ST[:, tq, S_RHO2:S_RHO2 + 1], ST[:, tq, S_DEN:S_DEN + 1]))
            vst(lambda e, tq=tq: e.tensor_tensor(
                ST[:, tq, S_MR2:S_MR2 + 1], ST[:, tq, S_MU:S_MU + 1],
                ST[:, tq, S_RHO2:S_RHO2 + 1], AL.mult))
        v_r2 = P.now("v")
        P.wait("scalar", "v", v_r2)
        P.op("scalar", lambda e: e.copy(
            ST[:, :, S_RHO2:S_RHO2 + 2], ST[:, :, S_RHO2:S_RHO2 + 2]), inc="s")
        P.wait("vector", "s", P.now("s"))
        for tt in range(NTT):
            tq = tt + NTT * gp
            tg = g * NTT + tt
            vst(lambda e, tq=tq, tt=tt: e.tensor_scalar(
                m_ar.ap()[:, tt, 0:2048], m_ar.ap()[:, tt, 0:2048],
                ST[:, tq, S_RHO2:S_RHO2 + 1], ST[:, tq, S_MR2:S_MR2 + 1],
                AL.mult, AL.subtract))
            v_fin = P.now("v")
            P.wait("scalar", "v", v_fin)
            P.op("scalar", lambda e, tt=tt, tg=tg: e.dma_start(
                out=dt["out"][128 * tg:128 * (tg + 1), :],
                in_=m_ar.ap()[:, tt, 0:2048]), inc="qs", inc_by=16)
        prev_qs[0] = P.now("qs")

    P.wait("gpsimd", "qs", P.now("qs"))
    return _finish(nc, P, dt)


def _finish(nc, P, dt):
    with nc.Block() as block:
        @block.sync
        def _(eng):
            P.emit("sync", eng)

        @block.vector
        def _(eng):
            P.emit("vector", eng)

        @block.scalar
        def _(eng):
            P.emit("scalar", eng)

        @block.tensor
        def _(eng):
            P.emit("tensor", eng)

        @block.gpsimd
        def _(eng):
            P.emit("gpsimd", eng)

    return nc, dt


def _host_layout(inputs):
    f32 = np.float32
    hs = np.asarray(inputs["L0_attn_hidden_states"][0], f32)
    hsT = np.ascontiguousarray(hs.T)
    hsT_ext = np.concatenate([np.zeros((D, 3), f32), hsT], axis=1)

    wxT = np.ascontiguousarray(np.asarray(inputs["L0_attn_W_in_x"], f32).T)
    wzT = np.ascontiguousarray(np.asarray(inputs["L0_attn_W_in_z"], f32).T)
    wpT = np.ascontiguousarray(np.asarray(inputs["L0_attn_W_param"], f32).T)
    woT = np.ascontiguousarray(np.asarray(inputs["L0_attn_W_out"], f32).T)
    wdt = np.asarray(inputs["L0_attn_W_dt"], f32)
    wdtT_rep = np.ascontiguousarray(np.repeat(wdt.T, 16, axis=1))
    wgT = np.ascontiguousarray(np.asarray(inputs["L0_ffn_w_gate"], f32).T)
    wuT = np.ascontiguousarray(np.asarray(inputs["L0_ffn_w_up"], f32).T)
    wdT = np.ascontiguousarray(np.asarray(inputs["L0_ffn_w_down"], f32).T)

    def chmat(v):
        return np.ascontiguousarray(np.asarray(v, f32).reshape(CG, 128).T)

    conv_w = np.asarray(inputs["L0_attn_conv_w"], f32).reshape(C, 4)
    conv_wm = np.zeros((128, CG * 4), f32)
    for g in range(CG):
        for k in range(4):
            conv_wm[:, 4 * g + k] = conv_w[128 * g:128 * (g + 1), k]
    conv_b = chmat(inputs["L0_attn_conv_b"])
    a_log = chmat(np.asarray(inputs["L0_attn_A_log"], f32).reshape(C))
    bdt_rep = chmat(np.repeat(np.asarray(inputs["L0_attn_b_dt"], f32), 16))
    d_vec = chmat(inputs["L0_attn_D"])

    ident = np.eye(128, dtype=ml_dtypes.bfloat16)
    ident32 = np.eye(128, dtype=f32)
    ones1 = np.ones((1, 128), f32)
    zcol = np.zeros((128, 1), f32)

    in_maps = []
    for c in range(NCORES):
        pm = np.zeros((128, NCORES), f32)
        pm[:, :c] = 1.0
        im = {
            "hsT": np.ascontiguousarray(hsT_ext[:, T * c:T * c + T + 3]),
            "hs": np.ascontiguousarray(hs[T * c:T * (c + 1)]),
            "wxT": wxT, "wzT": wzT, "wpT": wpT, "woT": woT,
            "wdtT_rep": wdtT_rep,
            "conv_w": conv_wm, "conv_b": conv_b, "a_log": a_log,
            "bdt_rep": bdt_rep, "d_vec": d_vec,
            "ident": ident, "ident32": ident32,
            "pmask": pm, "pmask1m": np.ascontiguousarray(1.0 - pm),
            "zcol": zcol, "ones1": ones1,
            "cvals": np.tile(np.array([[EPS, 1.0, 0.5, -0.5]], f32), (128, 1)),
            "wgT_sl": np.ascontiguousarray(wgT[:, 1024 * c:1024 * (c + 1)]),
            "wuT_sl": np.ascontiguousarray(wuT[:, 1024 * c:1024 * (c + 1)]),
            "wdT_sl": np.ascontiguousarray(wdT[1024 * c:1024 * (c + 1)]),
        }
        in_maps.append(im)
    return in_maps


_BUILT = {}


def kernel(**inputs):
    if "nc" not in _BUILT:
        nc, dtm = build_program()
        _BUILT["nc"] = nc
    nc = _BUILT["nc"]
    in_maps = _host_layout(inputs)
    res = run_bass_kernel_spmd(nc, in_maps, core_ids=list(range(NCORES)))
    out = np.concatenate([np.asarray(r["out"]) for r in res.results], axis=0)
    return out.reshape(1, L, D).astype(np.float32)


# version 16 (final)
# speedup vs baseline: 1.2701x; 1.2701x over previous
import numpy as np
import ml_dtypes
import concourse.bass as bass
import concourse.mybir as mybir
from concourse.bass_utils import run_bass_kernel_spmd

F32 = mybir.dt.float32
BF16 = mybir.dt.bfloat16
F8 = mybir.dt.float8e4
I32 = mybir.dt.int32
AL = mybir.AluOpType
AF = mybir.ActivationFunctionType
AX = mybir.AxisListType

NCORES = 8
D = 2048
L = 16384
T = L // NCORES
C = 512
CG = 4
DTR = 128
FF = 8192
NP = 1152
EPS = 1e-5
TT = 256
NG = T // TT
NTT = TT // 128

TAPS = {}
STOP_AFTER = None
DEV_BARRIER = False

KB = 1024


def rnd(x, m=32):
    return (x + m - 1) // m * m


class Prog:
    def __init__(self, nc):
        self.nc = nc
        self.eng = {e: [] for e in ("sync", "vector", "scalar", "tensor", "gpsimd")}
        self.sem = {}
        self.ctr = {}

    def add_sem(self, name):
        self.sem[name] = self.nc.semaphore(name).__enter__()
        self.ctr[name] = 0

    def op(self, engine, fn, inc=None, inc_by=1):
        if inc is not None:
            self.ctr[inc] += inc_by
        self.eng[engine].append(("op", fn, inc, inc_by))
        return self.ctr[inc] if inc is not None else None

    def dma(self, engine, out, in_, inc, slow=False):
        self.ctr[inc] += 16
        self.eng[engine].append(
            ("op", (lambda e, o=out, i=in_, s=slow: e.dma_start(
                out=o, in_=i, allow_slow_non_contiguous=s)), inc, 16))
        return self.ctr[inc]

    def drain(self, engine):
        self.eng[engine].append(("op", (lambda e: e.drain()), None, 1))

    def wait(self, engine, semname, value=None):
        v = self.ctr[semname] if value is None else value
        self.eng[engine].append(("wait", semname, v))
        return v

    def now(self, semname):
        return self.ctr[semname]

    def emit(self, engine_name, eng):
        for entry in self.eng[engine_name]:
            if entry[0] == "wait":
                eng.wait_ge(self.sem[entry[1]], entry[2])
            else:
                _, fn, inc, inc_by = entry
                inst = fn(eng)
                if inc is not None:
                    inst.then_inc(self.sem[inc], inc_by)


class Arena:
    SB_BYTES = 206 * KB

    def __init__(self, nc):
        self.nc = nc
        self.backing = nc.sbuf_tensor(
            "arena", [128, self.SB_BYTES], mybir.dt.uint8).__enter__()
        self.base = nc.lookup_mloc(self.backing).addr

    def t(self, name, shape, dtype, off):
        full = [128] + list(shape)
        per_part = int(np.prod(shape)) * mybir.dt.size(dtype)
        assert off % 32 == 0, (name, off)
        assert off + per_part <= self.SB_BYTES, (name, off, per_part, off + per_part)
        h = self.nc.virtual_sbuf_tensor(name, full, dtype)
        self.nc._tensor(h.name, h.shape, h.dtype, type="SB")
        mloc = self.nc.lookup_mloc(h)
        mloc.allocated = True
        mloc.addr = self.base + off
        return h


(S_SUM, S_SSQ, S_MU, S_VAR, S_DEN, S_RHO, S_MAX, S_MIN, S_MX, S_SX, S_RQ,
 S_AA, S_BB, S_SG, S_SU, S_MM, S_SM, S_RQM, S_SMW, S_MURHO, S_SUM2, S_T1,
 S_T2, S_T3, S_RHO2, S_MR2, S_SA, S_SB2, S_Q1, S_Q2) = range(30)


def build_program():
    nc = bass.Bass("TRN2", target_bir_lowering=False)
    A = Arena(nc)
    dt = {}

    def din(name, shape, dtype=F32):
        dt[name] = nc.dram_tensor(name, list(shape), dtype, kind="ExternalInput").ap()

    def dout(name, shape, dtype=F32):
        dt[name] = nc.dram_tensor(name, list(shape), dtype, kind="ExternalOutput").ap()

    def dint(name, shape, dtype=F32):
        dt[name] = nc.dram_tensor(name, list(shape), dtype, kind="Internal").ap()

    din("hsT", (D, T + 3))
    din("hs", (T, D))
    din("wxT", (D, C))
    din("wzT", (D, C))
    din("wpT", (C, NP))
    din("woT", (C, D))
    din("wdtT_rep", (DTR, C))
    din("conv_w", (128, CG * 4))
    din("conv_b", (128, CG))
    din("a_log", (128, CG))
    din("bdt_rep", (128, CG))
    din("d_vec", (128, CG))
    din("ident", (128, 128), BF16)
    din("ident32", (128, 128), F32)
    din("pmask", (128, NCORES))
    din("pmask1m", (128, NCORES))
    din("zcol", (128, 1))
    din("cvals", (128, 4))
    din("ones1", (1, 128))
    din("wgT_sl", (D, FF // NCORES))
    din("wuT_sl", (D, FF // NCORES))
    din("wdT_sl", (FF // NCORES, D))
    dout("out", (T, D))

    dint("ar_in", (128, 40))
    dint("ar_out", (128, 40))
    dint("red_lin", (40 * 128,))
    dint("agw_in_g", (4, 128, 16, 256), F8)
    dint("agw_out_g", (NCORES, 4, 128, 16, 256), F8)
    dint("agw_in_u", (4, 128, 16, 256), F8)
    dint("agw_out_u", (NCORES, 4, 128, 16, 256), F8)
    dint("agw_in_d", (FF // NCORES, D), F8)
    dint("agw_out_d", (FF, D), F8)
    dint("ags_in", (1024,))
    dint("ags_out", (NCORES, 1024))

    O_Y = 170 * KB
    O_WO = 186 * KB
    O_PERM = 202 * KB + 512
    y_bf = A.t("y_bf", (CG, T), BF16, O_Y)
    woT_bf = A.t("woT_bf", (CG, D), BF16, O_WO)

    consts = {}
    pbase = [O_PERM]

    def perm(name, shape, dtype):
        consts[name] = A.t("c_" + name, shape, dtype, pbase[0])
        pbase[0] += rnd(int(np.prod(shape)) * mybir.dt.size(dtype))
        return consts[name]

    perm("conv_w", (CG * 4,), F32)
    perm("conv_b", (CG,), F32)
    perm("a_flat", (CG,), F32)
    perm("bdt", (CG,), F32)
    perm("d_vec", (CG,), F32)
    perm("ident", (128,), BF16)
    perm("ident32", (128,), F32)
    perm("pmask", (NCORES,), F32)
    perm("pmask1m", (NCORES,), F32)
    perm("zcol", (1,), F32)
    perm("ones1", (128,), F32)
    perm("thbc", (12,), F32)
    perm("red", (48,), F32)
    perm("allst", (NCORES, 2, CG), F32)
    perm("hprev", (2, CG), F32)
    perm("ptot", (CG,), F32)

    perm("stats", (2 * NTT, 30), F32)
    perm("cvals", (4,), F32)
    assert pbase[0] <= Arena.SB_BYTES, pbase[0]
    nc.const_aps.register(F32, EPS, consts["cvals"].ap()[:, 0:1])
    nc.const_aps.register(F32, 1.0, consts["cvals"].ap()[:, 1:2])
    nc.const_aps.register(F32, 0.5, consts["cvals"].ap()[:, 2:3])
    nc.const_aps.register(F32, -0.5, consts["cvals"].ap()[:, 3:4])

    HS_W = 544
    hsT_r = [A.t(f"hsT_r{i}", (16, HS_W), BF16, i * rnd(16 * HS_W * 2))
             for i in range(2)]
    o = 2 * rnd(16 * HS_W * 2)
    O_WX = o
    wxT_bf = A.t("wxT_bf", (16, C), BF16, o); o += rnd(16 * C * 2)
    wzT_bf = A.t("wzT_bf", (16, C), BF16, o); o += rnd(16 * C * 2)
    wpT_bf = A.t("wpT_bf", (CG, NP), BF16, o); o += rnd(CG * NP * 2)
    wdt_bf = A.t("wdt_bf", (C,), BF16, o); o += rnd(C * 2)
    O_XFM = o
    x_fm = A.t("x_fm", (CG, T + 3), F32, o); o += rnd(CG * (T + 3) * 4)
    O_Z = max(rnd(o), 112 * KB)
    z_bf = A.t("z_bf", (CG, T), BF16, O_Z)
    O_ACC = O_Z + rnd(CG * T * 2)
    acc_fm = A.t("acc_fm", (CG, T), F32, O_ACC)
    b_fm = A.t("b_fm", (CG, T), F32, O_ACC)
    O_DT = O_ACC + rnd(CG * T * 4)
    dt_bf = A.t("dt_bf", (T,), BF16, O_DT)
    assert O_DT + rnd(T * 2) <= O_Y, (O_DT + rnd(T * 2), O_Y)

    x_act_bf = A.t("x_act_bf", (CG, T), BF16, 0)
    c_fm = A.t("c_fm", (CG, T), F32, 16 * KB)
    h_fm = A.t("h_fm", (CG, T), F32, 48 * KB)
    O_DELTA = 80 * KB
    delta_fm = A.t("delta_fm", (CG, T), F32, O_DELTA)
    assert O_DELTA >= O_XFM
    assert O_DELTA + CG * T * 4 <= O_Z, (O_DELTA + CG * T * 4, O_Z)
    assert 48 * KB + CG * T * 4 <= O_DELTA

    red1 = A.t("red1", (40 * 128,), F32, O_ACC)

    a1_ring = [A.t(f"a1_r{i}", (FF,), F32, i * rnd(FF * 4)) for i in range(2)]
    a2_ring = [A.t(f"a2_r{i}", (FF,), F32, i * rnd(FF * 4)) for i in range(2)]
    a2_out = [A.t(f"a2_o{i}", (FF,), F8, 2 * rnd(FF * 4) + i * rnd(FF * 2))
              for i in range(2)]
    a2_sc = A.t("a2_sc", (FF,), F32, 2 * rnd(FF * 4) + 2 * rnd(FF * 2))

    m_ar = A.t("m_ar", (NTT, FF), F32, 0)
    ui_sc = A.t("ui_sc", (1024,), I32, 64 * KB)
    c_sc = A.t("c_sc", (1024,), F32, 68 * KB)
    mqs = A.t("mqs", (1024,), BF16, 72 * KB)
    r1_t = A.t("r1_t", (NTT, D), F32, 74 * KB)
    O_BLK = 90 * KB
    sq_tr = A.t("sq_tr", (1024,), F32, O_BLK)
    xq_tm = A.t("xq_tm", (NTT, D), BF16, O_BLK + 4 * KB)
    xqT = A.t("xqT", (16, TT), BF16, O_BLK + 12 * KB)
    sg_sl = [A.t(f"sg_s{i}", (512,), F32, O_BLK + 20 * KB + i * rnd(512 * 4))
             for i in range(4)]
    mqT = A.t("mqT", (2, 64, 128), BF16, O_BLK)
    O_WD = 122 * KB
    wd_ring = [A.t(f"wd_r{i}", (D,), F8, O_WD + i * rnd(D))
               for i in range(4)]
    hs_sb = A.t("hs_sb", (D,), F32, 130 * KB)
    O_GU = 138 * KB
    gu_ring = [A.t(f"gu_r{i}", (16, 512), F8, O_GU + i * rnd(16 * 512))
               for i in range(2)]
    assert O_GU + 2 * rnd(16 * 512 * 2) <= O_Y, (O_GU + 2 * rnd(16 * 512 * 2), O_Y)

    psD = nc.psum_tensor("psD", [128, 2048], F32).__enter__()
    psD2 = nc.psum_tensor("psD2", [128, 2048], F32).__enter__()
    psA = psD2[:, 0:512]
    psB = psD2[:, 512:1024]
    psE = psD2[:, 1024:1536]
    psT = psD2[:, 1536:2048].bitcast(BF16)
    psU = psD2[:, 1536:2048]

    P = Prog(nc)
    for s in ("qc", "qmw", "qr", "qx", "qs", "qh", "qt0", "qt1",
              "qa0", "qa1", "qo0", "qo1",
              "qu0", "qu1", "qu2", "qd0", "qd1", "qd2", "qd3",
              "v", "s", "t", "cc"):
        P.add_sem(s)

    consts["ags_sb"] = A.t("c_ags_sb", (1024,), F32, O_DT)
    ID = consts["ident"].ap()
    ID32 = consts["ident32"].ap()
    ST = consts["stats"].ap()
    thbc = consts["thbc"].ap()

    def tap(name, src_ap, shape, dtype=F32, waits=(), freeze=()):
        if name not in TAPS:
            return
        d = nc.dram_tensor("tap_" + name, list(shape), dtype,
                           kind="ExternalOutput").ap()
        for sem, val in waits:
            P.wait("gpsimd", sem, val)
        qv = P.dma("gpsimd", d, src_ap, "qs")
        for engn in freeze:
            P.wait(engn, "qs", qv)

    ld = {}
    for nm, tgt in [("conv_w", "conv_w"), ("conv_b", "conv_b"),
                    ("a_log", "a_flat"), ("bdt_rep", "bdt"),
                    ("d_vec", "d_vec"), ("ident", "ident"),
                    ("ident32", "ident32"), ("pmask", "pmask"),
                    ("pmask1m", "pmask1m"), ("zcol", "zcol"),
                    ("cvals", "cvals")]:
        ld[tgt] = P.dma("sync", consts[tgt].ap()[:, :], dt[nm], "qc")
    ld["ones1"] = P.dma("sync", consts["ones1"].ap()[0:1, :], dt["ones1"][0:1, :],
                        "qc")
    QC_ALL = P.now("qc")
    for k in ld:
        ld[k] = QC_ALL
    P.wait("scalar", "qc", ld["a_flat"])
    s_af = P.op("scalar", lambda e: e.activation(
        consts["a_flat"].ap()[:, :], consts["a_flat"].ap()[:, :], AF.Exp), inc="s")
    P.wait("vector", "s", s_af)
    v_af = P.op("vector", lambda e: e.tensor_scalar(
        consts["a_flat"].ap()[:, :], consts["a_flat"].ap()[:, :], -1.0, 0.0,
        AL.mult, AL.add), inc="v")

    a_slices = ([("wgT_sl", i, FF // NCORES) for i in range(16)]
                + [("wuT_sl", i, FF // NCORES) for i in range(16)]
                + [("wdT_sl", i, D) for i in range(8)])
    red = consts["red"].ap()
    red_done = []
    for j, (nm, i, w) in enumerate(a_slices):
        if j >= 2:
            P.wait("sync", "v", red_done[j - 2])
        v = P.dma("sync", a1_ring[j % 2].ap()[:, 0:w],
                  dt[nm][128 * i:128 * (i + 1), :], f"qa{j % 2}")
        P.wait("vector", f"qa{j % 2}", v)
        P.op("vector", lambda e, j=j, w=w: e.tensor_reduce(
            red[:, j:j + 1], a1_ring[j % 2].ap()[:, 0:w], AX.X, AL.add,
            apply_absolute_value=True), inc="v")
        red_done.append(P.now("v"))
    v_a1 = P.now("v")

    P.wait("gpsimd", "v", v_a1)
    v = P.dma("gpsimd", dt["ar_in"][:, 0:40], red[:, 0:40], "qr")
    P.wait("gpsimd", "qr", v)
    P.op("gpsimd", lambda e: e.collective_compute(
        "AllReduce", AL.add, replica_groups=[list(range(NCORES))],
        ins=[dt["ar_in"].opt()], outs=[dt["ar_out"].opt()]), inc="cc")
    P.wait("gpsimd", "cc", P.now("cc"))
    v = P.dma("gpsimd", dt["red_lin"][:].rearrange("(p k) -> p k", p=128),
              dt["ar_out"][:, :], "qr")
    P.wait("gpsimd", "qr", v)
    v = P.dma("gpsimd", red1.ap()[0:1, :], dt["red_lin"].rearrange("(o a) -> o a", o=1),
              "qr")
    P.wait("vector", "qr", v)
    rview = red1.ap()[0:1, :].rearrange("a (p k) -> a p k", k=40)
    P.op("vector", lambda e: e.tensor_reduce(
        red[0:1, 0:1], rview[:, :, 0:16], AX.XY, AL.add))
    P.op("vector", lambda e: e.tensor_reduce(
        red[0:1, 1:2], rview[:, :, 16:32], AX.XY, AL.add))
    P.op("vector", lambda e: e.tensor_reduce(
        red[0:1, 2:3], rview[:, :, 32:40], AX.XY, AL.add))
    P.drain("vector")
    P.op("vector", lambda e: e.tensor_scalar(
        red[0:1, 9:12], red[0:1, 0:3], 1.0 / (FF * D), EPS, AL.mult, AL.add))
    P.drain("vector")
    P.op("vector", lambda e: e.tensor_scalar(
        red[0:1, 3:6], red[0:1, 9:12], 0.5 * (1.0 - 6e-7), 0.0, AL.mult, AL.add))
    P.drain("vector")
    P.op("vector", lambda e: e.tensor_scalar(
        red[0:1, 6:9], red[0:1, 3:6], -1.0, 0.0, AL.mult, AL.add), inc="v")
    P.drain("vector")
    v_thr = P.now("v")
    P.wait("tensor", "qc", ld["ones1"])
    P.wait("tensor", "v", v_thr)
    P.op("tensor", lambda e: e.matmul(
        psE[:, 0:9], consts["ones1"].ap()[0:1, :], red[0:1, 3:12],
        start=True, stop=True), inc="t")
    t_bc = P.now("t")
    P.wait("scalar", "t", t_bc)
    s_thbc = P.op("scalar", lambda e: e.copy(thbc[:, 0:9], psE[:, 0:9]), inc="s")
    tap("thbc", thbc[:, 0:9], (128, 9), waits=[("s", s_thbc)])
    tap("red", red[:, 0:16], (128, 16), waits=[("s", s_thbc)])

    P.dma("gpsimd", wxT_bf.ap()[:, :, :],
          dt["wxT"].rearrange("(a p) b -> p a b", p=128), "qmw")
    P.dma("gpsimd", wzT_bf.ap()[:, :, :],
          dt["wzT"].rearrange("(a p) b -> p a b", p=128), "qmw")
    P.dma("gpsimd", wpT_bf.ap()[:, :, :],
          dt["wpT"].rearrange("(a p) b -> p a b", p=128), "qmw")
    P.dma("gpsimd", woT_bf.ap()[:, :, :],
          dt["woT"].rearrange("(a p) b -> p a b", p=128), "qmw")
    P.dma("gpsimd", wdt_bf.ap()[:, :], dt["wdtT_rep"], "qmw")
    g_wx = g_wz = g_wp = g_wo = g_wdt = P.now("qmw")

    hsTr = dt["hsT"].rearrange("(a p) b -> p a b", p=128)
    strip_v = []
    P.wait("gpsimd", "v", v_a1)
    for s in range(4):
        c0 = 512 * s
        w = 515
        v = P.dma("gpsimd", hsT_r[s % 2].ap()[:, :, 0:w], hsTr[:, :, c0:c0 + w],
                  f"qt{s % 2}")
        strip_v.append(v)
        if s == 1:
            break
    P.wait("tensor", "qmw", g_wx)
    P.wait("tensor", "qmw", g_wz)

    def mm_xz(ps_ap, w_bf, m, slot, col0, n):
        def fn(e):
            r = None
            for k in range(16):
                r = e.matmul(ps_ap, w_bf.ap()[:, k, 128 * m:128 * (m + 1)],
                             hsT_r[slot].ap()[:, k, col0:col0 + n],
                             start=(k == 0), stop=(k == 15))
            return r
        return fn

    strip_t = []
    for s in range(4):
        if s >= 2:
            P.wait("gpsimd", "t", strip_t[s - 2])
            c0 = 512 * s
            w = 515 if s < 3 else 515
            v = P.dma("gpsimd", hsT_r[s % 2].ap()[:, :, 0:w],
                      hsTr[:, :, c0:c0 + w], f"qt{s % 2}")
            strip_v.append(v)
        P.wait("tensor", f"qt{s % 2}", strip_v[s])
        for m in range(CG):
            P.op("tensor", mm_xz(psD[:, 512 * m:512 * m + 512], wxT_bf, m,
                                 s % 2, 0, 512), inc="t")
        t_x = P.now("t")
        P.wait("scalar", "t", t_x)
        for m in range(CG):
            P.op("scalar", lambda e, m=m, s=s: e.copy(
                x_fm.ap()[:, m, 512 * s:512 * s + 512],
                psD[:, 512 * m:512 * m + 512]), inc="s")
        s_xe = P.now("s")
        P.wait("tensor", "s", s_xe)
        for m in range(CG):
            P.op("tensor", mm_xz(psD[:, 512 * m:512 * m + 512], wzT_bf, m,
                                 s % 2, 3, 512), inc="t")
        t_z = P.now("t")
        P.wait("vector", "t", t_z)
        for m in range(CG):
            P.op("vector", lambda e, m=m, s=s: e.tensor_copy(
                z_bf.ap()[:, m, 512 * s:512 * s + 512],
                psD[:, 512 * m:512 * m + 512]), inc="v")
        v_z = P.now("v")
        P.wait("tensor", "v", v_z)
        strip_t.append(P.now("t"))
    for m in range(CG):
        def tailf(e, m=m):
            r = None
            for k in range(16):
                r = e.matmul(psA[:, 4 * m:4 * m + 3],
                             wxT_bf.ap()[:, k, 128 * m:128 * (m + 1)],
                             hsT_r[1].ap()[:, k, 512:515],
                             start=(k == 0), stop=(k == 15))
            return r
        P.op("tensor", tailf, inc="t")
    t_tail = P.now("t")
    P.wait("scalar", "t", t_tail)
    for m in range(CG):
        P.op("scalar", lambda e, m=m: e.copy(
            x_fm.ap()[:, m, T:T + 3], psA[:, 4 * m:4 * m + 3]), inc="s")
    s_xall = P.now("s")
    P.wait("tensor", "s", s_xall)

    P.wait("vector", "s", s_xall)
    P.wait("vector", "qc", ld["conv_w"])
    cw = consts["conv_w"].ap()
    for m in range(CG):
        P.op("vector", lambda e, m=m: e.tensor_scalar(
            acc_fm.ap()[:, m, :], x_fm.ap()[:, m, 0:T],
            cw[:, 4 * m:4 * m + 1], 0.0, AL.mult, AL.add), inc="v")
    P.drain("vector")
    for k in range(1, 4):
        for m in range(CG):
            P.op("vector", lambda e, m=m, k=k: e.scalar_tensor_tensor(
                acc_fm.ap()[:, m, :], x_fm.ap()[:, m, k:k + T],
                cw[:, 4 * m + k:4 * m + k + 1], acc_fm.ap()[:, m, :],
                AL.mult, AL.add), inc="v")
        P.drain("vector")
    v_conv = P.now("v")
    P.wait("scalar", "v", v_conv)
    P.wait("scalar", "qc", ld["conv_b"])
    for m in range(CG):
        P.op("scalar", lambda e, m=m: e.activation(
            x_act_bf.ap()[:, m, :], acc_fm.ap()[:, m, :], AF.Silu,
            bias=consts["conv_b"].ap()[:, m:m + 1], scale=1.0), inc="s")
    s_act = P.now("s")
    tap("x_act", x_act_bf.ap()[:, :, :], (128, CG, T), BF16, waits=[("s", s_act)])

    P.wait("tensor", "s", s_act)
    P.wait("tensor", "qmw", g_wp)

    def mm_par(ms, strip, ps_ap):
        def fn(e):
            r = None
            for k in range(CG):
                r = e.matmul(ps_ap, wpT_bf.ap()[:, k, 128 * ms:128 * (ms + 1)],
                             x_act_bf.ap()[:, k, 512 * strip:512 * strip + 512],
                             start=(k == 0), stop=(k == CG - 1))
            return r
        return fn

    for strip in range(4):
        P.op("tensor", mm_par(0, strip, psA), inc="t")
        for m in range(1, 5):
            P.op("tensor", mm_par(m, strip, psD[:, 512 * (m - 1):512 * m]), inc="t")
        t1 = P.now("t")
        P.wait("vector", "t", t1)
        P.op("vector", lambda e, strip=strip: e.tensor_copy(
            dt_bf.ap()[:, 512 * strip:512 * strip + 512], psA), inc="v")
        P.wait("scalar", "t", t1)
        for m in range(1, 5):
            P.op("scalar", lambda e, m=m, strip=strip: e.copy(
                b_fm.ap()[:, m - 1, 512 * strip:512 * strip + 512],
                psD[:, 512 * (m - 1):512 * m]), inc="s")
        P.wait("tensor", "v", P.now("v"))
        P.wait("tensor", "s", P.now("s"))
        for m in range(5, 9):
            P.op("tensor", mm_par(m, strip, psD[:, 512 * (m - 5):512 * (m - 4)]),
                 inc="t")
        t2 = P.now("t")
        P.wait("scalar", "t", t2)
        for m in range(5, 9):
            P.op("scalar", lambda e, m=m, strip=strip: e.copy(
                c_fm.ap()[:, m - 5, 512 * strip:512 * strip + 512],
                psD[:, 512 * (m - 5):512 * (m - 4)]), inc="s")
        P.wait("tensor", "s", P.now("s"))
    v_par = P.now("v")

    P.wait("tensor", "v", v_par)
    P.wait("tensor", "qmw", g_wdt)
    P.wait("scalar", "qc", ld["bdt"])
    for strip in range(4):
        for m in range(CG):
            P.op("tensor", lambda e, m=m, strip=strip: e.matmul(
                psE, wdt_bf.ap()[:, 128 * m:128 * (m + 1)],
                dt_bf.ap()[:, 512 * strip:512 * strip + 512],
                start=True, stop=True), inc="t")
            tm = P.now("t")
            P.wait("scalar", "t", tm)
            P.op("scalar", lambda e, m=m, strip=strip: e.activation(
                delta_fm.ap()[:, m, 512 * strip:512 * strip + 512], psE,
                AF.Exp, bias=consts["bdt"].ap()[:, m:m + 1], scale=1.0), inc="s")
            P.wait("tensor", "s", P.now("s"))
    P.drain("scalar")
    for m in range(CG):
        P.op("scalar", lambda e, m=m: e.activation(
            delta_fm.ap()[:, m, :], delta_fm.ap()[:, m, :], AF.Ln, bias=1.0),
            inc="s")
    P.drain("scalar")
    s_delta = P.now("s")
    tap("delta", delta_fm.ap()[:, :, :], (128, CG, T), waits=[("s", s_delta)])

    P.wait("vector", "s", s_delta)
    for m in range(CG):
        P.op("vector", lambda e, m=m: e.tensor_tensor(
            b_fm.ap()[:, m, :], b_fm.ap()[:, m, :], delta_fm.ap()[:, m, :],
            AL.mult), inc="v")
    P.drain("vector")
    for m in range(CG):
        P.op("vector", lambda e, m=m: e.tensor_tensor(
            b_fm.ap()[:, m, :], b_fm.ap()[:, m, :], x_act_bf.ap()[:, m, :],
            AL.mult), inc="v")
    P.drain("vector")
    for m in range(CG):
        P.op("vector", lambda e, m=m: e.tensor_scalar(
            delta_fm.ap()[:, m, :], delta_fm.ap()[:, m, :],
            consts["a_flat"].ap()[:, m:m + 1], 0.0, AL.mult, AL.add), inc="v")
    P.drain("vector")
    for m in range(CG):
        P.op("vector", lambda e, m=m: e.tensor_reduce(
            consts["ptot"].ap()[:, m:m + 1], delta_fm.ap()[:, m, :], AX.X,
            AL.add), inc="v")
    P.drain("vector")
    v_su = P.now("v")
    if "slog" in TAPS:
        dslog = nc.dram_tensor("tap_slog", [128, CG, T], F32,
                               kind="ExternalOutput").ap()
        P.wait("gpsimd", "v", v_su)
        qv = P.dma("gpsimd", dslog, delta_fm.ap()[:, :, :], "qs")
        P.wait("scalar", "qs", qv)
    P.wait("scalar", "v", v_su)
    P.op("scalar", lambda e: e.activation(
        consts["ptot"].ap()[:, :], consts["ptot"].ap()[:, :], AF.Exp), inc="s")
    for m in range(CG):
        P.op("scalar", lambda e, m=m: e.activation(
            delta_fm.ap()[:, m, :], delta_fm.ap()[:, m, :], AF.Exp), inc="s")
    P.drain("scalar")
    s_abar = P.now("s")

    tap("bu", b_fm.ap()[:, :, :], (128, CG, T), waits=[("v", v_su)])
    tap("aflat", consts["a_flat"].ap()[:, :], (128, CG), waits=[("v", v_su)])
    tap("abar", delta_fm.ap()[:, :, :], (128, CG, T), waits=[("s", s_abar)])
    P.wait("vector", "s", s_abar)
    P.wait("vector", "qc", ld["zcol"])
    for m in range(CG):
        P.op("vector", lambda e, m=m: e.tensor_tensor_scan(
            h_fm.ap()[:, m, :], delta_fm.ap()[:, m, :], b_fm.ap()[:, m, :],
            consts["zcol"].ap()[:, 0:1], AL.mult, AL.add), inc="v")
    P.drain("vector")
    v_scan1 = P.now("v")

    tap("h1", h_fm.ap()[:, :, :], (128, CG, T), waits=[("v", v_scan1)])
    P.wait("gpsimd", "v", v_scan1)
    P.dma("gpsimd", dt["ags_in"][0:512].rearrange("(g p) -> p g", p=128),
          consts["ptot"].ap()[:, 0:CG], "qx", slow=True)
    v = P.dma("gpsimd", dt["ags_in"][512:1024].rearrange("(g p) -> p g", p=128),
              h_fm.ap()[:, :, T - 1:T].rearrange("p g a -> p (g a)"), "qx",
              slow=True)
    P.wait("gpsimd", "qx", v)
    P.op("gpsimd", lambda e: e.collective_compute(
        "AllGather", AL.bypass, replica_groups=[list(range(NCORES))],
        ins=[dt["ags_in"].opt()], outs=[dt["ags_out"].opt()]), inc="cc")
    P.wait("gpsimd", "cc", P.now("cc"))
    v_ag = P.dma("gpsimd", consts["ags_sb"].ap()[0:8, :], dt["ags_out"][:, :],
                 "qx")
    P.wait("tensor", "qx", v_ag)
    P.wait("tensor", "qc", ld["ident32"])
    for kk in range(8):
        P.op("tensor", lambda e, kk=kk: e.transpose(
            psD[:, 8 * kk:8 * kk + 8],
            consts["ags_sb"].ap()[0:8, 128 * kk:128 * (kk + 1)],
            ID32[0:8, 0:8]), inc="t")
    t_ag = P.now("t")
    P.wait("vector", "t", t_ag)
    al = consts["allst"].ap()
    P.op("vector", lambda e: e.tensor_copy(
        al[:, :, :, :], psD[:, 0:64].rearrange("p (k g j) -> p j k g", k=2, g=4)),
        inc="v")
    P.drain("vector")
    hp = consts["hprev"].ap()
    P.op("vector", lambda e: e.memset(hp[:, 1, :], 0.0), inc="v")
    P.drain("vector")
    P.wait("vector", "qc", ld["pmask1m"])
    for j in range(NCORES):
        P.op("vector", lambda e, j=j: e.tensor_scalar(
            al[:, j, 0, :], al[:, j, 0, :], consts["pmask"].ap()[:, j:j + 1],
            consts["pmask1m"].ap()[:, j:j + 1], AL.mult, AL.add), inc="v")
        P.drain("vector")
        P.op("vector", lambda e, j=j: e.tensor_tensor(
            hp[:, 1, :], hp[:, 1, :], al[:, j, 0, :], AL.mult), inc="v")
        P.drain("vector")
        P.op("vector", lambda e, j=j: e.scalar_tensor_tensor(
            hp[:, 1, :], al[:, j, 1, :], consts["pmask"].ap()[:, j:j + 1],
            hp[:, 1, :], AL.mult, AL.add), inc="v")
        P.drain("vector")
    v_fold = P.now("v")
    P.wait("scalar", "v", v_fold)
    s_hp = P.op("scalar", lambda e: e.copy(hp[:, 0, :], hp[:, 1, :]), inc="s")

    tap("hprev", hp[:, :, :], (128, 2, CG), waits=[("s", s_hp)])
    tap("allst", al[:, :, :, :], (128, NCORES, 2, CG), waits=[("v", v_fold)])
    tap("ptot", consts["ptot"].ap()[:, :], (128, CG), waits=[("s", s_abar)])
    P.wait("vector", "s", s_hp)
    for m in range(CG):
        P.op("vector", lambda e, m=m: e.tensor_tensor_scan(
            h_fm.ap()[:, m, :], delta_fm.ap()[:, m, :], b_fm.ap()[:, m, :],
            hp[:, 0, m:m + 1], AL.mult, AL.add), inc="v")
    P.drain("vector")
    v_scan2 = P.now("v")
    tap("h", h_fm.ap()[:, :, :], (128, CG, T), waits=[("v", v_scan2)])

    P.wait("scalar", "v", v_scan2)
    for m in range(CG):
        P.op("scalar", lambda e, m=m: e.activation(
            b_fm.ap()[:, m, :], z_bf.ap()[:, m, :], AF.Silu), inc="s")
    s_sz = P.now("s")
    for m in range(CG):
        P.op("vector", lambda e, m=m: e.tensor_tensor(
            c_fm.ap()[:, m, :], c_fm.ap()[:, m, :], h_fm.ap()[:, m, :], AL.mult),
            inc="v")
    P.drain("vector")
    P.wait("vector", "qc", ld["d_vec"])
    for m in range(CG):
        P.op("vector", lambda e, m=m: e.scalar_tensor_tensor(
            c_fm.ap()[:, m, :], x_act_bf.ap()[:, m, :],
            consts["d_vec"].ap()[:, m:m + 1], c_fm.ap()[:, m, :],
            AL.mult, AL.add), inc="v")
    P.drain("vector")
    P.wait("vector", "s", s_sz)
    for m in range(CG):
        P.op("vector", lambda e, m=m: e.tensor_tensor(
            y_bf.ap()[:, m, :], c_fm.ap()[:, m, :], b_fm.ap()[:, m, :], AL.mult),
            inc="v")
    P.drain("vector")
    v_y = P.now("v")
    tap("y", y_bf.ap()[:, :, :], (128, CG, T), BF16, waits=[("v", v_y)])

    if STOP_AFTER == "mamba":
        P.wait("gpsimd", "v", v_y)
        P.wait("gpsimd", "qs", P.now("qs"))
        return _finish(nc, P, dt)
    P.wait("sync", "v", v_y)
    P.wait("vector", "s", s_thbc)
    a2_store = []
    vq_done = []
    for j, (nm, i, w) in enumerate(a_slices):
        col = {"wgT_sl": 0, "wuT_sl": 1, "wdT_sl": 2}[nm]
        agname = {"wgT_sl": "agw_in_g", "wuT_sl": "agw_in_u",
                  "wdT_sl": "agw_in_d"}[nm]
        if j >= 2:
            P.wait("sync", "v", vq_done[j - 2])
        v = P.dma("sync", a2_ring[j % 2].ap()[:, 0:w],
                  dt[nm][128 * i:128 * (i + 1), :], f"qa{j % 2}")
        P.wait("vector", f"qa{j % 2}", v)
        if j >= 2:
            P.wait("vector", a2_store[j - 2][0], a2_store[j - 2][1])
        P.op("vector", lambda e, j=j, w=w, col=col: e.tensor_scalar(
            a2_sc.ap()[:, 0:w], a2_ring[j % 2].ap()[:, 0:w],
            thbc[:, col:col + 1], 0.0, AL.is_gt, AL.add), inc="v")
        P.op("vector", lambda e, j=j, w=w, col=col: e.tensor_scalar(
            a2_ring[j % 2].ap()[:, 0:w], a2_ring[j % 2].ap()[:, 0:w],
            thbc[:, 3 + col:4 + col], 0.0, AL.is_lt, AL.add), inc="v")
        P.drain("vector")
        P.op("vector", lambda e, j=j, w=w: e.tensor_tensor(
            a2_out[j % 2].ap()[:, 0:w], a2_sc.ap()[:, 0:w],
            a2_ring[j % 2].ap()[:, 0:w], AL.subtract), inc="v")
        P.drain("vector")
        vq_done.append(P.now("v"))
        P.wait("gpsimd", "v", vq_done[j])
        qo = f"qo{j % 2}"
        if nm == "wdT_sl":
            v = P.dma("gpsimd", dt[agname][128 * i:128 * (i + 1), :],
                      a2_out[j % 2].ap()[:, 0:w], qo)
        else:
            for jc in range(4):
                v = P.dma("gpsimd", dt[agname][jc, :, i, :],
                          a2_out[j % 2].ap()[:, 256 * jc:256 * jc + 256], qo)
        a2_store.append((qo, v))

    P.wait("gpsimd", a2_store[14][0], a2_store[14][1])
    P.wait("gpsimd", a2_store[15][0], a2_store[15][1])
    P.op("gpsimd", lambda e: e.collective_compute(
        "AllGather", AL.bypass, replica_groups=[list(range(NCORES))],
        ins=[dt["agw_in_g"].opt()], outs=[dt["agw_out_g"].opt()]), inc="cc")
    cc_g = P.now("cc")
    P.wait("gpsimd", a2_store[30][0], a2_store[30][1])
    P.wait("gpsimd", a2_store[31][0], a2_store[31][1])
    P.op("gpsimd", lambda e: e.collective_compute(
        "AllGather", AL.bypass, replica_groups=[list(range(NCORES))],
        ins=[dt["agw_in_u"].opt()], outs=[dt["agw_out_u"].opt()]), inc="cc")
    cc_u = P.now("cc")
    P.wait("gpsimd", a2_store[38][0], a2_store[38][1])
    P.wait("gpsimd", a2_store[39][0], a2_store[39][1])
    P.op("gpsimd", lambda e: e.collective_compute(
        "AllGather", AL.bypass, replica_groups=[list(range(NCORES))],
        ins=[dt["agw_in_d"].opt()], outs=[dt["agw_out_d"].opt()]), inc="cc")
    cc_d = P.now("cc")
    tap("wgq", dt["agw_out_g"][:, :, :, :, :], (NCORES, 4, 128, 16, 256), F8, waits=[("cc", cc_d)])
    tap("wdq", dt["agw_out_d"][:, :], (FF, D), F8, waits=[("cc", cc_d)])

    if STOP_AFTER == "a2":
        P.wait("gpsimd", "cc", cc_d)
        P.wait("gpsimd", "qs", P.now("qs"))
        return _finish(nc, P, dt)
    wgTr = dt["agw_out_g"]
    wuTr = dt["agw_out_u"]
    RC = 1.0 / D

    v_a2 = P.now("v")

    gu_consumed = {}
    wd_consumed = {}
    prev_gu = []
    prev_wd = []

    def load_gu(key, src, c):
        slot = len(prev_gu) % 2
        if len(prev_gu) == 0:
            P.wait("sync", "cc", cc_g)
        elif len(prev_gu) == 1:
            P.wait("sync", "cc", cc_u)
        if len(prev_gu) >= 2:
            P.wait("sync", "t", gu_consumed[prev_gu[-2]])
        b0 = 2 * c
        P.dma("sync", gu_ring[slot].ap()[:, :, 0:256],
              src[b0 // 4, b0 % 4, :, :, :], f"qu{slot}")
        v = P.dma("sync", gu_ring[slot].ap()[:, :, 256:512],
                  src[(b0 + 1) // 4, (b0 + 1) % 4, :, :, :], f"qu{slot}")
        prev_gu.append(key)
        return slot, v

    def load_wd(key, k):
        slot = len(prev_wd) % 4
        eng = "gpsimd" if slot % 2 == 0 else "sync"
        if len(prev_wd) >= 4:
            P.wait(eng, "t", wd_consumed[prev_wd[-4]])
        v = P.dma(eng, wd_ring[slot].ap()[:, :],
                  dt["agw_out_d"][128 * k:128 * (k + 1), :], f"qd{slot}")
        prev_wd.append(key)
        return slot, v

    hs_cons = [v_a2]

    def load_hs(tg):
        P.wait("gpsimd", "v", hs_cons[0])
        return P.dma("gpsimd", hs_sb.ap()[:, :],
                     dt["hs"][128 * tg:128 * (tg + 1), :], "qh")

    def vst(fn):
        P.op("vector", fn, inc="v")
        P.drain("vector")

    P.wait("tensor", "qmw", g_wo)
    P.wait("tensor", "v", v_y)
    P.wait("sync", "cc", cc_d)
    P.wait("gpsimd", "cc", cc_d)
    prev_qs = [P.now("qs")]

    STAGE = 3 * D

    def emit_wout(gx, tt):
        tg = gx * NTT + tt

        def wof(e, tg=tg):
            r = None
            for n in range(4):
                for k in range(CG):
                    r = e.matmul(psD[:, 512 * n:512 * n + 512],
                                 y_bf.ap()[:, k, 128 * tg:128 * (tg + 1)],
                                 woT_bf.ap()[:, k, 512 * n:512 * n + 512],
                                 start=(k == 0), stop=(k == CG - 1))
            return r
        P.op("tensor", wof, inc="t")
        return P.now("t")

    def emit_r1evac(gx, tt, dst, v_hs, t_wo):
        tq = tt + NTT * (gx % 2)
        P.wait("vector", "qh", v_hs)
        P.wait("vector", "t", t_wo)
        vst(lambda e, tq=tq, dst=dst: e.scalar_tensor_tensor(
            dst, psD[:, 0:2048], 1.0, hs_sb.ap()[:, :],
            AL.mult, AL.add, accum_out=ST[:, tq, S_SUM:S_SUM + 1]))
        hs_cons[0] = P.now("v")
        return P.now("v")

    def emit_ln1(gx, src, t_gate=None, v_gate=None):
        if t_gate is not None:
            P.wait("scalar", "t", t_gate)
        if v_gate is not None:
            P.wait("scalar", "v", v_gate)
        for tt in range(NTT):
            tq = tt + NTT * (gx % 2)
            for hh in range(2):
                P.op("scalar", lambda e, tq=tq, tt=tt, hh=hh: e.activation(
                    sq_tr.ap()[:, :], src(tt, slice(1024 * hh, 1024 * (hh + 1))),
                    AF.Square, accum_out=ST[:, tq, S_Q1 + hh:S_Q1 + hh + 1]),
                    inc="s")
                P.drain("scalar")
        s_sq = P.now("s")
        for tt in range(NTT):
            tq = tt + NTT * (gx % 2)
            vst(lambda e, tq=tq, tt=tt: e.tensor_reduce(
                ST[:, tq, S_MAX:S_MAX + 1], src(tt, slice(0, 2048)), AX.X, AL.max))
            vst(lambda e, tq=tq, tt=tt: e.tensor_reduce(
                ST[:, tq, S_MIN:S_MIN + 1], src(tt, slice(0, 2048)), AX.X, AL.min))
        P.wait("vector", "s", s_sq)
        for tt in range(NTT):
            tq = tt + NTT * (gx % 2)
            vst(lambda e, tq=tq: e.tensor_tensor(
                ST[:, tq, S_SSQ:S_SSQ + 1], ST[:, tq, S_Q1:S_Q1 + 1],
                ST[:, tq, S_Q2:S_Q2 + 1], AL.add))
            vst(lambda e, tq=tq: e.tensor_scalar(
                ST[:, tq, S_MU:S_MU + 1], ST[:, tq, S_SUM:S_SUM + 1], RC, 0.0,
                AL.mult, AL.add))
            vst(lambda e, tq=tq: e.tensor_tensor(
                ST[:, tq, S_T1:S_T1 + 1], ST[:, tq, S_MU:S_MU + 1],
                ST[:, tq, S_MU:S_MU + 1], AL.mult))
            vst(lambda e, tq=tq: e.scalar_tensor_tensor(
                ST[:, tq, S_VAR:S_VAR + 1], ST[:, tq, S_SSQ:S_SSQ + 1], RC,
                ST[:, tq, S_T1:S_T1 + 1], AL.mult, AL.subtract))
        v_var = P.now("v")
        P.wait("scalar", "v", v_var)
        for tt in range(NTT):
            tq = tt + NTT * (gx % 2)
            P.op("scalar", lambda e, tq=tq: e.activation(
                ST[:, tq, S_DEN:S_DEN + 1], ST[:, tq, S_VAR:S_VAR + 1], AF.Sqrt,
                bias=EPS, scale=1.0), inc="s")
        s_den = P.now("s")
        P.wait("vector", "s", s_den)
        for tt in range(NTT):
            tq = tt + NTT * (gx % 2)
            vst(lambda e, tq=tq: e.reciprocal(
                ST[:, tq, S_T1:S_T1 + 1], ST[:, tq, S_DEN:S_DEN + 1]))
            vst(lambda e, tq=tq: e.tensor_tensor(
                ST[:, tq, S_T2:S_T2 + 1], ST[:, tq, S_DEN:S_DEN + 1],
                ST[:, tq, S_T1:S_T1 + 1], AL.mult))
            vst(lambda e, tq=tq: e.tensor_scalar(
                ST[:, tq, S_T2:S_T2 + 1], ST[:, tq, S_T2:S_T2 + 1], -1.0, 2.0,
                AL.mult, AL.add))
            vst(lambda e, tq=tq: e.tensor_tensor(
                ST[:, tq, S_RHO:S_RHO + 1], ST[:, tq, S_T1:S_T1 + 1],
                ST[:, tq, S_T2:S_T2 + 1], AL.mult))
            vst(lambda e, tq=tq: e.tensor_tensor(
                ST[:, tq, S_T1:S_T1 + 1], ST[:, tq, S_MAX:S_MAX + 1],
                ST[:, tq, S_MU:S_MU + 1], AL.subtract))
            vst(lambda e, tq=tq: e.tensor_tensor(
                ST[:, tq, S_T2:S_T2 + 1], ST[:, tq, S_MU:S_MU + 1],
                ST[:, tq, S_MIN:S_MIN + 1], AL.subtract))
            vst(lambda e, tq=tq: e.tensor_tensor(
                ST[:, tq, S_MX:S_MX + 1], ST[:, tq, S_T1:S_T1 + 1],
                ST[:, tq, S_T2:S_T2 + 1], AL.max))
            vst(lambda e, tq=tq: e.tensor_tensor(
                ST[:, tq, S_T1:S_T1 + 1], ST[:, tq, S_MX:S_MX + 1],
                ST[:, tq, S_RHO:S_RHO + 1], AL.mult))
            vst(lambda e, tq=tq: e.tensor_scalar(
                ST[:, tq, S_SX:S_SX + 1], ST[:, tq, S_T1:S_T1 + 1], 1.0 / 127.0,
                EPS, AL.mult, AL.add))
            vst(lambda e, tq=tq: e.reciprocal(
                ST[:, tq, S_T1:S_T1 + 1], ST[:, tq, S_SX:S_SX + 1]))
            vst(lambda e, tq=tq: e.tensor_tensor(
                ST[:, tq, S_T2:S_T2 + 1], ST[:, tq, S_SX:S_SX + 1],
                ST[:, tq, S_T1:S_T1 + 1], AL.mult))
            vst(lambda e, tq=tq: e.tensor_scalar(
                ST[:, tq, S_T2:S_T2 + 1], ST[:, tq, S_T2:S_T2 + 1], -1.0, 2.0,
                AL.mult, AL.add))
            vst(lambda e, tq=tq: e.tensor_tensor(
                ST[:, tq, S_RQ:S_RQ + 1], ST[:, tq, S_T1:S_T1 + 1],
                ST[:, tq, S_T2:S_T2 + 1], AL.mult))
            vst(lambda e, tq=tq: e.tensor_tensor(
                ST[:, tq, S_AA:S_AA + 1], ST[:, tq, S_RHO:S_RHO + 1],
                ST[:, tq, S_RQ:S_RQ + 1], AL.mult))
            vst(lambda e, tq=tq: e.tensor_tensor(
                ST[:, tq, S_BB:S_BB + 1], ST[:, tq, S_MU:S_MU + 1],
                ST[:, tq, S_AA:S_AA + 1], AL.mult))
            vst(lambda e, tq=tq: e.tensor_tensor(
                ST[:, tq, S_SG:S_SG + 1], ST[:, tq, S_SX:S_SX + 1],
                thbc[:, 6:7], AL.mult))
            vst(lambda e, tq=tq: e.tensor_tensor(
                ST[:, tq, S_SU:S_SU + 1], ST[:, tq, S_SX:S_SX + 1],
                thbc[:, 7:8], AL.mult))
            vst(lambda e, tq=tq: e.tensor_tensor(
                ST[:, tq, S_MURHO:S_MURHO + 1], ST[:, tq, S_MU:S_MU + 1],
                ST[:, tq, S_RHO:S_RHO + 1], AL.mult))
        v_stats = P.now("v")
        P.wait("scalar", "v", v_stats)
        P.op("scalar", lambda e: e.copy(
            ST[:, :, S_SA:S_SA + 2], ST[:, :, S_AA:S_AA + 2]), inc="s")
        P.wait("vector", "s", P.now("s"))

    def emit_xq(gx, src, t_gate=None):
        if t_gate is not None:
            P.wait("vector", "t", t_gate)
        for tt in range(NTT):
            tq = tt + NTT * (gx % 2)
            for hh in range(2):
                cl = slice(1024 * hh, 1024 * (hh + 1))
                vst(lambda e, tq=tq, tt=tt, hh=hh, cl=cl: e.tensor_scalar(
                    c_sc.ap()[:, :], src(tt, cl),
                    ST[:, tq, S_SA:S_SA + 1], ST[:, tq, S_SB2:S_SB2 + 1],
                    AL.mult, AL.subtract))
                vst(lambda e: e.tensor_copy(ui_sc.ap()[:, :], c_sc.ap()[:, :]))
                vst(lambda e, tt=tt, cl=cl: e.tensor_copy(
                    xq_tm.ap()[:, tt, cl], ui_sc.ap()[:, :]))
        return P.now("v")

    def emit_xqT(v_xq, v_bank=None):
        P.wait("tensor", "v", v_xq)
        if v_bank is not None:
            P.wait("tensor", "v", v_bank)
        for tt in range(NTT):
            for kb in range(2):
                for j in range(8):
                    k = kb * 8 + j
                    P.op("tensor", lambda e, tt=tt, k=k, j=j: e.transpose(
                        psT[:, 128 * j:128 * (j + 1)],
                        xq_tm.ap()[:, tt, 128 * k:128 * (k + 1)], ID), inc="t")
                t_tr = P.now("t")
                P.wait("scalar", "t", t_tr)
                P.op("scalar", lambda e, tt=tt, kb=kb: e.copy(
                    xqT.ap()[:, 8 * kb:8 * kb + 8, 128 * tt:128 * (tt + 1)],
                    psT[:, 0:1024].rearrange("p (j b) -> p j b", j=8)), inc="s")
                P.wait("tensor", "s", P.now("s"))
        return P.now("s")

    v_hs = load_hs(0)
    t_wo = emit_wout(0, 0)
    v_r1 = emit_r1evac(0, 0, r1_t.ap()[:, 0, :], v_hs, t_wo)
    v_hs = load_hs(1)
    P.wait("tensor", "v", v_r1)
    t_wo = emit_wout(0, 1)
    v_r1 = emit_r1evac(0, 1, r1_t.ap()[:, 1, :], v_hs, t_wo)

    def src_r1t(tt, cl):
        return r1_t.ap()[:, tt, cl]

    def src_stage(tt, cl):
        return m_ar.ap()[:, tt, STAGE + cl.start:STAGE + cl.stop]

    emit_ln1(0, src_r1t, v_gate=v_r1)
    v_xq = emit_xq(0, src_r1t)
    s_xqT = {0: emit_xqT(v_xq)}
    ln2v = {-1: (0, 0)}

    for g in range(NG):
        gp = g % 2
        last = (g == NG - 1)

        P.wait("tensor", "s", s_xqT[g])
        if g >= 1:
            P.wait("tensor", "v", ln2v[g - 1][0])
        if not last:
            v_hs = load_hs((g + 1) * NTT + 0)
        sil_prev = [0]
        mstt_prev = [0]
        for c in range(16):
            slot_g, vg = load_gu((g, 2 * c), wgTr, c)
            slot_u, vu = load_gu((g, 2 * c + 1), wuTr, c)
            if c == 0:
                P.wait("vector", "qs", prev_qs[0])
            P.wait("tensor", f"qu{slot_g}", vg)
            P.wait("tensor", "s", sil_prev[0])
            for tt in range(NTT):
                ps = psA if tt == 0 else psB
                def gmf(e, tt=tt, ps=ps, slot=slot_g):
                    r = None
                    for k in range(16):
                        r = e.matmul(ps[:, 0:512],
                                     xqT.ap()[:, k, 128 * tt:128 * (tt + 1)],
                                     gu_ring[slot].ap()[:, k, :],
                                     start=(k == 0), stop=(k == 15))
                    return r
                P.op("tensor", gmf, inc="t")
            gu_consumed[(g, 2 * c)] = P.now("t")
            t_gate = P.now("t")
            P.wait("scalar", "t", t_gate)
            for tt in range(NTT):
                ps = psA if tt == 0 else psB
                tq = tt + NTT * gp
                sgi = 2 * (c % 2) + tt
                P.op("scalar", lambda e, tq=tq, ps=ps, sgi=sgi: e.activation(
                    sg_sl[sgi].ap()[:, 0:512], ps[:, 0:512],
                    AF.Silu, scale=ST[:, tq, S_SG:S_SG + 1]), inc="s")
            sil_prev[0] = P.now("s")
            P.wait("tensor", f"qu{slot_u}", vu)
            P.wait("tensor", "v", mstt_prev[0])
            for tt in range(NTT):
                ps = psE if tt == 0 else psU
                def umf(e, tt=tt, ps=ps, slot=slot_u):
                    r = None
                    for k in range(16):
                        r = e.matmul(ps[:, 0:512],
                                     xqT.ap()[:, k, 128 * tt:128 * (tt + 1)],
                                     gu_ring[slot].ap()[:, k, :],
                                     start=(k == 0), stop=(k == 15))
                    return r
                P.op("tensor", umf, inc="t")
            gu_consumed[(g, 2 * c + 1)] = P.now("t")
            t_up = P.now("t")
            P.wait("vector", "t", t_up)
            P.wait("vector", "s", sil_prev[0])
            for tt in range(NTT):
                ps = psE if tt == 0 else psU
                tq = tt + NTT * gp
                sgi = 2 * (c % 2) + tt
                P.op("vector", lambda e, tq=tq, tt=tt, ps=ps, c=c, sgi=sgi:
                     e.scalar_tensor_tensor(
                         m_ar.ap()[:, tt, 512 * c:512 * c + 512], ps[:, 0:512],
                         ST[:, tq, S_SU:S_SU + 1], sg_sl[sgi].ap()[:, 0:512],
                         AL.mult, AL.mult), inc="v")
            mstt_prev[0] = P.now("v")
        P.drain("vector")

        if not last:
            P.wait("tensor", "v", ln2v[g - 1][1])
            t_wo0 = emit_wout(g + 1, 0)

        for tt in range(NTT):
            tq = tt + NTT * gp
            vst(lambda e, tq=tq, tt=tt: e.tensor_reduce(
                ST[:, tq, S_MM:S_MM + 1], m_ar.ap()[:, tt, :], AX.X, AL.max,
                apply_absolute_value=True))
            vst(lambda e, tq=tq: e.tensor_scalar(
                ST[:, tq, S_SM:S_SM + 1], ST[:, tq, S_MM:S_MM + 1], 1.0 / 127.0,
                EPS, AL.mult, AL.add))
            vst(lambda e, tq=tq: e.reciprocal(
                ST[:, tq, S_T1:S_T1 + 1], ST[:, tq, S_SM:S_SM + 1]))
            vst(lambda e, tq=tq: e.tensor_tensor(
                ST[:, tq, S_T2:S_T2 + 1], ST[:, tq, S_SM:S_SM + 1],
                ST[:, tq, S_T1:S_T1 + 1], AL.mult))
            vst(lambda e, tq=tq: e.tensor_scalar(
                ST[:, tq, S_T2:S_T2 + 1], ST[:, tq, S_T2:S_T2 + 1], -1.0, 2.0,
                AL.mult, AL.add))
            vst(lambda e, tq=tq: e.tensor_tensor(
                ST[:, tq, S_RQM:S_RQM + 1], ST[:, tq, S_T1:S_T1 + 1],
                ST[:, tq, S_T2:S_T2 + 1], AL.mult))
            vst(lambda e, tq=tq: e.tensor_tensor(
                ST[:, tq, S_SMW:S_SMW + 1], ST[:, tq, S_SM:S_SM + 1],
                thbc[:, 8:9], AL.mult))
        v_msc = P.now("v")
        P.wait("scalar", "v", v_msc)
        P.op("scalar", lambda e: e.copy(
            ST[:, :, S_SA:S_SA + 1], ST[:, :, S_RQM:S_RQM + 1]), inc="s")
        P.wait("vector", "s", P.now("s"))

        tr_prev = [0]

        def qchunk(tt, cc):
            tq = tt + NTT * gp
            cl = slice(1024 * cc, 1024 * (cc + 1))
            vst(lambda e, tq=tq, tt=tt, cl=cl: e.tensor_scalar(
                m_ar.ap()[:, tt, cl], m_ar.ap()[:, tt, cl],
                ST[:, tq, S_SA:S_SA + 1], 0.0, AL.mult, AL.add))
            vst(lambda e, tt=tt, cl=cl: e.tensor_copy(
                ui_sc.ap()[:, :], m_ar.ap()[:, tt, cl]))
            P.wait("vector", "t", tr_prev[0])
            vst(lambda e: e.tensor_copy(mqs.ap()[:, :], ui_sc.ap()[:, :]))
            v_mq = P.now("v")
            P.wait("tensor", "v", v_mq)
            for j in range(8):
                P.op("tensor", lambda e, j=j: e.transpose(
                    psT[:, 128 * j:128 * (j + 1)],
                    mqs.ap()[:, 128 * j:128 * (j + 1)], ID), inc="t")
            tr_prev[0] = P.now("t")
            P.wait("scalar", "t", tr_prev[0])
            P.op("scalar", lambda e, tt=tt, cc=cc: e.copy(
                mqT.ap()[:, tt, 8 * cc:8 * cc + 8, :],
                psT[:, 0:1024].rearrange("p (j b) -> p j b", j=8)), inc="s")
            P.wait("tensor", "s", P.now("s"))

        for cc in range(8):
            qchunk(0, cc)
        if not last:
            v_r1e = emit_r1evac(g + 1, 0, m_ar.ap()[:, 0, STAGE:STAGE + 2048],
                                v_hs, t_wo0)
            v_hs = load_hs((g + 1) * NTT + 1)
            P.wait("tensor", "v", v_r1e)
            t_wo1 = emit_wout(g + 1, 1)
        for cc in range(8):
            qchunk(1, cc)
        if not last:
            v_r1e = emit_r1evac(g + 1, 1, m_ar.ap()[:, 1, STAGE:STAGE + 2048],
                                v_hs, t_wo1)

        if not last:
            P.wait("tensor", "v", v_r1e)
        t_k16 = t_k48 = None
        for k in range(64):
            slot, vw = load_wd((g, k), k)
            P.wait("tensor", f"qd{slot}", vw)
            for tt in range(NTT):
                ps = psD if tt == 0 else psD2
                for n in range(4):
                    P.op("tensor", lambda e, tt=tt, k=k, n=n, ps=ps, slot=slot:
                         e.matmul(ps[:, 512 * n:512 * n + 512],
                                  mqT.ap()[:, tt, k, :],
                                  wd_ring[slot].ap()[:, 512 * n:512 * n + 512],
                                  start=(k == 0), stop=(k == 63)), inc="t")
            wd_consumed[(g, k)] = P.now("t")
            if k == 15:
                t_k16 = P.now("t")
            elif k == 47:
                t_k48 = P.now("t")
        t_down = P.now("t")

        if not last:
            emit_ln1(g + 1, src_stage, t_gate=t_k16, v_gate=v_r1e)
            v_xq = emit_xq(g + 1, src_stage, t_gate=t_k48)

        P.wait("vector", "t", t_down)
        marks = {}
        for tt in (1, 0):
            tq = tt + NTT * gp
            ps = psD if tt == 0 else psD2
            for hh in range(2):
                cl = slice(1024 * hh, 1024 * (hh + 1))
                vst(lambda e, tq=tq, tt=tt, ps=ps, cl=cl: e.tensor_scalar(
                    c_sc.ap()[:, :], ps[:, cl],
                    ST[:, tq, S_SMW:S_SMW + 1], ST[:, tq, S_MURHO:S_MURHO + 1],
                    AL.mult, AL.subtract))
                vst(lambda e, tq=tq, tt=tt, hh=hh, cl=cl:
                    e.scalar_tensor_tensor(
                        m_ar.ap()[:, tt, cl], r1_t.ap()[:, tt, cl],
                        ST[:, tq, S_RHO:S_RHO + 1], c_sc.ap()[:, :],
                        AL.mult, AL.add,
                        accum_out=ST[:, tq, S_T1 + hh:S_T1 + hh + 1]))
            vst(lambda e, tq=tq: e.tensor_tensor(
                ST[:, tq, S_SUM2:S_SUM2 + 1], ST[:, tq, S_T1:S_T1 + 1],
                ST[:, tq, S_T2:S_T2 + 1], AL.add))
            marks[tt] = P.now("v")
            if tt == 1 and not last:
                s_xqT[g + 1] = emit_xqT(v_xq, v_bank=marks[1])
        ln2v[g] = (marks[1], marks[0])

        if not last:
            for tt in range(NTT):
                vst(lambda e, tt=tt: e.tensor_copy(
                    r1_t.ap()[:, tt, :], m_ar.ap()[:, tt, STAGE:STAGE + 2048]))

        P.wait("scalar", "v", marks[0])
        for tt in range(NTT):
            tq = tt + NTT * gp
            for hh in range(2):
                P.op("scalar", lambda e, tq=tq, tt=tt, hh=hh: e.activation(
                    sq_tr.ap()[:, :],
                    m_ar.ap()[:, tt, 1024 * hh:1024 * (hh + 1)],
                    AF.Square, accum_out=ST[:, tq, S_Q1 + hh:S_Q1 + hh + 1]),
                    inc="s")
                P.drain("scalar")
        s_sq2 = P.now("s")
        P.wait("vector", "s", s_sq2)
        for tt in range(NTT):
            tq = tt + NTT * gp
            vst(lambda e, tq=tq: e.tensor_tensor(
                ST[:, tq, S_SSQ:S_SSQ + 1], ST[:, tq, S_Q1:S_Q1 + 1],
                ST[:, tq, S_Q2:S_Q2 + 1], AL.add))
            vst(lambda e, tq=tq: e.tensor_scalar(
                ST[:, tq, S_MU:S_MU + 1], ST[:, tq, S_SUM2:S_SUM2 + 1], RC, 0.0,
                AL.mult, AL.add))
            vst(lambda e, tq=tq: e.tensor_tensor(
                ST[:, tq, S_T1:S_T1 + 1], ST[:, tq, S_MU:S_MU + 1],
                ST[:, tq, S_MU:S_MU + 1], AL.mult))
            vst(lambda e, tq=tq: e.scalar_tensor_tensor(
                ST[:, tq, S_VAR:S_VAR + 1], ST[:, tq, S_SSQ:S_SSQ + 1], RC,
                ST[:, tq, S_T1:S_T1 + 1], AL.mult, AL.subtract))
        v_var2 = P.now("v")
        P.wait("scalar", "v", v_var2)
        for tt in range(NTT):
            tq = tt + NTT * gp
            P.op("scalar", lambda e, tq=tq: e.activation(
                ST[:, tq, S_DEN:S_DEN + 1], ST[:, tq, S_VAR:S_VAR + 1], AF.Sqrt,
                bias=EPS, scale=1.0), inc="s")
        s_den2 = P.now("s")
        P.wait("vector", "s", s_den2)
        for tt in range(NTT):
            tq = tt + NTT * gp
            vst(lambda e, tq=tq: e.reciprocal(
                ST[:, tq, S_RHO2:S_RHO2 + 1], ST[:, tq, S_DEN:S_DEN + 1]))
            vst(lambda e, tq=tq: e.tensor_tensor(
                ST[:, tq, S_MR2:S_MR2 + 1], ST[:, tq, S_MU:S_MU + 1],
                ST[:, tq, S_RHO2:S_RHO2 + 1], AL.mult))
        v_r2 = P.now("v")
        P.wait("scalar", "v", v_r2)
        P.op("scalar", lambda e: e.copy(
            ST[:, :, S_RHO2:S_RHO2 + 2], ST[:, :, S_RHO2:S_RHO2 + 2]), inc="s")
        P.wait("vector", "s", P.now("s"))
        for tt in range(NTT):
            tq = tt + NTT * gp
            tg = g * NTT + tt
            vst(lambda e, tq=tq, tt=tt: e.tensor_scalar(
                m_ar.ap()[:, tt, 0:2048], m_ar.ap()[:, tt, 0:2048],
                ST[:, tq, S_RHO2:S_RHO2 + 1], ST[:, tq, S_MR2:S_MR2 + 1],
                AL.mult, AL.subtract))
            v_fin = P.now("v")
            P.wait("scalar", "v", v_fin)
            P.op("scalar", lambda e, tt=tt, tg=tg: e.dma_start(
                out=dt["out"][128 * tg:128 * (tg + 1), :],
                in_=m_ar.ap()[:, tt, 0:2048]), inc="qs", inc_by=16)
        prev_qs[0] = P.now("qs")

    P.wait("gpsimd", "qs", P.now("qs"))
    return _finish(nc, P, dt)


def _finish(nc, P, dt):
    with nc.Block() as block:
        @block.sync
        def _(eng):
            P.emit("sync", eng)

        @block.vector
        def _(eng):
            P.emit("vector", eng)

        @block.scalar
        def _(eng):
            P.emit("scalar", eng)

        @block.tensor
        def _(eng):
            P.emit("tensor", eng)

        @block.gpsimd
        def _(eng):
            P.emit("gpsimd", eng)

    return nc, dt


def _host_layout(inputs):
    f32 = np.float32
    hs = np.asarray(inputs["L0_attn_hidden_states"][0], f32)
    hsT = np.ascontiguousarray(hs.T)
    hsT_ext = np.concatenate([np.zeros((D, 3), f32), hsT], axis=1)

    wxT = np.ascontiguousarray(np.asarray(inputs["L0_attn_W_in_x"], f32).T)
    wzT = np.ascontiguousarray(np.asarray(inputs["L0_attn_W_in_z"], f32).T)
    wpT = np.ascontiguousarray(np.asarray(inputs["L0_attn_W_param"], f32).T)
    woT = np.ascontiguousarray(np.asarray(inputs["L0_attn_W_out"], f32).T)
    wdt = np.asarray(inputs["L0_attn_W_dt"], f32)
    wdtT_rep = np.ascontiguousarray(np.repeat(wdt.T, 16, axis=1))
    wgT = np.ascontiguousarray(np.asarray(inputs["L0_ffn_w_gate"], f32).T)
    wuT = np.ascontiguousarray(np.asarray(inputs["L0_ffn_w_up"], f32).T)
    wdT = np.ascontiguousarray(np.asarray(inputs["L0_ffn_w_down"], f32).T)

    def chmat(v):
        return np.ascontiguousarray(np.asarray(v, f32).reshape(CG, 128).T)

    conv_w = np.asarray(inputs["L0_attn_conv_w"], f32).reshape(C, 4)
    conv_wm = np.zeros((128, CG * 4), f32)
    for g in range(CG):
        for k in range(4):
            conv_wm[:, 4 * g + k] = conv_w[128 * g:128 * (g + 1), k]
    conv_b = chmat(inputs["L0_attn_conv_b"])
    a_log = chmat(np.asarray(inputs["L0_attn_A_log"], f32).reshape(C))
    bdt_rep = chmat(np.repeat(np.asarray(inputs["L0_attn_b_dt"], f32), 16))
    d_vec = chmat(inputs["L0_attn_D"])

    ident = np.eye(128, dtype=ml_dtypes.bfloat16)
    ident32 = np.eye(128, dtype=f32)
    ones1 = np.ones((1, 128), f32)
    zcol = np.zeros((128, 1), f32)

    in_maps = []
    for c in range(NCORES):
        pm = np.zeros((128, NCORES), f32)
        pm[:, :c] = 1.0
        im = {
            "hsT": np.ascontiguousarray(hsT_ext[:, T * c:T * c + T + 3]),
            "hs": np.ascontiguousarray(hs[T * c:T * (c + 1)]),
            "wxT": wxT, "wzT": wzT, "wpT": wpT, "woT": woT,
            "wdtT_rep": wdtT_rep,
            "conv_w": conv_wm, "conv_b": conv_b, "a_log": a_log,
            "bdt_rep": bdt_rep, "d_vec": d_vec,
            "ident": ident, "ident32": ident32,
            "pmask": pm, "pmask1m": np.ascontiguousarray(1.0 - pm),
            "zcol": zcol, "ones1": ones1,
            "cvals": np.tile(np.array([[EPS, 1.0, 0.5, -0.5]], f32), (128, 1)),
            "wgT_sl": np.ascontiguousarray(wgT[:, 1024 * c:1024 * (c + 1)]),
            "wuT_sl": np.ascontiguousarray(wuT[:, 1024 * c:1024 * (c + 1)]),
            "wdT_sl": np.ascontiguousarray(wdT[1024 * c:1024 * (c + 1)]),
        }
        in_maps.append(im)
    return in_maps


_BUILT = {}


def kernel(**inputs):
    if "nc" not in _BUILT:
        nc, dtm = build_program()
        _BUILT["nc"] = nc
    nc = _BUILT["nc"]
    in_maps = _host_layout(inputs)
    res = run_bass_kernel_spmd(nc, in_maps, core_ids=list(range(NCORES)))
    out = np.concatenate([np.asarray(r["out"]) for r in res.results], axis=0)
    return out.reshape(1, L, D).astype(np.float32)


# version 18
# speedup vs baseline: 1.3065x; 1.0286x over previous
import numpy as np
import ml_dtypes
import concourse.bass as bass
import concourse.mybir as mybir
from concourse.bass_utils import run_bass_kernel_spmd

F32 = mybir.dt.float32
BF16 = mybir.dt.bfloat16
F8 = mybir.dt.float8e4
I32 = mybir.dt.int32
AL = mybir.AluOpType
AF = mybir.ActivationFunctionType
AX = mybir.AxisListType

NCORES = 8
D = 2048
L = 16384
T = L // NCORES
C = 512
CG = 4
DTR = 128
FF = 8192
NP = 1152
EPS = 1e-5
TT = 256
NG = T // TT
NTT = TT // 128

TAPS = {}
STOP_AFTER = None
DEV_BARRIER = False

KB = 1024


def rnd(x, m=32):
    return (x + m - 1) // m * m


class Prog:
    def __init__(self, nc):
        self.nc = nc
        self.eng = {e: [] for e in ("sync", "vector", "scalar", "tensor", "gpsimd")}
        self.sem = {}
        self.ctr = {}

    def add_sem(self, name):
        self.sem[name] = self.nc.semaphore(name).__enter__()
        self.ctr[name] = 0

    def op(self, engine, fn, inc=None, inc_by=1):
        if inc is not None:
            self.ctr[inc] += inc_by
        self.eng[engine].append(("op", fn, inc, inc_by))
        return self.ctr[inc] if inc is not None else None

    def dma(self, engine, out, in_, inc, slow=False):
        self.ctr[inc] += 16
        self.eng[engine].append(
            ("op", (lambda e, o=out, i=in_, s=slow: e.dma_start(
                out=o, in_=i, allow_slow_non_contiguous=s)), inc, 16))
        return self.ctr[inc]

    def drain(self, engine):
        self.eng[engine].append(("op", (lambda e: e.drain()), None, 1))

    def wait(self, engine, semname, value=None):
        v = self.ctr[semname] if value is None else value
        self.eng[engine].append(("wait", semname, v))
        return v

    def now(self, semname):
        return self.ctr[semname]

    def emit(self, engine_name, eng):
        for entry in self.eng[engine_name]:
            if entry[0] == "wait":
                eng.wait_ge(self.sem[entry[1]], entry[2])
            else:
                _, fn, inc, inc_by = entry
                inst = fn(eng)
                if inc is not None:
                    inst.then_inc(self.sem[inc], inc_by)


class Arena:
    SB_BYTES = 206 * KB

    def __init__(self, nc):
        self.nc = nc
        self.backing = nc.sbuf_tensor(
            "arena", [128, self.SB_BYTES], mybir.dt.uint8).__enter__()
        self.base = nc.lookup_mloc(self.backing).addr

    def t(self, name, shape, dtype, off):
        full = [128] + list(shape)
        per_part = int(np.prod(shape)) * mybir.dt.size(dtype)
        assert off % 32 == 0, (name, off)
        assert off + per_part <= self.SB_BYTES, (name, off, per_part, off + per_part)
        h = self.nc.virtual_sbuf_tensor(name, full, dtype)
        self.nc._tensor(h.name, h.shape, h.dtype, type="SB")
        mloc = self.nc.lookup_mloc(h)
        mloc.allocated = True
        mloc.addr = self.base + off
        return h


(S_SUM, S_SSQ, S_MU, S_VAR, S_DEN, S_RHO, S_MAX, S_MIN, S_MX, S_SX, S_RQ,
 S_AA, S_BB, S_SG, S_SU, S_MM, S_SM, S_RQM, S_SMW, S_MURHO, S_SUM2, S_T1,
 S_T2, S_T3, S_RHO2, S_MR2, S_SA, S_SB2, S_Q1, S_Q2) = range(30)


def build_program():
    nc = bass.Bass("TRN2", target_bir_lowering=False)
    A = Arena(nc)
    dt = {}

    def din(name, shape, dtype=F32):
        dt[name] = nc.dram_tensor(name, list(shape), dtype, kind="ExternalInput").ap()

    def dout(name, shape, dtype=F32):
        dt[name] = nc.dram_tensor(name, list(shape), dtype, kind="ExternalOutput").ap()

    def dint(name, shape, dtype=F32):
        dt[name] = nc.dram_tensor(name, list(shape), dtype, kind="Internal").ap()

    din("hsT", (D, T + 3))
    din("hs", (T, D))
    din("wxT", (D, C))
    din("wzT", (D, C))
    din("wpT", (C, NP))
    din("woT", (C, D))
    din("wdtT_rep", (DTR, C))
    din("conv_w", (128, CG * 4))
    din("conv_b", (128, CG))
    din("a_log", (128, CG))
    din("bdt_rep", (128, CG))
    din("d_vec", (128, CG))
    din("ident", (128, 128), BF16)
    din("ident32", (128, 128), F32)
    din("pmask", (128, NCORES))
    din("pmask1m", (128, NCORES))
    din("zcol", (128, 1))
    din("cvals", (128, 4))
    din("ones1", (1, 128))
    din("wgT_sl", (D, FF // NCORES))
    din("wuT_sl", (D, FF // NCORES))
    din("wdT_sl", (FF // NCORES, D))
    dout("out", (T, D))

    dint("ar_in", (128, 40))
    dint("ar_out", (128, 40))
    dint("red_lin", (40 * 128,))
    dint("agw_in_g", (4, 128, 16, 256), F8)
    dint("agw_out_g", (NCORES, 4, 128, 16, 256), F8)
    dint("agw_in_u", (4, 128, 16, 256), F8)
    dint("agw_out_u", (NCORES, 4, 128, 16, 256), F8)
    dint("agw_in_d", (FF // NCORES, D), F8)
    dint("agw_out_d", (FF, D), F8)
    dint("ags_in", (1024,))
    dint("ags_out", (NCORES, 1024))

    O_Y = 170 * KB
    O_WO = 186 * KB
    O_PERM = 202 * KB + 512
    y_bf = A.t("y_bf", (CG, T), BF16, O_Y)
    woT_bf = A.t("woT_bf", (CG, D), BF16, O_WO)

    consts = {}
    pbase = [O_PERM]

    def perm(name, shape, dtype):
        consts[name] = A.t("c_" + name, shape, dtype, pbase[0])
        pbase[0] += rnd(int(np.prod(shape)) * mybir.dt.size(dtype))
        return consts[name]

    perm("conv_w", (CG * 4,), F32)
    perm("conv_b", (CG,), F32)
    perm("a_flat", (CG,), F32)
    perm("bdt", (CG,), F32)
    perm("d_vec", (CG,), F32)
    perm("ident", (128,), BF16)
    perm("ident32", (128,), F32)
    perm("pmask", (NCORES,), F32)
    perm("pmask1m", (NCORES,), F32)
    perm("zcol", (1,), F32)
    perm("ones1", (128,), F32)
    perm("thbc", (12,), F32)
    perm("red", (48,), F32)
    perm("allst", (NCORES, 2, CG), F32)
    perm("hprev", (2, CG), F32)
    perm("ptot", (CG,), F32)

    perm("stats", (2 * NTT, 30), F32)
    perm("cvals", (4,), F32)
    assert pbase[0] <= Arena.SB_BYTES, pbase[0]
    nc.const_aps.register(F32, EPS, consts["cvals"].ap()[:, 0:1])
    nc.const_aps.register(F32, 1.0, consts["cvals"].ap()[:, 1:2])
    nc.const_aps.register(F32, 0.5, consts["cvals"].ap()[:, 2:3])
    nc.const_aps.register(F32, -0.5, consts["cvals"].ap()[:, 3:4])

    HS_W = 544
    hsT_r = [A.t(f"hsT_r{i}", (16, HS_W), BF16, i * rnd(16 * HS_W * 2))
             for i in range(2)]
    o = 2 * rnd(16 * HS_W * 2)
    O_WX = o
    wxT_bf = A.t("wxT_bf", (16, C), BF16, o); o += rnd(16 * C * 2)
    wzT_bf = A.t("wzT_bf", (16, C), BF16, o); o += rnd(16 * C * 2)
    wpT_bf = A.t("wpT_bf", (CG, NP), BF16, o); o += rnd(CG * NP * 2)
    wdt_bf = A.t("wdt_bf", (C,), BF16, o); o += rnd(C * 2)
    O_XFM = o
    x_fm = A.t("x_fm", (CG, T + 3), F32, o); o += rnd(CG * (T + 3) * 4)
    O_Z = max(rnd(o), 112 * KB)
    z_bf = A.t("z_bf", (CG, T), BF16, O_Z)
    O_ACC = O_Z + rnd(CG * T * 2)
    acc_fm = A.t("acc_fm", (CG, T), F32, O_ACC)
    b_fm = A.t("b_fm", (CG, T), F32, O_ACC)
    O_DT = O_ACC + rnd(CG * T * 4)
    dt_bf = A.t("dt_bf", (T,), BF16, O_DT)
    assert O_DT + rnd(T * 2) <= O_Y, (O_DT + rnd(T * 2), O_Y)

    x_act_bf = A.t("x_act_bf", (CG, T), BF16, 0)
    c_fm = A.t("c_fm", (CG, T), F32, 16 * KB)
    h_fm = A.t("h_fm", (CG, T), F32, 48 * KB)
    O_DELTA = 80 * KB
    delta_fm = A.t("delta_fm", (CG, T), F32, O_DELTA)
    assert O_DELTA >= O_XFM
    assert O_DELTA + CG * T * 4 <= O_Z, (O_DELTA + CG * T * 4, O_Z)
    assert 48 * KB + CG * T * 4 <= O_DELTA

    red1 = A.t("red1", (40 * 128,), F32, O_ACC)

    a1_ring = [A.t(f"a1_r{i}", (FF,), F32, i * rnd(FF * 4)) for i in range(2)]
    a2_ring = [A.t(f"a2_r{i}", (FF,), F32, i * rnd(FF * 4)) for i in range(2)]
    a2_out = [A.t(f"a2_o{i}", (FF,), F8, 2 * rnd(FF * 4) + i * rnd(FF * 2))
              for i in range(2)]
    a2_sc = A.t("a2_sc", (FF,), F32, 2 * rnd(FF * 4) + 2 * rnd(FF * 2))

    m_ar = A.t("m_ar", (NTT, FF), F32, 0)
    ui_sc = A.t("ui_sc", (1024,), I32, 64 * KB)
    c_sc = A.t("c_sc", (1024,), F32, 68 * KB)
    mqs = A.t("mqs", (1024,), BF16, 72 * KB)
    r1_t = A.t("r1_t", (NTT, D), F32, 74 * KB)
    O_BLK = 90 * KB
    sq_tr = A.t("sq_tr", (1024,), F32, O_BLK)
    xq_tm = A.t("xq_tm", (NTT, D), BF16, O_BLK + 4 * KB)
    xqT = A.t("xqT", (16, TT), BF16, O_BLK + 12 * KB)
    sg_sl = [A.t(f"sg_s{i}", (512,), F32, O_BLK + 20 * KB + i * rnd(512 * 4))
             for i in range(4)]
    mqT = A.t("mqT", (2, 64, 128), BF16, O_BLK)
    O_WD = 122 * KB
    wd_ring = [A.t(f"wd_r{i}", (D,), F8, O_WD + i * rnd(D))
               for i in range(4)]
    hs_sb = A.t("hs_sb", (D,), F32, 130 * KB)
    O_GU = 138 * KB
    gu_ring = [A.t(f"gu_r{i}", (16, 512), F8, O_GU + i * rnd(16 * 512))
               for i in range(2)]
    assert O_GU + 2 * rnd(16 * 512 * 2) <= O_Y, (O_GU + 2 * rnd(16 * 512 * 2), O_Y)

    psD = nc.psum_tensor("psD", [128, 2048], F32).__enter__()
    psD2 = nc.psum_tensor("psD2", [128, 2048], F32).__enter__()
    psA = psD2[:, 0:512]
    psB = psD2[:, 512:1024]
    psE = psD2[:, 1024:1536]
    psT = psD2[:, 1536:2048].bitcast(BF16)
    psU = psD2[:, 1536:2048]

    P = Prog(nc)
    for s in ("qc", "qmw", "qr", "qx", "qs", "qh", "qt0", "qt1",
              "qa0", "qa1", "qo0", "qo1",
              "qu0", "qu1", "qu2", "qd0", "qd1", "qd2", "qd3",
              "v", "s", "t", "cc"):
        P.add_sem(s)

    consts["ags_sb"] = A.t("c_ags_sb", (1024,), F32, O_DT)
    ID = consts["ident"].ap()
    ID32 = consts["ident32"].ap()
    ST = consts["stats"].ap()
    thbc = consts["thbc"].ap()

    def tap(name, src_ap, shape, dtype=F32, waits=(), freeze=()):
        if name not in TAPS:
            return
        d = nc.dram_tensor("tap_" + name, list(shape), dtype,
                           kind="ExternalOutput").ap()
        for sem, val in waits:
            P.wait("gpsimd", sem, val)
        qv = P.dma("gpsimd", d, src_ap, "qs")
        for engn in freeze:
            P.wait(engn, "qs", qv)

    ld = {}
    for nm, tgt in [("conv_w", "conv_w"), ("conv_b", "conv_b"),
                    ("a_log", "a_flat"), ("bdt_rep", "bdt"),
                    ("d_vec", "d_vec"), ("ident", "ident"),
                    ("ident32", "ident32"), ("pmask", "pmask"),
                    ("pmask1m", "pmask1m"), ("zcol", "zcol"),
                    ("cvals", "cvals")]:
        ld[tgt] = P.dma("sync", consts[tgt].ap()[:, :], dt[nm], "qc")
    ld["ones1"] = P.dma("sync", consts["ones1"].ap()[0:1, :], dt["ones1"][0:1, :],
                        "qc")
    QC_ALL = P.now("qc")
    for k in ld:
        ld[k] = QC_ALL
    P.wait("scalar", "qc", ld["a_flat"])
    s_af = P.op("scalar", lambda e: e.activation(
        consts["a_flat"].ap()[:, :], consts["a_flat"].ap()[:, :], AF.Exp), inc="s")
    P.wait("vector", "s", s_af)
    v_af = P.op("vector", lambda e: e.tensor_scalar(
        consts["a_flat"].ap()[:, :], consts["a_flat"].ap()[:, :], -1.0, 0.0,
        AL.mult, AL.add), inc="v")

    a_slices = ([("wgT_sl", i, FF // NCORES) for i in range(16)]
                + [("wuT_sl", i, FF // NCORES) for i in range(16)]
                + [("wdT_sl", i, D) for i in range(8)])
    red = consts["red"].ap()
    red_done = []
    for j, (nm, i, w) in enumerate(a_slices):
        if j >= 2:
            P.wait("sync", "v", red_done[j - 2])
        v = P.dma("sync", a1_ring[j % 2].ap()[:, 0:w],
                  dt[nm][128 * i:128 * (i + 1), :], f"qa{j % 2}")
        P.wait("vector", f"qa{j % 2}", v)
        P.op("vector", lambda e, j=j, w=w: e.tensor_reduce(
            red[:, j:j + 1], a1_ring[j % 2].ap()[:, 0:w], AX.X, AL.add,
            apply_absolute_value=True), inc="v")
        red_done.append(P.now("v"))
    v_a1 = P.now("v")

    P.wait("gpsimd", "v", v_a1)
    v = P.dma("gpsimd", dt["ar_in"][:, 0:40], red[:, 0:40], "qr")
    P.wait("gpsimd", "qr", v)
    P.op("gpsimd", lambda e: e.collective_compute(
        "AllReduce", AL.add, replica_groups=[list(range(NCORES))],
        ins=[dt["ar_in"].opt()], outs=[dt["ar_out"].opt()]), inc="cc")
    P.wait("gpsimd", "cc", P.now("cc"))
    v = P.dma("gpsimd", dt["red_lin"][:].rearrange("(p k) -> p k", p=128),
              dt["ar_out"][:, :], "qr")
    P.wait("gpsimd", "qr", v)
    v = P.dma("gpsimd", red1.ap()[0:1, :], dt["red_lin"].rearrange("(o a) -> o a", o=1),
              "qr")
    P.wait("vector", "qr", v)
    rview = red1.ap()[0:1, :].rearrange("a (p k) -> a p k", k=40)
    P.op("vector", lambda e: e.tensor_reduce(
        red[0:1, 0:1], rview[:, :, 0:16], AX.XY, AL.add))
    P.op("vector", lambda e: e.tensor_reduce(
        red[0:1, 1:2], rview[:, :, 16:32], AX.XY, AL.add))
    P.op("vector", lambda e: e.tensor_reduce(
        red[0:1, 2:3], rview[:, :, 32:40], AX.XY, AL.add))
    P.drain("vector")
    P.op("vector", lambda e: e.tensor_scalar(
        red[0:1, 9:12], red[0:1, 0:3], 1.0 / (FF * D), EPS, AL.mult, AL.add))
    P.drain("vector")
    P.op("vector", lambda e: e.tensor_scalar(
        red[0:1, 3:6], red[0:1, 9:12], 0.5 * (1.0 - 6e-7), 0.0, AL.mult, AL.add))
    P.drain("vector")
    P.op("vector", lambda e: e.tensor_scalar(
        red[0:1, 6:9], red[0:1, 3:6], -1.0, 0.0, AL.mult, AL.add), inc="v")
    P.drain("vector")
    v_thr = P.now("v")
    P.wait("tensor", "qc", ld["ones1"])
    P.wait("tensor", "v", v_thr)
    P.op("tensor", lambda e: e.matmul(
        psE[:, 0:9], consts["ones1"].ap()[0:1, :], red[0:1, 3:12],
        start=True, stop=True), inc="t")
    t_bc = P.now("t")
    P.wait("scalar", "t", t_bc)
    s_thbc = P.op("scalar", lambda e: e.copy(thbc[:, 0:9], psE[:, 0:9]), inc="s")
    tap("thbc", thbc[:, 0:9], (128, 9), waits=[("s", s_thbc)])
    tap("red", red[:, 0:16], (128, 16), waits=[("s", s_thbc)])

    P.dma("gpsimd", wxT_bf.ap()[:, :, :],
          dt["wxT"].rearrange("(a p) b -> p a b", p=128), "qmw")
    P.dma("gpsimd", wzT_bf.ap()[:, :, :],
          dt["wzT"].rearrange("(a p) b -> p a b", p=128), "qmw")
    P.dma("gpsimd", wpT_bf.ap()[:, :, :],
          dt["wpT"].rearrange("(a p) b -> p a b", p=128), "qmw")
    P.dma("gpsimd", woT_bf.ap()[:, :, :],
          dt["woT"].rearrange("(a p) b -> p a b", p=128), "qmw")
    P.dma("gpsimd", wdt_bf.ap()[:, :], dt["wdtT_rep"], "qmw")
    g_wx = g_wz = g_wp = g_wo = g_wdt = P.now("qmw")

    hsTr = dt["hsT"].rearrange("(a p) b -> p a b", p=128)
    strip_v = []
    P.wait("gpsimd", "v", v_a1)
    for s in range(4):
        c0 = 512 * s
        w = 515
        v = P.dma("gpsimd", hsT_r[s % 2].ap()[:, :, 0:w], hsTr[:, :, c0:c0 + w],
                  f"qt{s % 2}")
        strip_v.append(v)
        if s == 1:
            break
    P.wait("tensor", "qmw", g_wx)
    P.wait("tensor", "qmw", g_wz)

    def mm_xz(ps_ap, w_bf, m, slot, col0, n):
        def fn(e):
            r = None
            for k in range(16):
                r = e.matmul(ps_ap, w_bf.ap()[:, k, 128 * m:128 * (m + 1)],
                             hsT_r[slot].ap()[:, k, col0:col0 + n],
                             start=(k == 0), stop=(k == 15))
            return r
        return fn

    strip_t = []
    for s in range(4):
        if s >= 2:
            P.wait("gpsimd", "t", strip_t[s - 2])
            c0 = 512 * s
            w = 515 if s < 3 else 515
            v = P.dma("gpsimd", hsT_r[s % 2].ap()[:, :, 0:w],
                      hsTr[:, :, c0:c0 + w], f"qt{s % 2}")
            strip_v.append(v)
        P.wait("tensor", f"qt{s % 2}", strip_v[s])
        for m in range(CG):
            P.op("tensor", mm_xz(psD[:, 512 * m:512 * m + 512], wxT_bf, m,
                                 s % 2, 0, 512), inc="t")
        t_x = P.now("t")
        P.wait("scalar", "t", t_x)
        for m in range(CG):
            P.op("scalar", lambda e, m=m, s=s: e.copy(
                x_fm.ap()[:, m, 512 * s:512 * s + 512],
                psD[:, 512 * m:512 * m + 512]), inc="s")
        s_xe = P.now("s")
        P.wait("tensor", "s", s_xe)
        for m in range(CG):
            P.op("tensor", mm_xz(psD[:, 512 * m:512 * m + 512], wzT_bf, m,
                                 s % 2, 3, 512), inc="t")
        t_z = P.now("t")
        P.wait("vector", "t", t_z)
        for m in range(CG):
            P.op("vector", lambda e, m=m, s=s: e.tensor_copy(
                z_bf.ap()[:, m, 512 * s:512 * s + 512],
                psD[:, 512 * m:512 * m + 512]), inc="v")
        v_z = P.now("v")
        P.wait("tensor", "v", v_z)
        strip_t.append(P.now("t"))
    for m in range(CG):
        def tailf(e, m=m):
            r = None
            for k in range(16):
                r = e.matmul(psA[:, 4 * m:4 * m + 3],
                             wxT_bf.ap()[:, k, 128 * m:128 * (m + 1)],
                             hsT_r[1].ap()[:, k, 512:515],
                             start=(k == 0), stop=(k == 15))
            return r
        P.op("tensor", tailf, inc="t")
    t_tail = P.now("t")
    P.wait("scalar", "t", t_tail)
    for m in range(CG):
        P.op("scalar", lambda e, m=m: e.copy(
            x_fm.ap()[:, m, T:T + 3], psA[:, 4 * m:4 * m + 3]), inc="s")
    s_xall = P.now("s")
    P.wait("tensor", "s", s_xall)

    P.wait("vector", "s", s_xall)
    P.wait("vector", "qc", ld["conv_w"])
    cw = consts["conv_w"].ap()
    for m in range(CG):
        P.op("vector", lambda e, m=m: e.tensor_scalar(
            acc_fm.ap()[:, m, :], x_fm.ap()[:, m, 0:T],
            cw[:, 4 * m:4 * m + 1], 0.0, AL.mult, AL.add), inc="v")
    P.drain("vector")
    for k in range(1, 4):
        for m in range(CG):
            P.op("vector", lambda e, m=m, k=k: e.scalar_tensor_tensor(
                acc_fm.ap()[:, m, :], x_fm.ap()[:, m, k:k + T],
                cw[:, 4 * m + k:4 * m + k + 1], acc_fm.ap()[:, m, :],
                AL.mult, AL.add), inc="v")
        P.drain("vector")
    v_conv = P.now("v")
    P.wait("scalar", "v", v_conv)
    P.wait("scalar", "qc", ld["conv_b"])
    for m in range(CG):
        P.op("scalar", lambda e, m=m: e.activation(
            x_act_bf.ap()[:, m, :], acc_fm.ap()[:, m, :], AF.Silu,
            bias=consts["conv_b"].ap()[:, m:m + 1], scale=1.0), inc="s")
    s_act = P.now("s")
    tap("x_act", x_act_bf.ap()[:, :, :], (128, CG, T), BF16, waits=[("s", s_act)])

    P.wait("tensor", "s", s_act)
    P.wait("tensor", "qmw", g_wp)

    def mm_par(ms, strip, ps_ap):
        def fn(e):
            r = None
            for k in range(CG):
                r = e.matmul(ps_ap, wpT_bf.ap()[:, k, 128 * ms:128 * (ms + 1)],
                             x_act_bf.ap()[:, k, 512 * strip:512 * strip + 512],
                             start=(k == 0), stop=(k == CG - 1))
            return r
        return fn

    for strip in range(4):
        P.op("tensor", mm_par(0, strip, psA), inc="t")
        for m in range(1, 5):
            P.op("tensor", mm_par(m, strip, psD[:, 512 * (m - 1):512 * m]), inc="t")
        t1 = P.now("t")
        P.wait("vector", "t", t1)
        P.op("vector", lambda e, strip=strip: e.tensor_copy(
            dt_bf.ap()[:, 512 * strip:512 * strip + 512], psA), inc="v")
        P.wait("scalar", "t", t1)
        for m in range(1, 5):
            P.op("scalar", lambda e, m=m, strip=strip: e.copy(
                b_fm.ap()[:, m - 1, 512 * strip:512 * strip + 512],
                psD[:, 512 * (m - 1):512 * m]), inc="s")
        P.wait("tensor", "v", P.now("v"))
        P.wait("tensor", "s", P.now("s"))
        for m in range(5, 9):
            P.op("tensor", mm_par(m, strip, psD[:, 512 * (m - 5):512 * (m - 4)]),
                 inc="t")
        t2 = P.now("t")
        P.wait("scalar", "t", t2)
        for m in range(5, 9):
            P.op("scalar", lambda e, m=m, strip=strip: e.copy(
                c_fm.ap()[:, m - 5, 512 * strip:512 * strip + 512],
                psD[:, 512 * (m - 5):512 * (m - 4)]), inc="s")
        P.wait("tensor", "s", P.now("s"))
    v_par = P.now("v")

    P.wait("tensor", "v", v_par)
    P.wait("tensor", "qmw", g_wdt)
    P.wait("scalar", "qc", ld["bdt"])
    for strip in range(4):
        for m in range(CG):
            P.op("tensor", lambda e, m=m, strip=strip: e.matmul(
                psE, wdt_bf.ap()[:, 128 * m:128 * (m + 1)],
                dt_bf.ap()[:, 512 * strip:512 * strip + 512],
                start=True, stop=True), inc="t")
            tm = P.now("t")
            P.wait("scalar", "t", tm)
            P.op("scalar", lambda e, m=m, strip=strip: e.activation(
                delta_fm.ap()[:, m, 512 * strip:512 * strip + 512], psE,
                AF.Exp, bias=consts["bdt"].ap()[:, m:m + 1], scale=1.0), inc="s")
            P.wait("tensor", "s", P.now("s"))
    P.drain("scalar")
    for m in range(CG):
        P.op("scalar", lambda e, m=m: e.activation(
            delta_fm.ap()[:, m, :], delta_fm.ap()[:, m, :], AF.Ln, bias=1.0),
            inc="s")
    P.drain("scalar")
    s_delta = P.now("s")
    tap("delta", delta_fm.ap()[:, :, :], (128, CG, T), waits=[("s", s_delta)])

    P.wait("vector", "s", s_delta)
    for m in range(CG):
        P.op("vector", lambda e, m=m: e.tensor_tensor(
            b_fm.ap()[:, m, :], b_fm.ap()[:, m, :], delta_fm.ap()[:, m, :],
            AL.mult), inc="v")
    P.drain("vector")
    for m in range(CG):
        P.op("vector", lambda e, m=m: e.tensor_tensor(
            b_fm.ap()[:, m, :], b_fm.ap()[:, m, :], x_act_bf.ap()[:, m, :],
            AL.mult), inc="v")
    P.drain("vector")
    for m in range(CG):
        P.op("vector", lambda e, m=m: e.tensor_scalar(
            delta_fm.ap()[:, m, :], delta_fm.ap()[:, m, :],
            consts["a_flat"].ap()[:, m:m + 1], 0.0, AL.mult, AL.add), inc="v")
    P.drain("vector")
    for m in range(CG):
        P.op("vector", lambda e, m=m: e.tensor_reduce(
            consts["ptot"].ap()[:, m:m + 1], delta_fm.ap()[:, m, :], AX.X,
            AL.add), inc="v")
    P.drain("vector")
    v_su = P.now("v")
    if "slog" in TAPS:
        dslog = nc.dram_tensor("tap_slog", [128, CG, T], F32,
                               kind="ExternalOutput").ap()
        P.wait("gpsimd", "v", v_su)
        qv = P.dma("gpsimd", dslog, delta_fm.ap()[:, :, :], "qs")
        P.wait("scalar", "qs", qv)
    P.wait("scalar", "v", v_su)
    P.op("scalar", lambda e: e.activation(
        consts["ptot"].ap()[:, :], consts["ptot"].ap()[:, :], AF.Exp), inc="s")
    for m in range(CG):
        P.op("scalar", lambda e, m=m: e.activation(
            delta_fm.ap()[:, m, :], delta_fm.ap()[:, m, :], AF.Exp), inc="s")
    P.drain("scalar")
    s_abar = P.now("s")

    tap("bu", b_fm.ap()[:, :, :], (128, CG, T), waits=[("v", v_su)])
    tap("aflat", consts["a_flat"].ap()[:, :], (128, CG), waits=[("v", v_su)])
    tap("abar", delta_fm.ap()[:, :, :], (128, CG, T), waits=[("s", s_abar)])
    P.wait("vector", "s", s_abar)
    P.wait("vector", "qc", ld["zcol"])
    for m in range(CG):
        P.op("vector", lambda e, m=m: e.tensor_tensor_scan(
            h_fm.ap()[:, m, :], delta_fm.ap()[:, m, :], b_fm.ap()[:, m, :],
            consts["zcol"].ap()[:, 0:1], AL.mult, AL.add), inc="v")
    P.drain("vector")
    v_scan1 = P.now("v")

    tap("h1", h_fm.ap()[:, :, :], (128, CG, T), waits=[("v", v_scan1)])
    P.wait("gpsimd", "v", v_scan1)
    P.dma("gpsimd", dt["ags_in"][0:512].rearrange("(g p) -> p g", p=128),
          consts["ptot"].ap()[:, 0:CG], "qx", slow=True)
    v = P.dma("gpsimd", dt["ags_in"][512:1024].rearrange("(g p) -> p g", p=128),
              h_fm.ap()[:, :, T - 1:T].rearrange("p g a -> p (g a)"), "qx",
              slow=True)
    P.wait("gpsimd", "qx", v)
    P.op("gpsimd", lambda e: e.collective_compute(
        "AllGather", AL.bypass, replica_groups=[list(range(NCORES))],
        ins=[dt["ags_in"].opt()], outs=[dt["ags_out"].opt()]), inc="cc")
    P.wait("gpsimd", "cc", P.now("cc"))
    v_ag = P.dma("gpsimd", consts["ags_sb"].ap()[0:8, :], dt["ags_out"][:, :],
                 "qx")
    P.wait("tensor", "qx", v_ag)
    P.wait("tensor", "qc", ld["ident32"])
    for kk in range(8):
        P.op("tensor", lambda e, kk=kk: e.transpose(
            psD[:, 8 * kk:8 * kk + 8],
            consts["ags_sb"].ap()[0:8, 128 * kk:128 * (kk + 1)],
            ID32[0:8, 0:8]), inc="t")
    t_ag = P.now("t")
    P.wait("vector", "t", t_ag)
    al = consts["allst"].ap()
    P.op("vector", lambda e: e.tensor_copy(
        al[:, :, :, :], psD[:, 0:64].rearrange("p (k g j) -> p j k g", k=2, g=4)),
        inc="v")
    P.drain("vector")
    hp = consts["hprev"].ap()
    P.op("vector", lambda e: e.memset(hp[:, 1, :], 0.0), inc="v")
    P.drain("vector")
    P.wait("vector", "qc", ld["pmask1m"])
    for j in range(NCORES):
        P.op("vector", lambda e, j=j: e.tensor_scalar(
            al[:, j, 0, :], al[:, j, 0, :], consts["pmask"].ap()[:, j:j + 1],
            consts["pmask1m"].ap()[:, j:j + 1], AL.mult, AL.add), inc="v")
        P.drain("vector")
        P.op("vector", lambda e, j=j: e.tensor_tensor(
            hp[:, 1, :], hp[:, 1, :], al[:, j, 0, :], AL.mult), inc="v")
        P.drain("vector")
        P.op("vector", lambda e, j=j: e.scalar_tensor_tensor(
            hp[:, 1, :], al[:, j, 1, :], consts["pmask"].ap()[:, j:j + 1],
            hp[:, 1, :], AL.mult, AL.add), inc="v")
        P.drain("vector")
    v_fold = P.now("v")
    P.wait("scalar", "v", v_fold)
    s_hp = P.op("scalar", lambda e: e.copy(hp[:, 0, :], hp[:, 1, :]), inc="s")

    tap("hprev", hp[:, :, :], (128, 2, CG), waits=[("s", s_hp)])
    tap("allst", al[:, :, :, :], (128, NCORES, 2, CG), waits=[("v", v_fold)])
    tap("ptot", consts["ptot"].ap()[:, :], (128, CG), waits=[("s", s_abar)])
    P.wait("vector", "s", s_hp)
    for m in range(CG):
        P.op("vector", lambda e, m=m: e.tensor_tensor_scan(
            h_fm.ap()[:, m, :], delta_fm.ap()[:, m, :], b_fm.ap()[:, m, :],
            hp[:, 0, m:m + 1], AL.mult, AL.add), inc="v")
    P.drain("vector")
    v_scan2 = P.now("v")
    tap("h", h_fm.ap()[:, :, :], (128, CG, T), waits=[("v", v_scan2)])

    P.wait("scalar", "v", v_scan2)
    for m in range(CG):
        P.op("scalar", lambda e, m=m: e.activation(
            b_fm.ap()[:, m, :], z_bf.ap()[:, m, :], AF.Silu), inc="s")
    s_sz = P.now("s")
    for m in range(CG):
        P.op("vector", lambda e, m=m: e.tensor_tensor(
            c_fm.ap()[:, m, :], c_fm.ap()[:, m, :], h_fm.ap()[:, m, :], AL.mult),
            inc="v")
    P.drain("vector")
    P.wait("vector", "qc", ld["d_vec"])
    for m in range(CG):
        P.op("vector", lambda e, m=m: e.scalar_tensor_tensor(
            c_fm.ap()[:, m, :], x_act_bf.ap()[:, m, :],
            consts["d_vec"].ap()[:, m:m + 1], c_fm.ap()[:, m, :],
            AL.mult, AL.add), inc="v")
    P.drain("vector")
    P.wait("vector", "s", s_sz)
    for m in range(CG):
        P.op("vector", lambda e, m=m: e.tensor_tensor(
            y_bf.ap()[:, m, :], c_fm.ap()[:, m, :], b_fm.ap()[:, m, :], AL.mult),
            inc="v")
    P.drain("vector")
    v_y = P.now("v")
    tap("y", y_bf.ap()[:, :, :], (128, CG, T), BF16, waits=[("v", v_y)])

    if STOP_AFTER == "mamba":
        P.wait("gpsimd", "v", v_y)
        P.wait("gpsimd", "qs", P.now("qs"))
        return _finish(nc, P, dt)
    P.wait("sync", "v", v_y)
    P.wait("vector", "s", s_thbc)
    a2_store = []
    vq_done = []
    for j, (nm, i, w) in enumerate(a_slices):
        col = {"wgT_sl": 0, "wuT_sl": 1, "wdT_sl": 2}[nm]
        agname = {"wgT_sl": "agw_in_g", "wuT_sl": "agw_in_u",
                  "wdT_sl": "agw_in_d"}[nm]
        if j >= 2:
            P.wait("sync", "v", vq_done[j - 2])
        v = P.dma("sync", a2_ring[j % 2].ap()[:, 0:w],
                  dt[nm][128 * i:128 * (i + 1), :], f"qa{j % 2}")
        P.wait("vector", f"qa{j % 2}", v)
        if j >= 2:
            P.wait("vector", a2_store[j - 2][0], a2_store[j - 2][1])
        P.op("vector", lambda e, j=j, w=w, col=col: e.tensor_scalar(
            a2_sc.ap()[:, 0:w], a2_ring[j % 2].ap()[:, 0:w],
            thbc[:, col:col + 1], 0.0, AL.is_gt, AL.add), inc="v")
        P.op("vector", lambda e, j=j, w=w, col=col: e.tensor_scalar(
            a2_ring[j % 2].ap()[:, 0:w], a2_ring[j % 2].ap()[:, 0:w],
            thbc[:, 3 + col:4 + col], 0.0, AL.is_lt, AL.add), inc="v")
        P.drain("vector")
        P.op("vector", lambda e, j=j, w=w: e.tensor_tensor(
            a2_out[j % 2].ap()[:, 0:w], a2_sc.ap()[:, 0:w],
            a2_ring[j % 2].ap()[:, 0:w], AL.subtract), inc="v")
        P.drain("vector")
        vq_done.append(P.now("v"))
        P.wait("gpsimd", "v", vq_done[j])
        qo = f"qo{j % 2}"
        if nm == "wdT_sl":
            v = P.dma("gpsimd", dt[agname][128 * i:128 * (i + 1), :],
                      a2_out[j % 2].ap()[:, 0:w], qo)
        else:
            for jc in range(4):
                v = P.dma("gpsimd", dt[agname][jc, :, i, :],
                          a2_out[j % 2].ap()[:, 256 * jc:256 * jc + 256], qo)
        a2_store.append((qo, v))

    P.wait("gpsimd", a2_store[14][0], a2_store[14][1])
    P.wait("gpsimd", a2_store[15][0], a2_store[15][1])
    P.op("gpsimd", lambda e: e.collective_compute(
        "AllGather", AL.bypass, replica_groups=[list(range(NCORES))],
        ins=[dt["agw_in_g"].opt()], outs=[dt["agw_out_g"].opt()]), inc="cc")
    cc_g = P.now("cc")
    P.wait("gpsimd", a2_store[30][0], a2_store[30][1])
    P.wait("gpsimd", a2_store[31][0], a2_store[31][1])
    P.op("gpsimd", lambda e: e.collective_compute(
        "AllGather", AL.bypass, replica_groups=[list(range(NCORES))],
        ins=[dt["agw_in_u"].opt()], outs=[dt["agw_out_u"].opt()]), inc="cc")
    cc_u = P.now("cc")
    P.wait("gpsimd", a2_store[38][0], a2_store[38][1])
    P.wait("gpsimd", a2_store[39][0], a2_store[39][1])
    P.op("gpsimd", lambda e: e.collective_compute(
        "AllGather", AL.bypass, replica_groups=[list(range(NCORES))],
        ins=[dt["agw_in_d"].opt()], outs=[dt["agw_out_d"].opt()]), inc="cc")
    cc_d = P.now("cc")
    tap("wgq", dt["agw_out_g"][:, :, :, :, :], (NCORES, 4, 128, 16, 256), F8, waits=[("cc", cc_d)])
    tap("wdq", dt["agw_out_d"][:, :], (FF, D), F8, waits=[("cc", cc_d)])

    if STOP_AFTER == "a2":
        P.wait("gpsimd", "cc", cc_d)
        P.wait("gpsimd", "qs", P.now("qs"))
        return _finish(nc, P, dt)
    wgTr = dt["agw_out_g"]
    wuTr = dt["agw_out_u"]
    RC = 1.0 / D

    v_a2 = P.now("v")

    gu_consumed = {}
    wd_consumed = {}
    prev_gu = []
    prev_wd = []

    def load_gu(key, src, c):
        slot = len(prev_gu) % 2
        if len(prev_gu) == 0:
            P.wait("sync", "cc", cc_g)
        elif len(prev_gu) == 1:
            P.wait("sync", "cc", cc_u)
        if len(prev_gu) >= 2:
            P.wait("sync", "t", gu_consumed[prev_gu[-2]])
        b0 = 2 * c
        P.dma("sync", gu_ring[slot].ap()[:, :, 0:256],
              src[b0 // 4, b0 % 4, :, :, :], f"qu{slot}")
        v = P.dma("sync", gu_ring[slot].ap()[:, :, 256:512],
                  src[(b0 + 1) // 4, (b0 + 1) % 4, :, :, :], f"qu{slot}")
        prev_gu.append(key)
        return slot, v

    def load_wd(key, k):
        slot = len(prev_wd) % 4
        eng = "gpsimd" if slot % 2 == 0 else "sync"
        if len(prev_wd) >= 4:
            P.wait(eng, "t", wd_consumed[prev_wd[-4]])
        v = P.dma(eng, wd_ring[slot].ap()[:, :],
                  dt["agw_out_d"][128 * k:128 * (k + 1), :], f"qd{slot}")
        prev_wd.append(key)
        return slot, v

    hs_cons = [v_a2]

    def load_hs(tg):
        P.wait("gpsimd", "v", hs_cons[0])
        return P.dma("gpsimd", hs_sb.ap()[:, :],
                     dt["hs"][128 * tg:128 * (tg + 1), :], "qh")

    def vst(fn):
        P.op("vector", fn, inc="v")
        P.drain("vector")

    P.wait("tensor", "qmw", g_wo)
    P.wait("tensor", "v", v_y)
    P.wait("sync", "cc", cc_d)
    P.wait("gpsimd", "cc", cc_d)
    prev_qs = [P.now("qs")]

    STAGE = 3 * D

    def emit_wout(gx, tt):
        tg = gx * NTT + tt

        def wof(e, tg=tg):
            r = None
            for n in range(4):
                for k in range(CG):
                    r = e.matmul(psD[:, 512 * n:512 * n + 512],
                                 y_bf.ap()[:, k, 128 * tg:128 * (tg + 1)],
                                 woT_bf.ap()[:, k, 512 * n:512 * n + 512],
                                 start=(k == 0), stop=(k == CG - 1))
            return r
        P.op("tensor", wof, inc="t")
        return P.now("t")

    def emit_r1evac(gx, tt, dst, v_hs, t_wo):
        tq = tt + NTT * (gx % 2)
        P.wait("vector", "qh", v_hs)
        P.wait("vector", "t", t_wo)
        vst(lambda e, tq=tq, dst=dst: e.scalar_tensor_tensor(
            dst, psD[:, 0:2048], 1.0, hs_sb.ap()[:, :],
            AL.mult, AL.add, accum_out=ST[:, tq, S_SUM:S_SUM + 1]))
        hs_cons[0] = P.now("v")
        return P.now("v")

    def emit_ln1(gx, src, t_gate=None, v_gate=None):
        if t_gate is not None:
            P.wait("scalar", "t", t_gate)
        if v_gate is not None:
            P.wait("scalar", "v", v_gate)
        for tt in range(NTT):
            tq = tt + NTT * (gx % 2)
            for hh in range(2):
                P.op("scalar", lambda e, tq=tq, tt=tt, hh=hh: e.activation(
                    sq_tr.ap()[:, :], src(tt, slice(1024 * hh, 1024 * (hh + 1))),
                    AF.Square, accum_out=ST[:, tq, S_Q1 + hh:S_Q1 + hh + 1]),
                    inc="s")
                P.drain("scalar")
        s_sq = P.now("s")
        for tt in range(NTT):
            tq = tt + NTT * (gx % 2)
            vst(lambda e, tq=tq, tt=tt: e.tensor_reduce(
                ST[:, tq, S_MAX:S_MAX + 1], src(tt, slice(0, 2048)), AX.X, AL.max))
            vst(lambda e, tq=tq, tt=tt: e.tensor_reduce(
                ST[:, tq, S_MIN:S_MIN + 1], src(tt, slice(0, 2048)), AX.X, AL.min))
        P.wait("vector", "s", s_sq)
        for tt in range(NTT):
            tq = tt + NTT * (gx % 2)
            vst(lambda e, tq=tq: e.tensor_tensor(
                ST[:, tq, S_SSQ:S_SSQ + 1], ST[:, tq, S_Q1:S_Q1 + 1],
                ST[:, tq, S_Q2:S_Q2 + 1], AL.add))
            vst(lambda e, tq=tq: e.tensor_scalar(
                ST[:, tq, S_MU:S_MU + 1], ST[:, tq, S_SUM:S_SUM + 1], RC, 0.0,
                AL.mult, AL.add))
            vst(lambda e, tq=tq: e.tensor_tensor(
                ST[:, tq, S_T1:S_T1 + 1], ST[:, tq, S_MU:S_MU + 1],
                ST[:, tq, S_MU:S_MU + 1], AL.mult))
            vst(lambda e, tq=tq: e.scalar_tensor_tensor(
                ST[:, tq, S_VAR:S_VAR + 1], ST[:, tq, S_SSQ:S_SSQ + 1], RC,
                ST[:, tq, S_T1:S_T1 + 1], AL.mult, AL.subtract))
        v_var = P.now("v")
        P.wait("scalar", "v", v_var)
        for tt in range(NTT):
            tq = tt + NTT * (gx % 2)
            P.op("scalar", lambda e, tq=tq: e.activation(
                ST[:, tq, S_DEN:S_DEN + 1], ST[:, tq, S_VAR:S_VAR + 1], AF.Sqrt,
                bias=EPS, scale=1.0), inc="s")
        s_den = P.now("s")
        P.wait("vector", "s", s_den)
        for tt in range(NTT):
            tq = tt + NTT * (gx % 2)
            vst(lambda e, tq=tq: e.reciprocal(
                ST[:, tq, S_T1:S_T1 + 1], ST[:, tq, S_DEN:S_DEN + 1]))
            vst(lambda e, tq=tq: e.tensor_tensor(
                ST[:, tq, S_T2:S_T2 + 1], ST[:, tq, S_DEN:S_DEN + 1],
                ST[:, tq, S_T1:S_T1 + 1], AL.mult))
            vst(lambda e, tq=tq: e.tensor_scalar(
                ST[:, tq, S_T2:S_T2 + 1], ST[:, tq, S_T2:S_T2 + 1], -1.0, 2.0,
                AL.mult, AL.add))
            vst(lambda e, tq=tq: e.tensor_tensor(
                ST[:, tq, S_RHO:S_RHO + 1], ST[:, tq, S_T1:S_T1 + 1],
                ST[:, tq, S_T2:S_T2 + 1], AL.mult))
            vst(lambda e, tq=tq: e.tensor_tensor(
                ST[:, tq, S_T1:S_T1 + 1], ST[:, tq, S_MAX:S_MAX + 1],
                ST[:, tq, S_MU:S_MU + 1], AL.subtract))
            vst(lambda e, tq=tq: e.tensor_tensor(
                ST[:, tq, S_T2:S_T2 + 1], ST[:, tq, S_MU:S_MU + 1],
                ST[:, tq, S_MIN:S_MIN + 1], AL.subtract))
            vst(lambda e, tq=tq: e.tensor_tensor(
                ST[:, tq, S_MX:S_MX + 1], ST[:, tq, S_T1:S_T1 + 1],
                ST[:, tq, S_T2:S_T2 + 1], AL.max))
            vst(lambda e, tq=tq: e.tensor_tensor(
                ST[:, tq, S_T1:S_T1 + 1], ST[:, tq, S_MX:S_MX + 1],
                ST[:, tq, S_RHO:S_RHO + 1], AL.mult))
            vst(lambda e, tq=tq: e.tensor_scalar(
                ST[:, tq, S_SX:S_SX + 1], ST[:, tq, S_T1:S_T1 + 1], 1.0 / 127.0,
                EPS, AL.mult, AL.add))
            vst(lambda e, tq=tq: e.reciprocal(
                ST[:, tq, S_T1:S_T1 + 1], ST[:, tq, S_SX:S_SX + 1]))
            vst(lambda e, tq=tq: e.tensor_tensor(
                ST[:, tq, S_T2:S_T2 + 1], ST[:, tq, S_SX:S_SX + 1],
                ST[:, tq, S_T1:S_T1 + 1], AL.mult))
            vst(lambda e, tq=tq: e.tensor_scalar(
                ST[:, tq, S_T2:S_T2 + 1], ST[:, tq, S_T2:S_T2 + 1], -1.0, 2.0,
                AL.mult, AL.add))
            vst(lambda e, tq=tq: e.tensor_tensor(
                ST[:, tq, S_RQ:S_RQ + 1], ST[:, tq, S_T1:S_T1 + 1],
                ST[:, tq, S_T2:S_T2 + 1], AL.mult))
            vst(lambda e, tq=tq: e.tensor_tensor(
                ST[:, tq, S_AA:S_AA + 1], ST[:, tq, S_RHO:S_RHO + 1],
                ST[:, tq, S_RQ:S_RQ + 1], AL.mult))
            vst(lambda e, tq=tq: e.tensor_tensor(
                ST[:, tq, S_BB:S_BB + 1], ST[:, tq, S_MU:S_MU + 1],
                ST[:, tq, S_AA:S_AA + 1], AL.mult))
            vst(lambda e, tq=tq: e.tensor_tensor(
                ST[:, tq, S_SG:S_SG + 1], ST[:, tq, S_SX:S_SX + 1],
                thbc[:, 6:7], AL.mult))
            vst(lambda e, tq=tq: e.tensor_tensor(
                ST[:, tq, S_SU:S_SU + 1], ST[:, tq, S_SX:S_SX + 1],
                thbc[:, 7:8], AL.mult))
            vst(lambda e, tq=tq: e.tensor_tensor(
                ST[:, tq, S_MURHO:S_MURHO + 1], ST[:, tq, S_MU:S_MU + 1],
                ST[:, tq, S_RHO:S_RHO + 1], AL.mult))
        v_stats = P.now("v")
        P.wait("scalar", "v", v_stats)
        P.op("scalar", lambda e: e.copy(
            ST[:, :, S_SA:S_SA + 2], ST[:, :, S_AA:S_AA + 2]), inc="s")
        P.wait("vector", "s", P.now("s"))

    def emit_xq(gx, src, t_gate=None):
        if t_gate is not None:
            P.wait("vector", "t", t_gate)
        for tt in range(NTT):
            tq = tt + NTT * (gx % 2)
            for hh in range(2):
                cl = slice(1024 * hh, 1024 * (hh + 1))
                vst(lambda e, tq=tq, tt=tt, hh=hh, cl=cl: e.tensor_scalar(
                    ui_sc.ap()[:, :], src(tt, cl),
                    ST[:, tq, S_SA:S_SA + 1], ST[:, tq, S_SB2:S_SB2 + 1],
                    AL.mult, AL.subtract))
                vst(lambda e, tt=tt, cl=cl: e.tensor_copy(
                    xq_tm.ap()[:, tt, cl], ui_sc.ap()[:, :]))
        return P.now("v")

    def emit_xqT(v_xq, v_bank=None):
        P.wait("tensor", "v", v_xq)
        if v_bank is not None:
            P.wait("tensor", "v", v_bank)
        for tt in range(NTT):
            for kb in range(2):
                for j in range(8):
                    k = kb * 8 + j
                    P.op("tensor", lambda e, tt=tt, k=k, j=j: e.transpose(
                        psT[:, 128 * j:128 * (j + 1)],
                        xq_tm.ap()[:, tt, 128 * k:128 * (k + 1)], ID), inc="t")
                t_tr = P.now("t")
                P.wait("scalar", "t", t_tr)
                P.op("scalar", lambda e, tt=tt, kb=kb: e.copy(
                    xqT.ap()[:, 8 * kb:8 * kb + 8, 128 * tt:128 * (tt + 1)],
                    psT[:, 0:1024].rearrange("p (j b) -> p j b", j=8)), inc="s")
                P.wait("tensor", "s", P.now("s"))
        return P.now("s")

    v_hs = load_hs(0)
    t_wo = emit_wout(0, 0)
    v_r1 = emit_r1evac(0, 0, r1_t.ap()[:, 0, :], v_hs, t_wo)
    v_hs = load_hs(1)
    P.wait("tensor", "v", v_r1)
    t_wo = emit_wout(0, 1)
    v_r1 = emit_r1evac(0, 1, r1_t.ap()[:, 1, :], v_hs, t_wo)

    def src_r1t(tt, cl):
        return r1_t.ap()[:, tt, cl]

    def src_stage(tt, cl):
        return m_ar.ap()[:, tt, STAGE + cl.start:STAGE + cl.stop]

    emit_ln1(0, src_r1t, v_gate=v_r1)
    v_xq = emit_xq(0, src_r1t)
    s_xqT = {0: emit_xqT(v_xq)}
    ln2v = {-1: (0, 0)}

    for g in range(NG):
        gp = g % 2
        last = (g == NG - 1)

        P.wait("tensor", "s", s_xqT[g])
        if g >= 1:
            P.wait("tensor", "v", ln2v[g - 1][0])
        if not last:
            v_hs = load_hs((g + 1) * NTT + 0)
        sil_prev = [0]
        mstt_prev = [0]
        for c in range(16):
            slot_g, vg = load_gu((g, 2 * c), wgTr, c)
            slot_u, vu = load_gu((g, 2 * c + 1), wuTr, c)
            if c == 0:
                P.wait("vector", "qs", prev_qs[0])
            P.wait("tensor", f"qu{slot_g}", vg)
            P.wait("tensor", "s", sil_prev[0])
            for tt in range(NTT):
                ps = psA if tt == 0 else psB
                def gmf(e, tt=tt, ps=ps, slot=slot_g):
                    r = None
                    for k in range(16):
                        r = e.matmul(ps[:, 0:512],
                                     xqT.ap()[:, k, 128 * tt:128 * (tt + 1)],
                                     gu_ring[slot].ap()[:, k, :],
                                     start=(k == 0), stop=(k == 15))
                    return r
                P.op("tensor", gmf, inc="t")
            gu_consumed[(g, 2 * c)] = P.now("t")
            t_gate = P.now("t")
            P.wait("scalar", "t", t_gate)
            for tt in range(NTT):
                ps = psA if tt == 0 else psB
                tq = tt + NTT * gp
                sgi = 2 * (c % 2) + tt
                P.op("scalar", lambda e, tq=tq, ps=ps, sgi=sgi: e.activation(
                    sg_sl[sgi].ap()[:, 0:512], ps[:, 0:512],
                    AF.Silu, scale=ST[:, tq, S_SG:S_SG + 1]), inc="s")
            sil_prev[0] = P.now("s")
            P.wait("tensor", f"qu{slot_u}", vu)
            P.wait("tensor", "v", mstt_prev[0])
            for tt in range(NTT):
                ps = psE if tt == 0 else psU
                def umf(e, tt=tt, ps=ps, slot=slot_u):
                    r = None
                    for k in range(16):
                        r = e.matmul(ps[:, 0:512],
                                     xqT.ap()[:, k, 128 * tt:128 * (tt + 1)],
                                     gu_ring[slot].ap()[:, k, :],
                                     start=(k == 0), stop=(k == 15))
                    return r
                P.op("tensor", umf, inc="t")
            gu_consumed[(g, 2 * c + 1)] = P.now("t")
            t_up = P.now("t")
            P.wait("vector", "t", t_up)
            P.wait("vector", "s", sil_prev[0])
            for tt in range(NTT):
                ps = psE if tt == 0 else psU
                tq = tt + NTT * gp
                sgi = 2 * (c % 2) + tt
                P.op("vector", lambda e, tq=tq, tt=tt, ps=ps, c=c, sgi=sgi:
                     e.scalar_tensor_tensor(
                         m_ar.ap()[:, tt, 512 * c:512 * c + 512], ps[:, 0:512],
                         ST[:, tq, S_SU:S_SU + 1], sg_sl[sgi].ap()[:, 0:512],
                         AL.mult, AL.mult), inc="v")
            mstt_prev[0] = P.now("v")
        P.drain("vector")

        if not last:
            P.wait("tensor", "v", ln2v[g - 1][1])
            t_wo0 = emit_wout(g + 1, 0)

        for tt in range(NTT):
            tq = tt + NTT * gp
            vst(lambda e, tq=tq, tt=tt: e.tensor_reduce(
                ST[:, tq, S_MM:S_MM + 1], m_ar.ap()[:, tt, :], AX.X, AL.max,
                apply_absolute_value=True))
            vst(lambda e, tq=tq: e.tensor_scalar(
                ST[:, tq, S_SM:S_SM + 1], ST[:, tq, S_MM:S_MM + 1], 1.0 / 127.0,
                EPS, AL.mult, AL.add))
            vst(lambda e, tq=tq: e.reciprocal(
                ST[:, tq, S_T1:S_T1 + 1], ST[:, tq, S_SM:S_SM + 1]))
            vst(lambda e, tq=tq: e.tensor_tensor(
                ST[:, tq, S_T2:S_T2 + 1], ST[:, tq, S_SM:S_SM + 1],
                ST[:, tq, S_T1:S_T1 + 1], AL.mult))
            vst(lambda e, tq=tq: e.tensor_scalar(
                ST[:, tq, S_T2:S_T2 + 1], ST[:, tq, S_T2:S_T2 + 1], -1.0, 2.0,
                AL.mult, AL.add))
            vst(lambda e, tq=tq: e.tensor_tensor(
                ST[:, tq, S_RQM:S_RQM + 1], ST[:, tq, S_T1:S_T1 + 1],
                ST[:, tq, S_T2:S_T2 + 1], AL.mult))
            vst(lambda e, tq=tq: e.tensor_tensor(
                ST[:, tq, S_SMW:S_SMW + 1], ST[:, tq, S_SM:S_SM + 1],
                thbc[:, 8:9], AL.mult))
        v_msc = P.now("v")
        P.wait("scalar", "v", v_msc)
        P.op("scalar", lambda e: e.copy(
            ST[:, :, S_SA:S_SA + 1], ST[:, :, S_RQM:S_RQM + 1]), inc="s")
        P.wait("vector", "s", P.now("s"))

        tr_prev = [0]

        def qrun(tt):
            tq = tt + NTT * gp
            for cc in range(8):
                cl = slice(1024 * cc, 1024 * (cc + 1))
                vst(lambda e, tq=tq, tt=tt, cl=cl: e.tensor_scalar(
                    ui_sc.ap()[:, :], m_ar.ap()[:, tt, cl],
                    ST[:, tq, S_SA:S_SA + 1], 0.0, AL.mult, AL.add))
                P.wait("vector", "t", tr_prev[0])
                vst(lambda e: e.tensor_copy(mqs.ap()[:, :], ui_sc.ap()[:, :]))
                v_mq = P.now("v")
                P.wait("tensor", "v", v_mq)
                for j in range(8):
                    P.op("tensor", lambda e, j=j: e.transpose(
                        psT[:, 128 * j:128 * (j + 1)],
                        mqs.ap()[:, 128 * j:128 * (j + 1)], ID), inc="t")
                tr_prev[0] = P.now("t")
                P.wait("scalar", "t", tr_prev[0])
                P.op("scalar", lambda e, tt=tt, cc=cc: e.copy(
                    mqT.ap()[:, tt, 8 * cc:8 * cc + 8, :],
                    psT[:, 0:1024].rearrange("p (j b) -> p j b", j=8)), inc="s")
                P.wait("tensor", "s", P.now("s"))

        qrun(0)
        if not last:
            v_r1e = emit_r1evac(g + 1, 0, m_ar.ap()[:, 0, STAGE:STAGE + 2048],
                                v_hs, t_wo0)
            v_hs = load_hs((g + 1) * NTT + 1)
            P.wait("tensor", "v", v_r1e)
            t_wo1 = emit_wout(g + 1, 1)
        qrun(1)
        if not last:
            v_r1e = emit_r1evac(g + 1, 1, m_ar.ap()[:, 1, STAGE:STAGE + 2048],
                                v_hs, t_wo1)

        if not last:
            P.wait("tensor", "v", v_r1e)
        t_k16 = t_k48 = None
        for k in range(64):
            slot, vw = load_wd((g, k), k)
            P.wait("tensor", f"qd{slot}", vw)
            for tt in range(NTT):
                ps = psD if tt == 0 else psD2
                for n in range(4):
                    P.op("tensor", lambda e, tt=tt, k=k, n=n, ps=ps, slot=slot:
                         e.matmul(ps[:, 512 * n:512 * n + 512],
                                  mqT.ap()[:, tt, k, :],
                                  wd_ring[slot].ap()[:, 512 * n:512 * n + 512],
                                  start=(k == 0), stop=(k == 63)), inc="t")
            wd_consumed[(g, k)] = P.now("t")
            if k == 15:
                t_k16 = P.now("t")
            elif k == 47:
                t_k48 = P.now("t")
        t_down = P.now("t")

        if not last:
            emit_ln1(g + 1, src_stage, t_gate=t_k16, v_gate=v_r1e)
            v_xq = emit_xq(g + 1, src_stage, t_gate=t_k48)

        P.wait("vector", "t", t_down)
        marks = {}
        for tt in (1, 0):
            tq = tt + NTT * gp
            ps = psD if tt == 0 else psD2
            for hh in range(2):
                cl = slice(1024 * hh, 1024 * (hh + 1))
                vst(lambda e, tq=tq, tt=tt, ps=ps, cl=cl: e.tensor_scalar(
                    c_sc.ap()[:, :], ps[:, cl],
                    ST[:, tq, S_SMW:S_SMW + 1], ST[:, tq, S_MURHO:S_MURHO + 1],
                    AL.mult, AL.subtract))
                vst(lambda e, tq=tq, tt=tt, hh=hh, cl=cl:
                    e.scalar_tensor_tensor(
                        m_ar.ap()[:, tt, cl], r1_t.ap()[:, tt, cl],
                        ST[:, tq, S_RHO:S_RHO + 1], c_sc.ap()[:, :],
                        AL.mult, AL.add,
                        accum_out=ST[:, tq, S_T1 + hh:S_T1 + hh + 1]))
            vst(lambda e, tq=tq: e.tensor_tensor(
                ST[:, tq, S_SUM2:S_SUM2 + 1], ST[:, tq, S_T1:S_T1 + 1],
                ST[:, tq, S_T2:S_T2 + 1], AL.add))
            marks[tt] = P.now("v")
            if tt == 1 and not last:
                s_xqT[g + 1] = emit_xqT(v_xq, v_bank=marks[1])
        ln2v[g] = (marks[1], marks[0])

        if not last:
            for tt in range(NTT):
                vst(lambda e, tt=tt: e.tensor_copy(
                    r1_t.ap()[:, tt, :], m_ar.ap()[:, tt, STAGE:STAGE + 2048]))

        P.wait("scalar", "v", marks[0])
        for tt in range(NTT):
            tq = tt + NTT * gp
            for hh in range(2):
                P.op("scalar", lambda e, tq=tq, tt=tt, hh=hh: e.activation(
                    sq_tr.ap()[:, :],
                    m_ar.ap()[:, tt, 1024 * hh:1024 * (hh + 1)],
                    AF.Square, accum_out=ST[:, tq, S_Q1 + hh:S_Q1 + hh + 1]),
                    inc="s")
                P.drain("scalar")
        s_sq2 = P.now("s")
        P.wait("vector", "s", s_sq2)
        for tt in range(NTT):
            tq = tt + NTT * gp
            vst(lambda e, tq=tq: e.tensor_tensor(
                ST[:, tq, S_SSQ:S_SSQ + 1], ST[:, tq, S_Q1:S_Q1 + 1],
                ST[:, tq, S_Q2:S_Q2 + 1], AL.add))
            vst(lambda e, tq=tq: e.tensor_scalar(
                ST[:, tq, S_MU:S_MU + 1], ST[:, tq, S_SUM2:S_SUM2 + 1], RC, 0.0,
                AL.mult, AL.add))
            vst(lambda e, tq=tq: e.tensor_tensor(
                ST[:, tq, S_T1:S_T1 + 1], ST[:, tq, S_MU:S_MU + 1],
                ST[:, tq, S_MU:S_MU + 1], AL.mult))
            vst(lambda e, tq=tq: e.scalar_tensor_tensor(
                ST[:, tq, S_VAR:S_VAR + 1], ST[:, tq, S_SSQ:S_SSQ + 1], RC,
                ST[:, tq, S_T1:S_T1 + 1], AL.mult, AL.subtract))
        v_var2 = P.now("v")
        P.wait("scalar", "v", v_var2)
        for tt in range(NTT):
            tq = tt + NTT * gp
            P.op("scalar", lambda e, tq=tq: e.activation(
                ST[:, tq, S_DEN:S_DEN + 1], ST[:, tq, S_VAR:S_VAR + 1], AF.Sqrt,
                bias=EPS, scale=1.0), inc="s")
        s_den2 = P.now("s")
        P.wait("vector", "s", s_den2)
        for tt in range(NTT):
            tq = tt + NTT * gp
            vst(lambda e, tq=tq: e.reciprocal(
                ST[:, tq, S_RHO2:S_RHO2 + 1], ST[:, tq, S_DEN:S_DEN + 1]))
            vst(lambda e, tq=tq: e.tensor_tensor(
                ST[:, tq, S_MR2:S_MR2 + 1], ST[:, tq, S_MU:S_MU + 1],
                ST[:, tq, S_RHO2:S_RHO2 + 1], AL.mult))
        v_r2 = P.now("v")
        P.wait("scalar", "v", v_r2)
        P.op("scalar", lambda e: e.copy(
            ST[:, :, S_RHO2:S_RHO2 + 2], ST[:, :, S_RHO2:S_RHO2 + 2]), inc="s")
        P.wait("vector", "s", P.now("s"))
        for tt in range(NTT):
            tq = tt + NTT * gp
            tg = g * NTT + tt
            vst(lambda e, tq=tq, tt=tt: e.tensor_scalar(
                m_ar.ap()[:, tt, 0:2048], m_ar.ap()[:, tt, 0:2048],
                ST[:, tq, S_RHO2:S_RHO2 + 1], ST[:, tq, S_MR2:S_MR2 + 1],
                AL.mult, AL.subtract))
            v_fin = P.now("v")
            P.wait("scalar", "v", v_fin)
            P.op("scalar", lambda e, tt=tt, tg=tg: e.dma_start(
                out=dt["out"][128 * tg:128 * (tg + 1), :],
                in_=m_ar.ap()[:, tt, 0:2048]), inc="qs", inc_by=16)
        prev_qs[0] = P.now("qs")

    P.wait("gpsimd", "qs", P.now("qs"))
    return _finish(nc, P, dt)


def _finish(nc, P, dt):
    with nc.Block() as block:
        @block.sync
        def _(eng):
            P.emit("sync", eng)

        @block.vector
        def _(eng):
            P.emit("vector", eng)

        @block.scalar
        def _(eng):
            P.emit("scalar", eng)

        @block.tensor
        def _(eng):
            P.emit("tensor", eng)

        @block.gpsimd
        def _(eng):
            P.emit("gpsimd", eng)

    return nc, dt


def _host_layout(inputs):
    f32 = np.float32
    hs = np.asarray(inputs["L0_attn_hidden_states"][0], f32)
    hsT = np.ascontiguousarray(hs.T)
    hsT_ext = np.concatenate([np.zeros((D, 3), f32), hsT], axis=1)

    wxT = np.ascontiguousarray(np.asarray(inputs["L0_attn_W_in_x"], f32).T)
    wzT = np.ascontiguousarray(np.asarray(inputs["L0_attn_W_in_z"], f32).T)
    wpT = np.ascontiguousarray(np.asarray(inputs["L0_attn_W_param"], f32).T)
    woT = np.ascontiguousarray(np.asarray(inputs["L0_attn_W_out"], f32).T)
    wdt = np.asarray(inputs["L0_attn_W_dt"], f32)
    wdtT_rep = np.ascontiguousarray(np.repeat(wdt.T, 16, axis=1))
    wgT = np.ascontiguousarray(np.asarray(inputs["L0_ffn_w_gate"], f32).T)
    wuT = np.ascontiguousarray(np.asarray(inputs["L0_ffn_w_up"], f32).T)
    wdT = np.ascontiguousarray(np.asarray(inputs["L0_ffn_w_down"], f32).T)

    def chmat(v):
        return np.ascontiguousarray(np.asarray(v, f32).reshape(CG, 128).T)

    conv_w = np.asarray(inputs["L0_attn_conv_w"], f32).reshape(C, 4)
    conv_wm = np.zeros((128, CG * 4), f32)
    for g in range(CG):
        for k in range(4):
            conv_wm[:, 4 * g + k] = conv_w[128 * g:128 * (g + 1), k]
    conv_b = chmat(inputs["L0_attn_conv_b"])
    a_log = chmat(np.asarray(inputs["L0_attn_A_log"], f32).reshape(C))
    bdt_rep = chmat(np.repeat(np.asarray(inputs["L0_attn_b_dt"], f32), 16))
    d_vec = chmat(inputs["L0_attn_D"])

    ident = np.eye(128, dtype=ml_dtypes.bfloat16)
    ident32 = np.eye(128, dtype=f32)
    ones1 = np.ones((1, 128), f32)
    zcol = np.zeros((128, 1), f32)

    in_maps = []
    for c in range(NCORES):
        pm = np.zeros((128, NCORES), f32)
        pm[:, :c] = 1.0
        im = {
            "hsT": np.ascontiguousarray(hsT_ext[:, T * c:T * c + T + 3]),
            "hs": np.ascontiguousarray(hs[T * c:T * (c + 1)]),
            "wxT": wxT, "wzT": wzT, "wpT": wpT, "woT": woT,
            "wdtT_rep": wdtT_rep,
            "conv_w": conv_wm, "conv_b": conv_b, "a_log": a_log,
            "bdt_rep": bdt_rep, "d_vec": d_vec,
            "ident": ident, "ident32": ident32,
            "pmask": pm, "pmask1m": np.ascontiguousarray(1.0 - pm),
            "zcol": zcol, "ones1": ones1,
            "cvals": np.tile(np.array([[EPS, 1.0, 0.5, -0.5]], f32), (128, 1)),
            "wgT_sl": np.ascontiguousarray(wgT[:, 1024 * c:1024 * (c + 1)]),
            "wuT_sl": np.ascontiguousarray(wuT[:, 1024 * c:1024 * (c + 1)]),
            "wdT_sl": np.ascontiguousarray(wdT[1024 * c:1024 * (c + 1)]),
        }
        in_maps.append(im)
    return in_maps


_BUILT = {}


def kernel(**inputs):
    if "nc" not in _BUILT:
        nc, dtm = build_program()
        _BUILT["nc"] = nc
    nc = _BUILT["nc"]
    in_maps = _host_layout(inputs)
    res = run_bass_kernel_spmd(nc, in_maps, core_ids=list(range(NCORES)))
    out = np.concatenate([np.asarray(r["out"]) for r in res.results], axis=0)
    return out.reshape(1, L, D).astype(np.float32)


# version 20
# speedup vs baseline: 1.3076x; 1.0008x over previous
import numpy as np
import ml_dtypes
import concourse.bass as bass
import concourse.mybir as mybir
from concourse.bass_utils import run_bass_kernel_spmd

F32 = mybir.dt.float32
BF16 = mybir.dt.bfloat16
F8 = mybir.dt.float8e4
I32 = mybir.dt.int32
AL = mybir.AluOpType
AF = mybir.ActivationFunctionType
AX = mybir.AxisListType

NCORES = 8
D = 2048
L = 16384
T = L // NCORES
C = 512
CG = 4
DTR = 128
FF = 8192
NP = 1152
EPS = 1e-5
TT = 256
NG = T // TT
NTT = TT // 128

TAPS = {}
STOP_AFTER = None
DEV_BARRIER = False

KB = 1024


def rnd(x, m=32):
    return (x + m - 1) // m * m


class Prog:
    def __init__(self, nc):
        self.nc = nc
        self.eng = {e: [] for e in ("sync", "vector", "scalar", "tensor", "gpsimd")}
        self.sem = {}
        self.ctr = {}

    def add_sem(self, name):
        self.sem[name] = self.nc.semaphore(name).__enter__()
        self.ctr[name] = 0

    def op(self, engine, fn, inc=None, inc_by=1):
        if inc is not None:
            self.ctr[inc] += inc_by
        self.eng[engine].append(("op", fn, inc, inc_by))
        return self.ctr[inc] if inc is not None else None

    def dma(self, engine, out, in_, inc, slow=False):
        self.ctr[inc] += 16
        self.eng[engine].append(
            ("op", (lambda e, o=out, i=in_, s=slow: e.dma_start(
                out=o, in_=i, allow_slow_non_contiguous=s)), inc, 16))
        return self.ctr[inc]

    def drain(self, engine):
        self.eng[engine].append(("op", (lambda e: e.drain()), None, 1))

    def wait(self, engine, semname, value=None):
        v = self.ctr[semname] if value is None else value
        self.eng[engine].append(("wait", semname, v))
        return v

    def now(self, semname):
        return self.ctr[semname]

    def emit(self, engine_name, eng):
        for entry in self.eng[engine_name]:
            if entry[0] == "wait":
                eng.wait_ge(self.sem[entry[1]], entry[2])
            else:
                _, fn, inc, inc_by = entry
                inst = fn(eng)
                if inc is not None:
                    inst.then_inc(self.sem[inc], inc_by)


class Arena:
    SB_BYTES = 206 * KB

    def __init__(self, nc):
        self.nc = nc
        self.backing = nc.sbuf_tensor(
            "arena", [128, self.SB_BYTES], mybir.dt.uint8).__enter__()
        self.base = nc.lookup_mloc(self.backing).addr

    def t(self, name, shape, dtype, off):
        full = [128] + list(shape)
        per_part = int(np.prod(shape)) * mybir.dt.size(dtype)
        assert off % 32 == 0, (name, off)
        assert off + per_part <= self.SB_BYTES, (name, off, per_part, off + per_part)
        h = self.nc.virtual_sbuf_tensor(name, full, dtype)
        self.nc._tensor(h.name, h.shape, h.dtype, type="SB")
        mloc = self.nc.lookup_mloc(h)
        mloc.allocated = True
        mloc.addr = self.base + off
        return h


(S_SUM, S_SSQ, S_MU, S_VAR, S_DEN, S_RHO, S_MAX, S_MIN, S_MX, S_SX, S_RQ,
 S_AA, S_BB, S_SG, S_SU, S_MM, S_SM, S_RQM, S_SMW, S_MURHO, S_SUM2, S_T1,
 S_T2, S_T3, S_RHO2, S_MR2, S_SA, S_SB2, S_Q1, S_Q2) = range(30)


def build_program():
    nc = bass.Bass("TRN2", target_bir_lowering=False)
    A = Arena(nc)
    dt = {}

    def din(name, shape, dtype=F32):
        dt[name] = nc.dram_tensor(name, list(shape), dtype, kind="ExternalInput").ap()

    def dout(name, shape, dtype=F32):
        dt[name] = nc.dram_tensor(name, list(shape), dtype, kind="ExternalOutput").ap()

    def dint(name, shape, dtype=F32):
        dt[name] = nc.dram_tensor(name, list(shape), dtype, kind="Internal").ap()

    din("hsT", (D, T + 3))
    din("hs", (T, D))
    din("wxT", (D, C))
    din("wzT", (D, C))
    din("wpT", (C, NP))
    din("woT", (C, D))
    din("wdtT_rep", (DTR, C))
    din("conv_w", (128, CG * 4))
    din("conv_b", (128, CG))
    din("a_log", (128, CG))
    din("bdt_rep", (128, CG))
    din("d_vec", (128, CG))
    din("ident", (128, 128), BF16)
    din("ident32", (128, 128), F32)
    din("pmask", (128, NCORES))
    din("pmask1m", (128, NCORES))
    din("zcol", (128, 1))
    din("cvals", (128, 4))
    din("ones1", (1, 128))
    din("wgT_sl", (D, FF // NCORES))
    din("wuT_sl", (D, FF // NCORES))
    din("wdT_sl", (FF // NCORES, D))
    dout("out", (T, D))

    dint("ar_in", (128, 40))
    dint("ar_out", (128, 40))
    dint("red_lin", (40 * 128,))
    dint("agw_in_g", (4, 128, 16, 256), F8)
    dint("agw_out_g", (NCORES, 4, 128, 16, 256), F8)
    dint("agw_in_u", (4, 128, 16, 256), F8)
    dint("agw_out_u", (NCORES, 4, 128, 16, 256), F8)
    dint("agw_in_d", (FF // NCORES, D), F8)
    dint("agw_out_d", (FF, D), F8)
    dint("ags_in", (1024,))
    dint("ags_out", (NCORES, 1024))

    O_Y = 170 * KB
    O_WO = 186 * KB
    O_PERM = 202 * KB + 512
    y_bf = A.t("y_bf", (CG, T), BF16, O_Y)
    woT_bf = A.t("woT_bf", (CG, D), BF16, O_WO)

    consts = {}
    pbase = [O_PERM]

    def perm(name, shape, dtype):
        consts[name] = A.t("c_" + name, shape, dtype, pbase[0])
        pbase[0] += rnd(int(np.prod(shape)) * mybir.dt.size(dtype))
        return consts[name]

    perm("conv_w", (CG * 4,), F32)
    perm("conv_b", (CG,), F32)
    perm("a_flat", (CG,), F32)
    perm("bdt", (CG,), F32)
    perm("d_vec", (CG,), F32)
    perm("ident", (128,), BF16)
    perm("ident32", (128,), F32)
    perm("pmask", (NCORES,), F32)
    perm("pmask1m", (NCORES,), F32)
    perm("zcol", (1,), F32)
    perm("ones1", (128,), F32)
    perm("thbc", (12,), F32)
    perm("red", (48,), F32)
    perm("allst", (NCORES, 2, CG), F32)
    perm("hprev", (2, CG), F32)
    perm("ptot", (CG,), F32)

    perm("stats", (2 * NTT, 30), F32)
    perm("cvals", (4,), F32)
    assert pbase[0] <= Arena.SB_BYTES, pbase[0]
    nc.const_aps.register(F32, EPS, consts["cvals"].ap()[:, 0:1])
    nc.const_aps.register(F32, 1.0, consts["cvals"].ap()[:, 1:2])
    nc.const_aps.register(F32, 0.5, consts["cvals"].ap()[:, 2:3])
    nc.const_aps.register(F32, -0.5, consts["cvals"].ap()[:, 3:4])

    HS_W = 544
    hsT_r = [A.t(f"hsT_r{i}", (16, HS_W), BF16, i * rnd(16 * HS_W * 2))
             for i in range(2)]
    o = 2 * rnd(16 * HS_W * 2)
    O_WX = o
    wxT_bf = A.t("wxT_bf", (16, C), BF16, o); o += rnd(16 * C * 2)
    wzT_bf = A.t("wzT_bf", (16, C), BF16, o); o += rnd(16 * C * 2)
    wpT_bf = A.t("wpT_bf", (CG, NP), BF16, o); o += rnd(CG * NP * 2)
    wdt_bf = A.t("wdt_bf", (C,), BF16, o); o += rnd(C * 2)
    O_XFM = o
    x_fm = A.t("x_fm", (CG, T + 3), F32, o); o += rnd(CG * (T + 3) * 4)
    O_Z = max(rnd(o), 112 * KB)
    z_bf = A.t("z_bf", (CG, T), BF16, O_Z)
    O_ACC = O_Z + rnd(CG * T * 2)
    acc_fm = A.t("acc_fm", (CG, T), F32, O_ACC)
    b_fm = A.t("b_fm", (CG, T), F32, O_ACC)
    O_DT = O_ACC + rnd(CG * T * 4)
    dt_bf = A.t("dt_bf", (T,), BF16, O_DT)
    assert O_DT + rnd(T * 2) <= O_Y, (O_DT + rnd(T * 2), O_Y)

    x_act_bf = A.t("x_act_bf", (CG, T), BF16, 0)
    c_fm = A.t("c_fm", (CG, T), F32, 16 * KB)
    h_fm = A.t("h_fm", (CG, T), F32, 48 * KB)
    O_DELTA = 80 * KB
    delta_fm = A.t("delta_fm", (CG, T), F32, O_DELTA)
    assert O_DELTA >= O_XFM
    assert O_DELTA + CG * T * 4 <= O_Z, (O_DELTA + CG * T * 4, O_Z)
    assert 48 * KB + CG * T * 4 <= O_DELTA

    red1 = A.t("red1", (40 * 128,), F32, O_ACC)

    a1_ring = [A.t(f"a1_r{i}", (FF,), F32, i * rnd(FF * 4)) for i in range(2)]
    a2_ring = [A.t(f"a2_r{i}", (FF,), F32, i * rnd(FF * 4)) for i in range(2)]
    a2_out = [A.t(f"a2_o{i}", (FF,), F8, 2 * rnd(FF * 4) + i * rnd(FF * 2))
              for i in range(2)]
    a2_sc = A.t("a2_sc", (FF,), F32, 2 * rnd(FF * 4) + 2 * rnd(FF * 2))

    m_ar = A.t("m_ar", (NTT, FF), F32, 0)
    ui_sc = A.t("ui_sc", (1024,), I32, 64 * KB)
    c_sc = A.t("c_sc", (1024,), F32, 68 * KB)
    mqs = A.t("mqs", (1024,), BF16, 72 * KB)
    r1_t = A.t("r1_t", (NTT, D), F32, 74 * KB)
    O_BLK = 90 * KB
    sq_tr = A.t("sq_tr", (1024,), F32, O_BLK)
    xq_tm = A.t("xq_tm", (NTT, D), BF16, O_BLK + 4 * KB)
    xqT = A.t("xqT", (16, TT), BF16, O_BLK + 12 * KB)
    sg_sl = [A.t(f"sg_s{i}", (512,), F32, O_BLK + 20 * KB + i * rnd(512 * 4))
             for i in range(4)]
    mqT = A.t("mqT", (2, 64, 128), BF16, O_BLK)
    O_WD = 122 * KB
    wd_ring = [A.t(f"wd_r{i}", (D,), F8, O_WD + i * rnd(D))
               for i in range(4)]
    hs_sb = A.t("hs_sb", (D,), F32, 130 * KB)
    O_GU = 138 * KB
    gu_ring = [A.t(f"gu_r{i}", (16, 512), F8, O_GU + i * rnd(16 * 512))
               for i in range(3)]
    assert O_GU + 3 * rnd(16 * 512) <= O_Y, (O_GU + 3 * rnd(16 * 512), O_Y)

    psD = nc.psum_tensor("psD", [128, 2048], F32).__enter__()
    psD2 = nc.psum_tensor("psD2", [128, 2048], F32).__enter__()
    psA = psD2[:, 0:512]
    psB = psD2[:, 512:1024]
    psE = psD2[:, 1024:1536]
    psT = psD2[:, 1536:2048].bitcast(BF16)
    psU = psD2[:, 1536:2048]

    P = Prog(nc)
    for s in ("qc", "qmw", "qr", "qx", "qs", "qh", "qt0", "qt1",
              "qa0", "qa1", "qo0", "qo1",
              "qu0", "qu1", "qu2", "qd0", "qd1", "qd2", "qd3",
              "v", "s", "t", "cc"):
        P.add_sem(s)

    consts["ags_sb"] = A.t("c_ags_sb", (1024,), F32, O_DT)
    ID = consts["ident"].ap()
    ID32 = consts["ident32"].ap()
    ST = consts["stats"].ap()
    thbc = consts["thbc"].ap()

    def tap(name, src_ap, shape, dtype=F32, waits=(), freeze=()):
        if name not in TAPS:
            return
        d = nc.dram_tensor("tap_" + name, list(shape), dtype,
                           kind="ExternalOutput").ap()
        for sem, val in waits:
            P.wait("gpsimd", sem, val)
        qv = P.dma("gpsimd", d, src_ap, "qs")
        for engn in freeze:
            P.wait(engn, "qs", qv)

    ld = {}
    for nm, tgt in [("conv_w", "conv_w"), ("conv_b", "conv_b"),
                    ("a_log", "a_flat"), ("bdt_rep", "bdt"),
                    ("d_vec", "d_vec"), ("ident", "ident"),
                    ("ident32", "ident32"), ("pmask", "pmask"),
                    ("pmask1m", "pmask1m"), ("zcol", "zcol"),
                    ("cvals", "cvals")]:
        ld[tgt] = P.dma("sync", consts[tgt].ap()[:, :], dt[nm], "qc")
    ld["ones1"] = P.dma("sync", consts["ones1"].ap()[0:1, :], dt["ones1"][0:1, :],
                        "qc")
    QC_ALL = P.now("qc")
    for k in ld:
        ld[k] = QC_ALL
    P.wait("scalar", "qc", ld["a_flat"])
    s_af = P.op("scalar", lambda e: e.activation(
        consts["a_flat"].ap()[:, :], consts["a_flat"].ap()[:, :], AF.Exp), inc="s")
    P.wait("vector", "s", s_af)
    v_af = P.op("vector", lambda e: e.tensor_scalar(
        consts["a_flat"].ap()[:, :], consts["a_flat"].ap()[:, :], -1.0, 0.0,
        AL.mult, AL.add), inc="v")

    a_slices = ([("wgT_sl", i, FF // NCORES) for i in range(16)]
                + [("wuT_sl", i, FF // NCORES) for i in range(16)]
                + [("wdT_sl", i, D) for i in range(8)])
    red = consts["red"].ap()
    red_done = []
    for j, (nm, i, w) in enumerate(a_slices):
        if j >= 2:
            P.wait("sync", "v", red_done[j - 2])
        v = P.dma("sync", a1_ring[j % 2].ap()[:, 0:w],
                  dt[nm][128 * i:128 * (i + 1), :], f"qa{j % 2}")
        P.wait("vector", f"qa{j % 2}", v)
        P.op("vector", lambda e, j=j, w=w: e.tensor_reduce(
            red[:, j:j + 1], a1_ring[j % 2].ap()[:, 0:w], AX.X, AL.add,
            apply_absolute_value=True), inc="v")
        red_done.append(P.now("v"))
    v_a1 = P.now("v")

    P.wait("gpsimd", "v", v_a1)
    v = P.dma("gpsimd", dt["ar_in"][:, 0:40], red[:, 0:40], "qr")
    P.wait("gpsimd", "qr", v)
    P.op("gpsimd", lambda e: e.collective_compute(
        "AllReduce", AL.add, replica_groups=[list(range(NCORES))],
        ins=[dt["ar_in"].opt()], outs=[dt["ar_out"].opt()]), inc="cc")
    P.wait("gpsimd", "cc", P.now("cc"))
    v = P.dma("gpsimd", dt["red_lin"][:].rearrange("(p k) -> p k", p=128),
              dt["ar_out"][:, :], "qr")
    P.wait("gpsimd", "qr", v)
    v = P.dma("gpsimd", red1.ap()[0:1, :], dt["red_lin"].rearrange("(o a) -> o a", o=1),
              "qr")
    P.wait("vector", "qr", v)
    rview = red1.ap()[0:1, :].rearrange("a (p k) -> a p k", k=40)
    P.op("vector", lambda e: e.tensor_reduce(
        red[0:1, 0:1], rview[:, :, 0:16], AX.XY, AL.add))
    P.op("vector", lambda e: e.tensor_reduce(
        red[0:1, 1:2], rview[:, :, 16:32], AX.XY, AL.add))
    P.op("vector", lambda e: e.tensor_reduce(
        red[0:1, 2:3], rview[:, :, 32:40], AX.XY, AL.add))
    P.drain("vector")
    P.op("vector", lambda e: e.tensor_scalar(
        red[0:1, 9:12], red[0:1, 0:3], 1.0 / (FF * D), EPS, AL.mult, AL.add))
    P.drain("vector")
    P.op("vector", lambda e: e.tensor_scalar(
        red[0:1, 3:6], red[0:1, 9:12], 0.5 * (1.0 - 6e-7), 0.0, AL.mult, AL.add))
    P.drain("vector")
    P.op("vector", lambda e: e.tensor_scalar(
        red[0:1, 6:9], red[0:1, 3:6], -1.0, 0.0, AL.mult, AL.add), inc="v")
    P.drain("vector")
    v_thr = P.now("v")
    P.wait("tensor", "qc", ld["ones1"])
    P.wait("tensor", "v", v_thr)
    P.op("tensor", lambda e: e.matmul(
        psE[:, 0:9], consts["ones1"].ap()[0:1, :], red[0:1, 3:12],
        start=True, stop=True), inc="t")
    t_bc = P.now("t")
    P.wait("scalar", "t", t_bc)
    s_thbc = P.op("scalar", lambda e: e.copy(thbc[:, 0:9], psE[:, 0:9]), inc="s")
    tap("thbc", thbc[:, 0:9], (128, 9), waits=[("s", s_thbc)])
    tap("red", red[:, 0:16], (128, 16), waits=[("s", s_thbc)])

    P.dma("gpsimd", wxT_bf.ap()[:, :, :],
          dt["wxT"].rearrange("(a p) b -> p a b", p=128), "qmw")
    P.dma("gpsimd", wzT_bf.ap()[:, :, :],
          dt["wzT"].rearrange("(a p) b -> p a b", p=128), "qmw")
    P.dma("gpsimd", wpT_bf.ap()[:, :, :],
          dt["wpT"].rearrange("(a p) b -> p a b", p=128), "qmw")
    P.dma("gpsimd", woT_bf.ap()[:, :, :],
          dt["woT"].rearrange("(a p) b -> p a b", p=128), "qmw")
    P.dma("gpsimd", wdt_bf.ap()[:, :], dt["wdtT_rep"], "qmw")
    g_wx = g_wz = g_wp = g_wo = g_wdt = P.now("qmw")

    hsTr = dt["hsT"].rearrange("(a p) b -> p a b", p=128)
    strip_v = []
    P.wait("gpsimd", "v", v_a1)
    for s in range(4):
        c0 = 512 * s
        w = 515
        v = P.dma("gpsimd", hsT_r[s % 2].ap()[:, :, 0:w], hsTr[:, :, c0:c0 + w],
                  f"qt{s % 2}")
        strip_v.append(v)
        if s == 1:
            break
    P.wait("tensor", "qmw", g_wx)
    P.wait("tensor", "qmw", g_wz)

    def mm_xz(ps_ap, w_bf, m, slot, col0, n):
        def fn(e):
            r = None
            for k in range(16):
                r = e.matmul(ps_ap, w_bf.ap()[:, k, 128 * m:128 * (m + 1)],
                             hsT_r[slot].ap()[:, k, col0:col0 + n],
                             start=(k == 0), stop=(k == 15))
            return r
        return fn

    strip_t = []
    for s in range(4):
        if s >= 2:
            P.wait("gpsimd", "t", strip_t[s - 2])
            c0 = 512 * s
            w = 515 if s < 3 else 515
            v = P.dma("gpsimd", hsT_r[s % 2].ap()[:, :, 0:w],
                      hsTr[:, :, c0:c0 + w], f"qt{s % 2}")
            strip_v.append(v)
        P.wait("tensor", f"qt{s % 2}", strip_v[s])
        for m in range(CG):
            P.op("tensor", mm_xz(psD[:, 512 * m:512 * m + 512], wxT_bf, m,
                                 s % 2, 0, 512), inc="t")
        t_x = P.now("t")
        P.wait("scalar", "t", t_x)
        for m in range(CG):
            P.op("scalar", lambda e, m=m, s=s: e.copy(
                x_fm.ap()[:, m, 512 * s:512 * s + 512],
                psD[:, 512 * m:512 * m + 512]), inc="s")
        s_xe = P.now("s")
        P.wait("tensor", "s", s_xe)
        for m in range(CG):
            P.op("tensor", mm_xz(psD[:, 512 * m:512 * m + 512], wzT_bf, m,
                                 s % 2, 3, 512), inc="t")
        t_z = P.now("t")
        P.wait("vector", "t", t_z)
        for m in range(CG):
            P.op("vector", lambda e, m=m, s=s: e.tensor_copy(
                z_bf.ap()[:, m, 512 * s:512 * s + 512],
                psD[:, 512 * m:512 * m + 512]), inc="v")
        v_z = P.now("v")
        P.wait("tensor", "v", v_z)
        strip_t.append(P.now("t"))
    for m in range(CG):
        def tailf(e, m=m):
            r = None
            for k in range(16):
                r = e.matmul(psA[:, 4 * m:4 * m + 3],
                             wxT_bf.ap()[:, k, 128 * m:128 * (m + 1)],
                             hsT_r[1].ap()[:, k, 512:515],
                             start=(k == 0), stop=(k == 15))
            return r
        P.op("tensor", tailf, inc="t")
    t_tail = P.now("t")
    P.wait("scalar", "t", t_tail)
    for m in range(CG):
        P.op("scalar", lambda e, m=m: e.copy(
            x_fm.ap()[:, m, T:T + 3], psA[:, 4 * m:4 * m + 3]), inc="s")
    s_xall = P.now("s")
    P.wait("tensor", "s", s_xall)

    P.wait("vector", "s", s_xall)
    P.wait("vector", "qc", ld["conv_w"])
    cw = consts["conv_w"].ap()
    for m in range(CG):
        P.op("vector", lambda e, m=m: e.tensor_scalar(
            acc_fm.ap()[:, m, :], x_fm.ap()[:, m, 0:T],
            cw[:, 4 * m:4 * m + 1], 0.0, AL.mult, AL.add), inc="v")
    P.drain("vector")
    for k in range(1, 4):
        for m in range(CG):
            P.op("vector", lambda e, m=m, k=k: e.scalar_tensor_tensor(
                acc_fm.ap()[:, m, :], x_fm.ap()[:, m, k:k + T],
                cw[:, 4 * m + k:4 * m + k + 1], acc_fm.ap()[:, m, :],
                AL.mult, AL.add), inc="v")
        P.drain("vector")
    v_conv = P.now("v")
    P.wait("scalar", "v", v_conv)
    P.wait("scalar", "qc", ld["conv_b"])
    for m in range(CG):
        P.op("scalar", lambda e, m=m: e.activation(
            x_act_bf.ap()[:, m, :], acc_fm.ap()[:, m, :], AF.Silu,
            bias=consts["conv_b"].ap()[:, m:m + 1], scale=1.0), inc="s")
    s_act = P.now("s")
    tap("x_act", x_act_bf.ap()[:, :, :], (128, CG, T), BF16, waits=[("s", s_act)])

    P.wait("tensor", "s", s_act)
    P.wait("tensor", "qmw", g_wp)

    def mm_par(ms, strip, ps_ap):
        def fn(e):
            r = None
            for k in range(CG):
                r = e.matmul(ps_ap, wpT_bf.ap()[:, k, 128 * ms:128 * (ms + 1)],
                             x_act_bf.ap()[:, k, 512 * strip:512 * strip + 512],
                             start=(k == 0), stop=(k == CG - 1))
            return r
        return fn

    for strip in range(4):
        P.op("tensor", mm_par(0, strip, psA), inc="t")
        for m in range(1, 5):
            P.op("tensor", mm_par(m, strip, psD[:, 512 * (m - 1):512 * m]), inc="t")
        t1 = P.now("t")
        P.wait("vector", "t", t1)
        P.op("vector", lambda e, strip=strip: e.tensor_copy(
            dt_bf.ap()[:, 512 * strip:512 * strip + 512], psA), inc="v")
        P.wait("scalar", "t", t1)
        for m in range(1, 5):
            P.op("scalar", lambda e, m=m, strip=strip: e.copy(
                b_fm.ap()[:, m - 1, 512 * strip:512 * strip + 512],
                psD[:, 512 * (m - 1):512 * m]), inc="s")
        P.wait("tensor", "v", P.now("v"))
        P.wait("tensor", "s", P.now("s"))
        for m in range(5, 9):
            P.op("tensor", mm_par(m, strip, psD[:, 512 * (m - 5):512 * (m - 4)]),
                 inc="t")
        t2 = P.now("t")
        P.wait("scalar", "t", t2)
        for m in range(5, 9):
            P.op("scalar", lambda e, m=m, strip=strip: e.copy(
                c_fm.ap()[:, m - 5, 512 * strip:512 * strip + 512],
                psD[:, 512 * (m - 5):512 * (m - 4)]), inc="s")
        P.wait("tensor", "s", P.now("s"))
    v_par = P.now("v")

    P.wait("tensor", "v", v_par)
    P.wait("tensor", "qmw", g_wdt)
    P.wait("scalar", "qc", ld["bdt"])
    for strip in range(4):
        for m in range(CG):
            P.op("tensor", lambda e, m=m, strip=strip: e.matmul(
                psE, wdt_bf.ap()[:, 128 * m:128 * (m + 1)],
                dt_bf.ap()[:, 512 * strip:512 * strip + 512],
                start=True, stop=True), inc="t")
            tm = P.now("t")
            P.wait("scalar", "t", tm)
            P.op("scalar", lambda e, m=m, strip=strip: e.activation(
                delta_fm.ap()[:, m, 512 * strip:512 * strip + 512], psE,
                AF.Exp, bias=consts["bdt"].ap()[:, m:m + 1], scale=1.0), inc="s")
            P.wait("tensor", "s", P.now("s"))
    P.drain("scalar")
    for m in range(CG):
        P.op("scalar", lambda e, m=m: e.activation(
            delta_fm.ap()[:, m, :], delta_fm.ap()[:, m, :], AF.Ln, bias=1.0),
            inc="s")
    P.drain("scalar")
    s_delta = P.now("s")
    tap("delta", delta_fm.ap()[:, :, :], (128, CG, T), waits=[("s", s_delta)])

    P.wait("vector", "s", s_delta)
    for m in range(CG):
        P.op("vector", lambda e, m=m: e.tensor_tensor(
            b_fm.ap()[:, m, :], b_fm.ap()[:, m, :], delta_fm.ap()[:, m, :],
            AL.mult), inc="v")
    P.drain("vector")
    for m in range(CG):
        P.op("vector", lambda e, m=m: e.tensor_tensor(
            b_fm.ap()[:, m, :], b_fm.ap()[:, m, :], x_act_bf.ap()[:, m, :],
            AL.mult), inc="v")
    P.drain("vector")
    for m in range(CG):
        P.op("vector", lambda e, m=m: e.tensor_scalar(
            delta_fm.ap()[:, m, :], delta_fm.ap()[:, m, :],
            consts["a_flat"].ap()[:, m:m + 1], 0.0, AL.mult, AL.add), inc="v")
    P.drain("vector")
    for m in range(CG):
        P.op("vector", lambda e, m=m: e.tensor_reduce(
            consts["ptot"].ap()[:, m:m + 1], delta_fm.ap()[:, m, :], AX.X,
            AL.add), inc="v")
    P.drain("vector")
    v_su = P.now("v")
    if "slog" in TAPS:
        dslog = nc.dram_tensor("tap_slog", [128, CG, T], F32,
                               kind="ExternalOutput").ap()
        P.wait("gpsimd", "v", v_su)
        qv = P.dma("gpsimd", dslog, delta_fm.ap()[:, :, :], "qs")
        P.wait("scalar", "qs", qv)
    P.wait("scalar", "v", v_su)
    P.op("scalar", lambda e: e.activation(
        consts["ptot"].ap()[:, :], consts["ptot"].ap()[:, :], AF.Exp), inc="s")
    for m in range(CG):
        P.op("scalar", lambda e, m=m: e.activation(
            delta_fm.ap()[:, m, :], delta_fm.ap()[:, m, :], AF.Exp), inc="s")
    P.drain("scalar")
    s_abar = P.now("s")

    tap("bu", b_fm.ap()[:, :, :], (128, CG, T), waits=[("v", v_su)])
    tap("aflat", consts["a_flat"].ap()[:, :], (128, CG), waits=[("v", v_su)])
    tap("abar", delta_fm.ap()[:, :, :], (128, CG, T), waits=[("s", s_abar)])
    P.wait("vector", "s", s_abar)
    P.wait("vector", "qc", ld["zcol"])
    for m in range(CG):
        P.op("vector", lambda e, m=m: e.tensor_tensor_scan(
            h_fm.ap()[:, m, :], delta_fm.ap()[:, m, :], b_fm.ap()[:, m, :],
            consts["zcol"].ap()[:, 0:1], AL.mult, AL.add), inc="v")
    P.drain("vector")
    v_scan1 = P.now("v")

    tap("h1", h_fm.ap()[:, :, :], (128, CG, T), waits=[("v", v_scan1)])
    P.wait("gpsimd", "v", v_scan1)
    P.dma("gpsimd", dt["ags_in"][0:512].rearrange("(g p) -> p g", p=128),
          consts["ptot"].ap()[:, 0:CG], "qx", slow=True)
    v = P.dma("gpsimd", dt["ags_in"][512:1024].rearrange("(g p) -> p g", p=128),
              h_fm.ap()[:, :, T - 1:T].rearrange("p g a -> p (g a)"), "qx",
              slow=True)
    P.wait("gpsimd", "qx", v)
    P.op("gpsimd", lambda e: e.collective_compute(
        "AllGather", AL.bypass, replica_groups=[list(range(NCORES))],
        ins=[dt["ags_in"].opt()], outs=[dt["ags_out"].opt()]), inc="cc")
    P.wait("gpsimd", "cc", P.now("cc"))
    v_ag = P.dma("gpsimd", consts["ags_sb"].ap()[0:8, :], dt["ags_out"][:, :],
                 "qx")
    P.wait("tensor", "qx", v_ag)
    P.wait("tensor", "qc", ld["ident32"])
    for kk in range(8):
        P.op("tensor", lambda e, kk=kk: e.transpose(
            psD[:, 8 * kk:8 * kk + 8],
            consts["ags_sb"].ap()[0:8, 128 * kk:128 * (kk + 1)],
            ID32[0:8, 0:8]), inc="t")
    t_ag = P.now("t")
    P.wait("vector", "t", t_ag)
    al = consts["allst"].ap()
    P.op("vector", lambda e: e.tensor_copy(
        al[:, :, :, :], psD[:, 0:64].rearrange("p (k g j) -> p j k g", k=2, g=4)),
        inc="v")
    P.drain("vector")
    hp = consts["hprev"].ap()
    P.op("vector", lambda e: e.memset(hp[:, 1, :], 0.0), inc="v")
    P.drain("vector")
    P.wait("vector", "qc", ld["pmask1m"])
    for j in range(NCORES):
        P.op("vector", lambda e, j=j: e.tensor_scalar(
            al[:, j, 0, :], al[:, j, 0, :], consts["pmask"].ap()[:, j:j + 1],
            consts["pmask1m"].ap()[:, j:j + 1], AL.mult, AL.add), inc="v")
        P.drain("vector")
        P.op("vector", lambda e, j=j: e.tensor_tensor(
            hp[:, 1, :], hp[:, 1, :], al[:, j, 0, :], AL.mult), inc="v")
        P.drain("vector")
        P.op("vector", lambda e, j=j: e.scalar_tensor_tensor(
            hp[:, 1, :], al[:, j, 1, :], consts["pmask"].ap()[:, j:j + 1],
            hp[:, 1, :], AL.mult, AL.add), inc="v")
        P.drain("vector")
    v_fold = P.now("v")
    P.wait("scalar", "v", v_fold)
    s_hp = P.op("scalar", lambda e: e.copy(hp[:, 0, :], hp[:, 1, :]), inc="s")

    tap("hprev", hp[:, :, :], (128, 2, CG), waits=[("s", s_hp)])
    tap("allst", al[:, :, :, :], (128, NCORES, 2, CG), waits=[("v", v_fold)])
    tap("ptot", consts["ptot"].ap()[:, :], (128, CG), waits=[("s", s_abar)])
    P.wait("vector", "s", s_hp)
    for m in range(CG):
        P.op("vector", lambda e, m=m: e.tensor_tensor_scan(
            h_fm.ap()[:, m, :], delta_fm.ap()[:, m, :], b_fm.ap()[:, m, :],
            hp[:, 0, m:m + 1], AL.mult, AL.add), inc="v")
    P.drain("vector")
    v_scan2 = P.now("v")
    tap("h", h_fm.ap()[:, :, :], (128, CG, T), waits=[("v", v_scan2)])

    P.wait("scalar", "v", v_scan2)
    for m in range(CG):
        P.op("scalar", lambda e, m=m: e.activation(
            b_fm.ap()[:, m, :], z_bf.ap()[:, m, :], AF.Silu), inc="s")
    s_sz = P.now("s")
    for m in range(CG):
        P.op("vector", lambda e, m=m: e.tensor_tensor(
            c_fm.ap()[:, m, :], c_fm.ap()[:, m, :], h_fm.ap()[:, m, :], AL.mult),
            inc="v")
    P.drain("vector")
    P.wait("vector", "qc", ld["d_vec"])
    for m in range(CG):
        P.op("vector", lambda e, m=m: e.scalar_tensor_tensor(
            c_fm.ap()[:, m, :], x_act_bf.ap()[:, m, :],
            consts["d_vec"].ap()[:, m:m + 1], c_fm.ap()[:, m, :],
            AL.mult, AL.add), inc="v")
    P.drain("vector")
    P.wait("vector", "s", s_sz)
    for m in range(CG):
        P.op("vector", lambda e, m=m: e.tensor_tensor(
            y_bf.ap()[:, m, :], c_fm.ap()[:, m, :], b_fm.ap()[:, m, :], AL.mult),
            inc="v")
    P.drain("vector")
    v_y = P.now("v")
    tap("y", y_bf.ap()[:, :, :], (128, CG, T), BF16, waits=[("v", v_y)])

    if STOP_AFTER == "mamba":
        P.wait("gpsimd", "v", v_y)
        P.wait("gpsimd", "qs", P.now("qs"))
        return _finish(nc, P, dt)
    P.wait("sync", "v", v_y)
    P.wait("vector", "s", s_thbc)
    a2_store = []
    vq_done = []
    for j, (nm, i, w) in enumerate(a_slices):
        col = {"wgT_sl": 0, "wuT_sl": 1, "wdT_sl": 2}[nm]
        agname = {"wgT_sl": "agw_in_g", "wuT_sl": "agw_in_u",
                  "wdT_sl": "agw_in_d"}[nm]
        if j >= 2:
            P.wait("sync", "v", vq_done[j - 2])
        v = P.dma("sync", a2_ring[j % 2].ap()[:, 0:w],
                  dt[nm][128 * i:128 * (i + 1), :], f"qa{j % 2}")
        P.wait("vector", f"qa{j % 2}", v)
        if j >= 2:
            P.wait("vector", a2_store[j - 2][0], a2_store[j - 2][1])
        P.op("vector", lambda e, j=j, w=w, col=col: e.tensor_scalar(
            a2_sc.ap()[:, 0:w], a2_ring[j % 2].ap()[:, 0:w],
            thbc[:, col:col + 1], 0.0, AL.is_gt, AL.add), inc="v")
        P.op("vector", lambda e, j=j, w=w, col=col: e.tensor_scalar(
            a2_ring[j % 2].ap()[:, 0:w], a2_ring[j % 2].ap()[:, 0:w],
            thbc[:, 3 + col:4 + col], 0.0, AL.is_lt, AL.add), inc="v")
        P.drain("vector")
        P.op("vector", lambda e, j=j, w=w: e.tensor_tensor(
            a2_out[j % 2].ap()[:, 0:w], a2_sc.ap()[:, 0:w],
            a2_ring[j % 2].ap()[:, 0:w], AL.subtract), inc="v")
        P.drain("vector")
        vq_done.append(P.now("v"))
        P.wait("gpsimd", "v", vq_done[j])
        qo = f"qo{j % 2}"
        if nm == "wdT_sl":
            v = P.dma("gpsimd", dt[agname][128 * i:128 * (i + 1), :],
                      a2_out[j % 2].ap()[:, 0:w], qo)
        else:
            for jc in range(4):
                v = P.dma("gpsimd", dt[agname][jc, :, i, :],
                          a2_out[j % 2].ap()[:, 256 * jc:256 * jc + 256], qo)
        a2_store.append((qo, v))

    P.wait("gpsimd", a2_store[14][0], a2_store[14][1])
    P.wait("gpsimd", a2_store[15][0], a2_store[15][1])
    P.op("gpsimd", lambda e: e.collective_compute(
        "AllGather", AL.bypass, replica_groups=[list(range(NCORES))],
        ins=[dt["agw_in_g"].opt()], outs=[dt["agw_out_g"].opt()]), inc="cc")
    cc_g = P.now("cc")
    P.wait("gpsimd", a2_store[30][0], a2_store[30][1])
    P.wait("gpsimd", a2_store[31][0], a2_store[31][1])
    P.op("gpsimd", lambda e: e.collective_compute(
        "AllGather", AL.bypass, replica_groups=[list(range(NCORES))],
        ins=[dt["agw_in_u"].opt()], outs=[dt["agw_out_u"].opt()]), inc="cc")
    cc_u = P.now("cc")
    P.wait("gpsimd", a2_store[38][0], a2_store[38][1])
    P.wait("gpsimd", a2_store[39][0], a2_store[39][1])
    P.op("gpsimd", lambda e: e.collective_compute(
        "AllGather", AL.bypass, replica_groups=[list(range(NCORES))],
        ins=[dt["agw_in_d"].opt()], outs=[dt["agw_out_d"].opt()]), inc="cc")
    cc_d = P.now("cc")
    tap("wgq", dt["agw_out_g"][:, :, :, :, :], (NCORES, 4, 128, 16, 256), F8, waits=[("cc", cc_d)])
    tap("wdq", dt["agw_out_d"][:, :], (FF, D), F8, waits=[("cc", cc_d)])

    if STOP_AFTER == "a2":
        P.wait("gpsimd", "cc", cc_d)
        P.wait("gpsimd", "qs", P.now("qs"))
        return _finish(nc, P, dt)
    wgTr = dt["agw_out_g"]
    wuTr = dt["agw_out_u"]
    RC = 1.0 / D

    v_a2 = P.now("v")

    gu_consumed = {}
    wd_consumed = {}
    prev_gu = []
    prev_wd = []

    def load_gu(key, src, c):
        slot = len(prev_gu) % 3
        if len(prev_gu) == 0:
            P.wait("sync", "cc", cc_g)
        elif len(prev_gu) == 1:
            P.wait("sync", "cc", cc_u)
        if len(prev_gu) >= 3:
            P.wait("sync", "t", gu_consumed[prev_gu[-3]])
        b0 = 2 * c
        P.dma("sync", gu_ring[slot].ap()[:, :, 0:256],
              src[b0 // 4, b0 % 4, :, :, :], f"qu{slot}")
        v = P.dma("sync", gu_ring[slot].ap()[:, :, 256:512],
                  src[(b0 + 1) // 4, (b0 + 1) % 4, :, :, :], f"qu{slot}")
        prev_gu.append(key)
        return slot, v

    def load_wd(key, k):
        slot = len(prev_wd) % 4
        eng = "gpsimd" if slot % 2 == 0 else "sync"
        if len(prev_wd) >= 4:
            P.wait(eng, "t", wd_consumed[prev_wd[-4]])
        v = P.dma(eng, wd_ring[slot].ap()[:, :],
                  dt["agw_out_d"][128 * k:128 * (k + 1), :], f"qd{slot}")
        prev_wd.append(key)
        return slot, v

    hs_cons = [v_a2]

    def load_hs(tg):
        P.wait("gpsimd", "v", hs_cons[0])
        return P.dma("gpsimd", hs_sb.ap()[:, :],
                     dt["hs"][128 * tg:128 * (tg + 1), :], "qh")

    def vst(fn):
        P.op("vector", fn, inc="v")
        P.drain("vector")

    P.wait("tensor", "qmw", g_wo)
    P.wait("tensor", "v", v_y)
    P.wait("sync", "cc", cc_d)
    P.wait("gpsimd", "cc", cc_d)
    prev_qs = [P.now("qs")]

    STAGE = 3 * D

    def emit_wout(gx, tt):
        tg = gx * NTT + tt

        def wof(e, tg=tg):
            r = None
            for n in range(4):
                for k in range(CG):
                    r = e.matmul(psD[:, 512 * n:512 * n + 512],
                                 y_bf.ap()[:, k, 128 * tg:128 * (tg + 1)],
                                 woT_bf.ap()[:, k, 512 * n:512 * n + 512],
                                 start=(k == 0), stop=(k == CG - 1))
            return r
        P.op("tensor", wof, inc="t")
        return P.now("t")

    def emit_r1evac(gx, tt, dst, v_hs, t_wo):
        tq = tt + NTT * (gx % 2)
        P.wait("vector", "qh", v_hs)
        P.wait("vector", "t", t_wo)
        vst(lambda e, tq=tq, dst=dst: e.scalar_tensor_tensor(
            dst, psD[:, 0:2048], 1.0, hs_sb.ap()[:, :],
            AL.mult, AL.add, accum_out=ST[:, tq, S_SUM:S_SUM + 1]))
        hs_cons[0] = P.now("v")
        return P.now("v")

    def emit_ln1(gx, src, t_gate=None, v_gate=None):
        if t_gate is not None:
            P.wait("scalar", "t", t_gate)
        if v_gate is not None:
            P.wait("scalar", "v", v_gate)
        for tt in range(NTT):
            tq = tt + NTT * (gx % 2)
            for hh in range(2):
                P.op("scalar", lambda e, tq=tq, tt=tt, hh=hh: e.activation(
                    sq_tr.ap()[:, :], src(tt, slice(1024 * hh, 1024 * (hh + 1))),
                    AF.Square, accum_out=ST[:, tq, S_Q1 + hh:S_Q1 + hh + 1]),
                    inc="s")
                P.drain("scalar")
        s_sq = P.now("s")
        for tt in range(NTT):
            tq = tt + NTT * (gx % 2)
            vst(lambda e, tq=tq, tt=tt: e.tensor_reduce(
                ST[:, tq, S_MAX:S_MAX + 1], src(tt, slice(0, 2048)), AX.X, AL.max))
            vst(lambda e, tq=tq, tt=tt: e.tensor_reduce(
                ST[:, tq, S_MIN:S_MIN + 1], src(tt, slice(0, 2048)), AX.X, AL.min))
        P.wait("vector", "s", s_sq)
        for tt in range(NTT):
            tq = tt + NTT * (gx % 2)
            vst(lambda e, tq=tq: e.tensor_tensor(
                ST[:, tq, S_SSQ:S_SSQ + 1], ST[:, tq, S_Q1:S_Q1 + 1],
                ST[:, tq, S_Q2:S_Q2 + 1], AL.add))
            vst(lambda e, tq=tq: e.tensor_scalar(
                ST[:, tq, S_MU:S_MU + 1], ST[:, tq, S_SUM:S_SUM + 1], RC, 0.0,
                AL.mult, AL.add))
            vst(lambda e, tq=tq: e.tensor_tensor(
                ST[:, tq, S_T1:S_T1 + 1], ST[:, tq, S_MU:S_MU + 1],
                ST[:, tq, S_MU:S_MU + 1], AL.mult))
            vst(lambda e, tq=tq: e.scalar_tensor_tensor(
                ST[:, tq, S_VAR:S_VAR + 1], ST[:, tq, S_SSQ:S_SSQ + 1], RC,
                ST[:, tq, S_T1:S_T1 + 1], AL.mult, AL.subtract))
        v_var = P.now("v")
        P.wait("scalar", "v", v_var)
        for tt in range(NTT):
            tq = tt + NTT * (gx % 2)
            P.op("scalar", lambda e, tq=tq: e.activation(
                ST[:, tq, S_DEN:S_DEN + 1], ST[:, tq, S_VAR:S_VAR + 1], AF.Sqrt,
                bias=EPS, scale=1.0), inc="s")
        s_den = P.now("s")
        P.wait("vector", "s", s_den)
        for tt in range(NTT):
            tq = tt + NTT * (gx % 2)
            vst(lambda e, tq=tq: e.reciprocal(
                ST[:, tq, S_T1:S_T1 + 1], ST[:, tq, S_DEN:S_DEN + 1]))
            vst(lambda e, tq=tq: e.tensor_tensor(
                ST[:, tq, S_T2:S_T2 + 1], ST[:, tq, S_DEN:S_DEN + 1],
                ST[:, tq, S_T1:S_T1 + 1], AL.mult))
            vst(lambda e, tq=tq: e.tensor_scalar(
                ST[:, tq, S_T2:S_T2 + 1], ST[:, tq, S_T2:S_T2 + 1], -1.0, 2.0,
                AL.mult, AL.add))
            vst(lambda e, tq=tq: e.tensor_tensor(
                ST[:, tq, S_RHO:S_RHO + 1], ST[:, tq, S_T1:S_T1 + 1],
                ST[:, tq, S_T2:S_T2 + 1], AL.mult))
            vst(lambda e, tq=tq: e.tensor_tensor(
                ST[:, tq, S_T1:S_T1 + 1], ST[:, tq, S_MAX:S_MAX + 1],
                ST[:, tq, S_MU:S_MU + 1], AL.subtract))
            vst(lambda e, tq=tq: e.tensor_tensor(
                ST[:, tq, S_T2:S_T2 + 1], ST[:, tq, S_MU:S_MU + 1],
                ST[:, tq, S_MIN:S_MIN + 1], AL.subtract))
            vst(lambda e, tq=tq: e.tensor_tensor(
                ST[:, tq, S_MX:S_MX + 1], ST[:, tq, S_T1:S_T1 + 1],
                ST[:, tq, S_T2:S_T2 + 1], AL.max))
            vst(lambda e, tq=tq: e.tensor_tensor(
                ST[:, tq, S_T1:S_T1 + 1], ST[:, tq, S_MX:S_MX + 1],
                ST[:, tq, S_RHO:S_RHO + 1], AL.mult))
            vst(lambda e, tq=tq: e.tensor_scalar(
                ST[:, tq, S_SX:S_SX + 1], ST[:, tq, S_T1:S_T1 + 1], 1.0 / 127.0,
                EPS, AL.mult, AL.add))
            vst(lambda e, tq=tq: e.reciprocal(
                ST[:, tq, S_T1:S_T1 + 1], ST[:, tq, S_SX:S_SX + 1]))
            vst(lambda e, tq=tq: e.tensor_tensor(
                ST[:, tq, S_T2:S_T2 + 1], ST[:, tq, S_SX:S_SX + 1],
                ST[:, tq, S_T1:S_T1 + 1], AL.mult))
            vst(lambda e, tq=tq: e.tensor_scalar(
                ST[:, tq, S_T2:S_T2 + 1], ST[:, tq, S_T2:S_T2 + 1], -1.0, 2.0,
                AL.mult, AL.add))
            vst(lambda e, tq=tq: e.tensor_tensor(
                ST[:, tq, S_RQ:S_RQ + 1], ST[:, tq, S_T1:S_T1 + 1],
                ST[:, tq, S_T2:S_T2 + 1], AL.mult))
            vst(lambda e, tq=tq: e.tensor_tensor(
                ST[:, tq, S_AA:S_AA + 1], ST[:, tq, S_RHO:S_RHO + 1],
                ST[:, tq, S_RQ:S_RQ + 1], AL.mult))
            vst(lambda e, tq=tq: e.tensor_tensor(
                ST[:, tq, S_BB:S_BB + 1], ST[:, tq, S_MU:S_MU + 1],
                ST[:, tq, S_AA:S_AA + 1], AL.mult))
            vst(lambda e, tq=tq: e.tensor_tensor(
                ST[:, tq, S_SG:S_SG + 1], ST[:, tq, S_SX:S_SX + 1],
                thbc[:, 6:7], AL.mult))
            vst(lambda e, tq=tq: e.tensor_tensor(
                ST[:, tq, S_SU:S_SU + 1], ST[:, tq, S_SX:S_SX + 1],
                thbc[:, 7:8], AL.mult))
            vst(lambda e, tq=tq: e.tensor_tensor(
                ST[:, tq, S_MURHO:S_MURHO + 1], ST[:, tq, S_MU:S_MU + 1],
                ST[:, tq, S_RHO:S_RHO + 1], AL.mult))
        v_stats = P.now("v")
        P.wait("scalar", "v", v_stats)
        P.op("scalar", lambda e: e.copy(
            ST[:, :, S_SA:S_SA + 2], ST[:, :, S_AA:S_AA + 2]), inc="s")
        P.wait("vector", "s", P.now("s"))

    def emit_xq(gx, src, t_gate=None):
        if t_gate is not None:
            P.wait("vector", "t", t_gate)
        for tt in range(NTT):
            tq = tt + NTT * (gx % 2)
            for hh in range(2):
                cl = slice(1024 * hh, 1024 * (hh + 1))
                vst(lambda e, tq=tq, tt=tt, hh=hh, cl=cl: e.tensor_scalar(
                    c_sc.ap()[:, :], src(tt, cl),
                    ST[:, tq, S_SA:S_SA + 1], ST[:, tq, S_SB2:S_SB2 + 1],
                    AL.mult, AL.subtract))
                vst(lambda e: e.tensor_copy(ui_sc.ap()[:, :], c_sc.ap()[:, :]))
                vst(lambda e, tt=tt, cl=cl: e.tensor_copy(
                    xq_tm.ap()[:, tt, cl], ui_sc.ap()[:, :]))
        return P.now("v")

    def emit_xqT(v_xq, v_bank=None):
        P.wait("tensor", "v", v_xq)
        if v_bank is not None:
            P.wait("tensor", "v", v_bank)
        for tt in range(NTT):
            for kb in range(2):
                for j in range(8):
                    k = kb * 8 + j
                    P.op("tensor", lambda e, tt=tt, k=k, j=j: e.transpose(
                        psT[:, 128 * j:128 * (j + 1)],
                        xq_tm.ap()[:, tt, 128 * k:128 * (k + 1)], ID), inc="t")
                t_tr = P.now("t")
                P.wait("scalar", "t", t_tr)
                P.op("scalar", lambda e, tt=tt, kb=kb: e.copy(
                    xqT.ap()[:, 8 * kb:8 * kb + 8, 128 * tt:128 * (tt + 1)],
                    psT[:, 0:1024].rearrange("p (j b) -> p j b", j=8)), inc="s")
                P.wait("tensor", "s", P.now("s"))
        return P.now("s")

    v_hs = load_hs(0)
    t_wo = emit_wout(0, 0)
    v_r1 = emit_r1evac(0, 0, r1_t.ap()[:, 0, :], v_hs, t_wo)
    v_hs = load_hs(1)
    P.wait("tensor", "v", v_r1)
    t_wo = emit_wout(0, 1)
    v_r1 = emit_r1evac(0, 1, r1_t.ap()[:, 1, :], v_hs, t_wo)

    def src_r1t(tt, cl):
        return r1_t.ap()[:, tt, cl]

    def src_stage(tt, cl):
        return m_ar.ap()[:, tt, STAGE + cl.start:STAGE + cl.stop]

    emit_ln1(0, src_r1t, v_gate=v_r1)
    v_xq = emit_xq(0, src_r1t)
    s_xqT = {0: emit_xqT(v_xq)}
    ln2v = {-1: (0, 0)}

    for g in range(NG):
        gp = g % 2
        last = (g == NG - 1)

        P.wait("tensor", "s", s_xqT[g])
        if g >= 1:
            P.wait("tensor", "v", ln2v[g - 1][0])
        if not last:
            v_hs = load_hs((g + 1) * NTT + 0)
        sil_prev = [0]
        mstt_prev = [0]
        for c in range(16):
            slot_g, vg = load_gu((g, 2 * c), wgTr, c)
            slot_u, vu = load_gu((g, 2 * c + 1), wuTr, c)
            if c == 0:
                P.wait("vector", "qs", prev_qs[0])
            P.wait("tensor", f"qu{slot_g}", vg)
            P.wait("tensor", "s", sil_prev[0])
            for tt in range(NTT):
                ps = psA if tt == 0 else psB
                def gmf(e, tt=tt, ps=ps, slot=slot_g):
                    r = None
                    for k in range(16):
                        r = e.matmul(ps[:, 0:512],
                                     xqT.ap()[:, k, 128 * tt:128 * (tt + 1)],
                                     gu_ring[slot].ap()[:, k, :],
                                     start=(k == 0), stop=(k == 15))
                    return r
                P.op("tensor", gmf, inc="t")
            gu_consumed[(g, 2 * c)] = P.now("t")
            t_gate = P.now("t")
            P.wait("scalar", "t", t_gate)
            for tt in range(NTT):
                ps = psA if tt == 0 else psB
                tq = tt + NTT * gp
                sgi = 2 * (c % 2) + tt
                P.op("scalar", lambda e, tq=tq, ps=ps, sgi=sgi: e.activation(
                    sg_sl[sgi].ap()[:, 0:512], ps[:, 0:512],
                    AF.Silu, scale=ST[:, tq, S_SG:S_SG + 1]), inc="s")
            sil_prev[0] = P.now("s")
            P.wait("tensor", f"qu{slot_u}", vu)
            P.wait("tensor", "v", mstt_prev[0])
            for tt in range(NTT):
                ps = psE if tt == 0 else psU
                def umf(e, tt=tt, ps=ps, slot=slot_u):
                    r = None
                    for k in range(16):
                        r = e.matmul(ps[:, 0:512],
                                     xqT.ap()[:, k, 128 * tt:128 * (tt + 1)],
                                     gu_ring[slot].ap()[:, k, :],
                                     start=(k == 0), stop=(k == 15))
                    return r
                P.op("tensor", umf, inc="t")
            gu_consumed[(g, 2 * c + 1)] = P.now("t")
            t_up = P.now("t")
            P.wait("vector", "t", t_up)
            P.wait("vector", "s", sil_prev[0])
            for tt in range(NTT):
                ps = psE if tt == 0 else psU
                tq = tt + NTT * gp
                sgi = 2 * (c % 2) + tt
                P.op("vector", lambda e, tq=tq, tt=tt, ps=ps, c=c, sgi=sgi:
                     e.scalar_tensor_tensor(
                         m_ar.ap()[:, tt, 512 * c:512 * c + 512], ps[:, 0:512],
                         ST[:, tq, S_SU:S_SU + 1], sg_sl[sgi].ap()[:, 0:512],
                         AL.mult, AL.mult), inc="v")
            mstt_prev[0] = P.now("v")
        P.drain("vector")

        if not last:
            P.wait("tensor", "v", ln2v[g - 1][1])
            t_wo0 = emit_wout(g + 1, 0)

        for tt in range(NTT):
            tq = tt + NTT * gp
            vst(lambda e, tq=tq, tt=tt: e.tensor_reduce(
                ST[:, tq, S_MM:S_MM + 1], m_ar.ap()[:, tt, :], AX.X, AL.max,
                apply_absolute_value=True))
            vst(lambda e, tq=tq: e.tensor_scalar(
                ST[:, tq, S_SM:S_SM + 1], ST[:, tq, S_MM:S_MM + 1], 1.0 / 127.0,
                EPS, AL.mult, AL.add))
            vst(lambda e, tq=tq: e.reciprocal(
                ST[:, tq, S_T1:S_T1 + 1], ST[:, tq, S_SM:S_SM + 1]))
            vst(lambda e, tq=tq: e.tensor_tensor(
                ST[:, tq, S_T2:S_T2 + 1], ST[:, tq, S_SM:S_SM + 1],
                ST[:, tq, S_T1:S_T1 + 1], AL.mult))
            vst(lambda e, tq=tq: e.tensor_scalar(
                ST[:, tq, S_T2:S_T2 + 1], ST[:, tq, S_T2:S_T2 + 1], -1.0, 2.0,
                AL.mult, AL.add))
            vst(lambda e, tq=tq: e.tensor_tensor(
                ST[:, tq, S_RQM:S_RQM + 1], ST[:, tq, S_T1:S_T1 + 1],
                ST[:, tq, S_T2:S_T2 + 1], AL.mult))
            vst(lambda e, tq=tq: e.tensor_tensor(
                ST[:, tq, S_SMW:S_SMW + 1], ST[:, tq, S_SM:S_SM + 1],
                thbc[:, 8:9], AL.mult))
        v_msc = P.now("v")
        P.wait("scalar", "v", v_msc)
        P.op("scalar", lambda e: e.copy(
            ST[:, :, S_SA:S_SA + 1], ST[:, :, S_RQM:S_RQM + 1]), inc="s")
        P.wait("vector", "s", P.now("s"))

        tr_prev = [0]

        def qchunk(tt, cc):
            tq = tt + NTT * gp
            cl = slice(1024 * cc, 1024 * (cc + 1))
            vst(lambda e, tq=tq, tt=tt, cl=cl: e.tensor_scalar(
                m_ar.ap()[:, tt, cl], m_ar.ap()[:, tt, cl],
                ST[:, tq, S_SA:S_SA + 1], 0.0, AL.mult, AL.add))
            vst(lambda e, tt=tt, cl=cl: e.tensor_copy(
                ui_sc.ap()[:, :], m_ar.ap()[:, tt, cl]))
            P.wait("vector", "t", tr_prev[0])
            vst(lambda e: e.tensor_copy(mqs.ap()[:, :], ui_sc.ap()[:, :]))
            v_mq = P.now("v")
            P.wait("tensor", "v", v_mq)
            for j in range(8):
                P.op("tensor", lambda e, j=j: e.transpose(
                    psT[:, 128 * j:128 * (j + 1)],
                    mqs.ap()[:, 128 * j:128 * (j + 1)], ID), inc="t")
            tr_prev[0] = P.now("t")
            P.wait("scalar", "t", tr_prev[0])
            P.op("scalar", lambda e, tt=tt, cc=cc: e.copy(
                mqT.ap()[:, tt, 8 * cc:8 * cc + 8, :],
                psT[:, 0:1024].rearrange("p (j b) -> p j b", j=8)), inc="s")
            P.wait("tensor", "s", P.now("s"))

        for cc in range(8):
            qchunk(0, cc)
        if not last:
            v_r1e = emit_r1evac(g + 1, 0, m_ar.ap()[:, 0, STAGE:STAGE + 2048],
                                v_hs, t_wo0)
            v_hs = load_hs((g + 1) * NTT + 1)
            P.wait("tensor", "v", v_r1e)
            t_wo1 = emit_wout(g + 1, 1)
        for cc in range(8):
            qchunk(1, cc)
        if not last:
            v_r1e = emit_r1evac(g + 1, 1, m_ar.ap()[:, 1, STAGE:STAGE + 2048],
                                v_hs, t_wo1)

        if not last:
            P.wait("tensor", "v", v_r1e)
        t_k16 = t_k48 = None
        for k in range(64):
            slot, vw = load_wd((g, k), k)
            P.wait("tensor", f"qd{slot}", vw)
            for tt in range(NTT):
                ps = psD if tt == 0 else psD2
                for n in range(4):
                    P.op("tensor", lambda e, tt=tt, k=k, n=n, ps=ps, slot=slot:
                         e.matmul(ps[:, 512 * n:512 * n + 512],
                                  mqT.ap()[:, tt, k, :],
                                  wd_ring[slot].ap()[:, 512 * n:512 * n + 512],
                                  start=(k == 0), stop=(k == 63)), inc="t")
            wd_consumed[(g, k)] = P.now("t")
            if k == 15:
                t_k16 = P.now("t")
            elif k == 47:
                t_k48 = P.now("t")
        t_down = P.now("t")

        if not last:
            emit_ln1(g + 1, src_stage, t_gate=t_k16, v_gate=v_r1e)
            v_xq = emit_xq(g + 1, src_stage, t_gate=t_k48)

        P.wait("vector", "t", t_down)
        marks = {}
        for tt in (1, 0):
            tq = tt + NTT * gp
            ps = psD if tt == 0 else psD2
            for hh in range(2):
                cl = slice(1024 * hh, 1024 * (hh + 1))
                vst(lambda e, tq=tq, tt=tt, ps=ps, cl=cl: e.tensor_scalar(
                    c_sc.ap()[:, :], ps[:, cl],
                    ST[:, tq, S_SMW:S_SMW + 1], ST[:, tq, S_MURHO:S_MURHO + 1],
                    AL.mult, AL.subtract))
                vst(lambda e, tq=tq, tt=tt, hh=hh, cl=cl:
                    e.scalar_tensor_tensor(
                        m_ar.ap()[:, tt, cl], r1_t.ap()[:, tt, cl],
                        ST[:, tq, S_RHO:S_RHO + 1], c_sc.ap()[:, :],
                        AL.mult, AL.add,
                        accum_out=ST[:, tq, S_T1 + hh:S_T1 + hh + 1]))
            vst(lambda e, tq=tq: e.tensor_tensor(
                ST[:, tq, S_SUM2:S_SUM2 + 1], ST[:, tq, S_T1:S_T1 + 1],
                ST[:, tq, S_T2:S_T2 + 1], AL.add))
            marks[tt] = P.now("v")
            if tt == 1 and not last:
                s_xqT[g + 1] = emit_xqT(v_xq, v_bank=marks[1])
        ln2v[g] = (marks[1], marks[0])

        if not last:
            for tt in range(NTT):
                vst(lambda e, tt=tt: e.tensor_copy(
                    r1_t.ap()[:, tt, :], m_ar.ap()[:, tt, STAGE:STAGE + 2048]))

        P.wait("scalar", "v", marks[0])
        for tt in range(NTT):
            tq = tt + NTT * gp
            for hh in range(2):
                P.op("scalar", lambda e, tq=tq, tt=tt, hh=hh: e.activation(
                    sq_tr.ap()[:, :],
                    m_ar.ap()[:, tt, 1024 * hh:1024 * (hh + 1)],
                    AF.Square, accum_out=ST[:, tq, S_Q1 + hh:S_Q1 + hh + 1]),
                    inc="s")
                P.drain("scalar")
        s_sq2 = P.now("s")
        P.wait("vector", "s", s_sq2)
        for tt in range(NTT):
            tq = tt + NTT * gp
            vst(lambda e, tq=tq: e.tensor_tensor(
                ST[:, tq, S_SSQ:S_SSQ + 1], ST[:, tq, S_Q1:S_Q1 + 1],
                ST[:, tq, S_Q2:S_Q2 + 1], AL.add))
            vst(lambda e, tq=tq: e.tensor_scalar(
                ST[:, tq, S_MU:S_MU + 1], ST[:, tq, S_SUM2:S_SUM2 + 1], RC, 0.0,
                AL.mult, AL.add))
            vst(lambda e, tq=tq: e.tensor_tensor(
                ST[:, tq, S_T1:S_T1 + 1], ST[:, tq, S_MU:S_MU + 1],
                ST[:, tq, S_MU:S_MU + 1], AL.mult))
            vst(lambda e, tq=tq: e.scalar_tensor_tensor(
                ST[:, tq, S_VAR:S_VAR + 1], ST[:, tq, S_SSQ:S_SSQ + 1], RC,
                ST[:, tq, S_T1:S_T1 + 1], AL.mult, AL.subtract))
        v_var2 = P.now("v")
        P.wait("scalar", "v", v_var2)
        for tt in range(NTT):
            tq = tt + NTT * gp
            P.op("scalar", lambda e, tq=tq: e.activation(
                ST[:, tq, S_DEN:S_DEN + 1], ST[:, tq, S_VAR:S_VAR + 1], AF.Sqrt,
                bias=EPS, scale=1.0), inc="s")
        s_den2 = P.now("s")
        P.wait("vector", "s", s_den2)
        for tt in range(NTT):
            tq = tt + NTT * gp
            vst(lambda e, tq=tq: e.reciprocal(
                ST[:, tq, S_RHO2:S_RHO2 + 1], ST[:, tq, S_DEN:S_DEN + 1]))
            vst(lambda e, tq=tq: e.tensor_tensor(
                ST[:, tq, S_MR2:S_MR2 + 1], ST[:, tq, S_MU:S_MU + 1],
                ST[:, tq, S_RHO2:S_RHO2 + 1], AL.mult))
        v_r2 = P.now("v")
        P.wait("scalar", "v", v_r2)
        P.op("scalar", lambda e: e.copy(
            ST[:, :, S_RHO2:S_RHO2 + 2], ST[:, :, S_RHO2:S_RHO2 + 2]), inc="s")
        P.wait("vector", "s", P.now("s"))
        for tt in range(NTT):
            tq = tt + NTT * gp
            tg = g * NTT + tt
            vst(lambda e, tq=tq, tt=tt: e.tensor_scalar(
                m_ar.ap()[:, tt, 0:2048], m_ar.ap()[:, tt, 0:2048],
                ST[:, tq, S_RHO2:S_RHO2 + 1], ST[:, tq, S_MR2:S_MR2 + 1],
                AL.mult, AL.subtract))
            v_fin = P.now("v")
            P.wait("scalar", "v", v_fin)
            P.op("scalar", lambda e, tt=tt, tg=tg: e.dma_start(
                out=dt["out"][128 * tg:128 * (tg + 1), :],
                in_=m_ar.ap()[:, tt, 0:2048]), inc="qs", inc_by=16)
        prev_qs[0] = P.now("qs")

    P.wait("gpsimd", "qs", P.now("qs"))
    return _finish(nc, P, dt)


def _finish(nc, P, dt):
    with nc.Block() as block:
        @block.sync
        def _(eng):
            P.emit("sync", eng)

        @block.vector
        def _(eng):
            P.emit("vector", eng)

        @block.scalar
        def _(eng):
            P.emit("scalar", eng)

        @block.tensor
        def _(eng):
            P.emit("tensor", eng)

        @block.gpsimd
        def _(eng):
            P.emit("gpsimd", eng)

    return nc, dt


def _host_layout(inputs):
    f32 = np.float32
    hs = np.asarray(inputs["L0_attn_hidden_states"][0], f32)
    hsT = np.ascontiguousarray(hs.T)
    hsT_ext = np.concatenate([np.zeros((D, 3), f32), hsT], axis=1)

    wxT = np.ascontiguousarray(np.asarray(inputs["L0_attn_W_in_x"], f32).T)
    wzT = np.ascontiguousarray(np.asarray(inputs["L0_attn_W_in_z"], f32).T)
    wpT = np.ascontiguousarray(np.asarray(inputs["L0_attn_W_param"], f32).T)
    woT = np.ascontiguousarray(np.asarray(inputs["L0_attn_W_out"], f32).T)
    wdt = np.asarray(inputs["L0_attn_W_dt"], f32)
    wdtT_rep = np.ascontiguousarray(np.repeat(wdt.T, 16, axis=1))
    wgT = np.ascontiguousarray(np.asarray(inputs["L0_ffn_w_gate"], f32).T)
    wuT = np.ascontiguousarray(np.asarray(inputs["L0_ffn_w_up"], f32).T)
    wdT = np.ascontiguousarray(np.asarray(inputs["L0_ffn_w_down"], f32).T)

    def chmat(v):
        return np.ascontiguousarray(np.asarray(v, f32).reshape(CG, 128).T)

    conv_w = np.asarray(inputs["L0_attn_conv_w"], f32).reshape(C, 4)
    conv_wm = np.zeros((128, CG * 4), f32)
    for g in range(CG):
        for k in range(4):
            conv_wm[:, 4 * g + k] = conv_w[128 * g:128 * (g + 1), k]
    conv_b = chmat(inputs["L0_attn_conv_b"])
    a_log = chmat(np.asarray(inputs["L0_attn_A_log"], f32).reshape(C))
    bdt_rep = chmat(np.repeat(np.asarray(inputs["L0_attn_b_dt"], f32), 16))
    d_vec = chmat(inputs["L0_attn_D"])

    ident = np.eye(128, dtype=ml_dtypes.bfloat16)
    ident32 = np.eye(128, dtype=f32)
    ones1 = np.ones((1, 128), f32)
    zcol = np.zeros((128, 1), f32)

    in_maps = []
    for c in range(NCORES):
        pm = np.zeros((128, NCORES), f32)
        pm[:, :c] = 1.0
        im = {
            "hsT": np.ascontiguousarray(hsT_ext[:, T * c:T * c + T + 3]),
            "hs": np.ascontiguousarray(hs[T * c:T * (c + 1)]),
            "wxT": wxT, "wzT": wzT, "wpT": wpT, "woT": woT,
            "wdtT_rep": wdtT_rep,
            "conv_w": conv_wm, "conv_b": conv_b, "a_log": a_log,
            "bdt_rep": bdt_rep, "d_vec": d_vec,
            "ident": ident, "ident32": ident32,
            "pmask": pm, "pmask1m": np.ascontiguousarray(1.0 - pm),
            "zcol": zcol, "ones1": ones1,
            "cvals": np.tile(np.array([[EPS, 1.0, 0.5, -0.5]], f32), (128, 1)),
            "wgT_sl": np.ascontiguousarray(wgT[:, 1024 * c:1024 * (c + 1)]),
            "wuT_sl": np.ascontiguousarray(wuT[:, 1024 * c:1024 * (c + 1)]),
            "wdT_sl": np.ascontiguousarray(wdT[1024 * c:1024 * (c + 1)]),
        }
        in_maps.append(im)
    return in_maps


_BUILT = {}


def kernel(**inputs):
    if "nc" not in _BUILT:
        nc, dtm = build_program()
        _BUILT["nc"] = nc
    nc = _BUILT["nc"]
    in_maps = _host_layout(inputs)
    res = run_bass_kernel_spmd(nc, in_maps, core_ids=list(range(NCORES)))
    out = np.concatenate([np.asarray(r["out"]) for r in res.results], axis=0)
    return out.reshape(1, L, D).astype(np.float32)
